# version 01
# baseline (speedup 1.0000x reference)
import math
import numpy as np
import concourse.bass as bass
import concourse.mybir as mybir
from concourse.bass import IndirectOffsetOnAxis
from concourse.bass_utils import run_bass_kernel_spmd

F32 = mybir.dt.float32
BF16 = mybir.dt.bfloat16
I32 = mybir.dt.int32
AF = mybir.ActivationFunctionType
ALU = mybir.AluOpType
AX = mybir.AxisListType

S = 4096
D = 1024
NT = S // 128
NC = D // 128
NCORE = 8
EPS = 1e-5


class Sem:
    def __init__(self, nc, name):
        self.h = nc.alloc_semaphore(name)
        self.n = 0

    def inc(self, instr, k=1):
        instr.then_inc(self.h, k)
        self.n += k
        return instr


class Prog:
    def __init__(self, name):
        self.nc = bass.Bass("TRN2", target_bir_lowering=False, name=name)
        self.sems = {}

    def sem(self, name):
        if name not in self.sems:
            self.sems[name] = Sem(self.nc, name)
        return self.sems[name]

    def sbuf(self, name, shape, dtype):
        return self.nc.alloc_sbuf_tensor(name, shape, dtype)

    def psum(self, name, shape, dtype=F32):
        return self.nc.alloc_psum_tensor(name, shape, dtype)


class DmaStream:

    def __init__(self, p, name, nslots):
        self.sems = [p.sem(f"{name}_s{i}") for i in range(nslots)]
        self.k = nslots
        self.base = [s.n for s in self.sems]
        self.count = 0

    def issue(self, instr):
        s = self.sems[self.count % self.k]
        s.inc(instr, 16)
        self.count += 1
        return instr

    def wait(self, eng, t):
        s = self.sems[t % self.k]
        eng.wait_ge(s.h, self.base[t % self.k] + 16 * (t // self.k + 1))


def emit_ln(p, block, Xap, xlnt=None, cst=0, off=0, xtok=None, xcst=D,
            pre="ln", identb=None, identb_ready=None, stat=None,
            sx=None, sq=None, pt=None):
    DV, ACT, PE, CP = (p.sem("DV"), p.sem("ACT"), p.sem("PE"), p.sem("CP"))
    SX = sx if sx is not None else p.sbuf(pre + "_sx", [128, 2 * D], F32)
    SQ = sq if sq is not None else p.sbuf(pre + "_sq", [128, 2 * D], F32)
    STAT = stat if stat is not None else p.sbuf(pre + "_stat", [128, 6 * NT], F32)
    XLB = p.sbuf(pre + "_xlb", [128, 2 * D], BF16) if (xtok is None and xlnt is not None) else None
    if xlnt is not None:
        PT = pt if pt is not None else p.psum(pre + "_pt", [128, 2048], BF16)

    dv0, act0, pe0, cp0 = DV.n, ACT.n, PE.n, CP.n
    do_t = xlnt is not None
    ldx = DmaStream(p, pre + "_ldx", 2)

    @block.sync
    def _(sync):
        for t in range(NT):
            if t >= 2:
                sync.wait_ge(DV.h, dv0 + 7 * (t - 1))
                sync.wait_ge(ACT.h, act0 + 2 * (t - 1))
                if xtok is None and xlnt is not None and t >= 2:
                    sync.wait_ge(PE.h, pe0 + t - 1)
            ldx.issue(sync.dma_start(out=SX[:, (t % 2) * D:(t % 2 + 1) * D],
                                     in_=Xap[t * 128:(t + 1) * 128, :]))

    @block.scalar
    def _(sc):
        for t in range(NT):
            b = t % 2
            ldx.wait(sc, t)
            if t >= 2:
                sc.wait_ge(DV.h, dv0 + 7 * (t - 1))
            ACT.inc(sc.activation(SQ[:, b * D:(b + 1) * D],
                                  SX[:, b * D:(b + 1) * D], AF.Square))
            sc.wait_ge(DV.h, dv0 + 7 * t + 5)
            ACT.inc(sc.activation(STAT[:, 6 * t + 4:6 * t + 5],
                                  STAT[:, 6 * t + 3:6 * t + 4], AF.Sqrt))
    assert ACT.n == act0 + 2 * NT

    def _ln_copy(v, j):
        v.wait_ge(PE.h, pe0 + j + 1)
        src = PT[:, (j % 2) * 1024:(j % 2) * 1024 + 1024].rearrange(
            "p (c x) -> p c x", c=NC)
        dst = xlnt.ap().rearrange("p (c w) -> p c w", c=NC)[
            :, :, off + j * 128:off + j * 128 + 128]
        CP.inc(v.tensor_copy(dst, src))

    @block.vector
    def _(v):
        for t in range(NT):
            b = t % 2
            c = 6 * t
            sl = lambda j: STAT[:, c + j:c + j + 1]
            ldx.wait(v, t)
            DV.inc(v.reduce_sum(sl(0), SX[:, b * D:(b + 1) * D], axis=AX.X))
            v.wait_ge(ACT.h, act0 + 2 * t + 1)
            DV.inc(v.tensor_reduce(sl(1), SQ[:, b * D:(b + 1) * D],
                                   axis=AX.X, op=ALU.add))
            v.wait_ge(DV.h, DV.n)
            DV.inc(v.tensor_scalar(sl(2), sl(0), -1.0 / D, None, ALU.mult))
            v.wait_ge(DV.h, DV.n)
            DV.inc(v.tensor_scalar(sl(5), sl(2), sl(2), EPS, ALU.mult,
                                   ALU.subtract))
            v.wait_ge(DV.h, DV.n)
            DV.inc(v.tensor_scalar(sl(3), sl(1), 1.0 / D, sl(5), ALU.mult,
                                   ALU.subtract))
            v.wait_ge(ACT.h, act0 + 2 * t + 2)
            DV.inc(v.reciprocal(sl(5), sl(4)))
            v.wait_ge(DV.h, DV.n)
            if xtok is not None:
                dest = xtok[:, t * xcst:t * xcst + D]
            else:
                dest = XLB[:, (t % 2) * D:(t % 2 + 1) * D]
            DV.inc(v.tensor_scalar(dest, SX[:, b * D:(b + 1) * D], sl(2), sl(5),
                                   ALU.add, ALU.mult))
            if do_t and t >= 1:
                _ln_copy(v, t - 1)
        if do_t:
            _ln_copy(v, NT - 1)
    assert DV.n == dv0 + 7 * NT

    if do_t:
        @block.tensor
        def _(te):
            te.wait_ge(identb_ready[0].h, identb_ready[1])
            for t in range(NT):
                te.wait_ge(DV.h, dv0 + 7 * t + 7)
                if t >= 2:
                    te.wait_ge(CP.h, cp0 + t - 1)
                src = (xtok[:, t * xcst:t * xcst + D] if xtok is not None
                       else XLB[:, (t % 2) * D:(t % 2 + 1) * D])
                ins = None
                for cc in range(NC):
                    ins = te.transpose(
                        PT[:, (t % 2) * 1024 + cc * 128:(t % 2) * 1024 + (cc + 1) * 128],
                        src[:, cc * 128:(cc + 1) * 128], identb[:])
                PE.inc(ins)
        assert PE.n == pe0 + NT

        assert CP.n == cp0 + NT
    return dict(dv0=dv0, act0=act0, pe0=pe0, cp0=cp0, STAT=STAT)


CST_FFN = 4104


def emit_l1ffn(p, block, Xap, XOap, wrt_d, wkt_d, wvt_d, mus_d):
    nc = p.nc
    LDC, LDM, DV, ACT, PE, ST, MM, CP = (
        p.sem("LDC"), p.sem("LDM"), p.sem("DV"), p.sem("ACT"),
        p.sem("PE"), p.sem("ST"), p.sem("MM"), p.sem("CP"))

    identb = p.sbuf("identb", [128, 128], BF16)
    idb_d = nc.inline_tensor(np.eye(128).astype(np.float32), "idb_d")
    XLNP = p.sbuf("xlnp", [128, NC * CST_FFN], BF16)
    WRT = p.sbuf("wrt", [128, NC * D], BF16)
    WKT = p.sbuf("wkt", [128, NC * D], BF16)
    WVT = p.sbuf("wvt", [128, NC * D], BF16)
    MU = p.sbuf("mu", [128, 16], F32)

    LDI = p.sem("LDI")

    @block.gpsimd
    def _(gp):
        LDI.inc(gp.dma_start(out=identb[:], in_=idb_d.ap()), 16)
        for w_d, w_sb in ((wrt_d, WRT), (wkt_d, WKT), (wvt_d, WVT)):
            LDC.inc(gp.dma_start(
                out=w_sb[:].rearrange("p (c d) -> p c d", c=NC),
                in_=w_d.rearrange("(c p) d -> p c d", p=128)), 16)
    ldi_all = LDI.n
    ldc_all = LDC.n

    @block.sync
    def _(sync):
        LDM.inc(sync.dma_start(out=MU[:], in_=mus_d), 16)
    ldm_all = LDM.n

    @block.vector
    def _(v):
        v.wait_ge(LDM.h, ldm_all)
        DV.inc(v.memset(
            XLNP.ap().rearrange("p (c w) -> p c w", c=NC)[:, :, 0:1], 0.0))

    emit_ln(p, block, Xap, xlnt=XLNP, cst=CST_FFN, off=1,
            pre="ln", identb=identb, identb_ready=(LDI, ldi_all))
    cp_ready = CP.n

    XRB = p.sbuf("xrb", [128, NC * 512], BF16)
    XKB = p.sbuf("xkb", [128, NC * 512], BF16)
    DBUF = p.sbuf("dbuf", [128, 512], BF16)
    KAB = p.sbuf("kab", [128, NC * 512], BF16)
    DKR = p.sbuf("dkr", [128, 2 * 512], BF16)
    RB = p.sbuf("rb", [128, 4 * D], F32)
    TOUT = p.sbuf("tout", [128, 2 * D], F32)
    SX2 = p.sbuf("sx2", [128, 2 * D], F32)
    PA = p.psum("pa", [128, 2048], F32)
    PB = p.psum("pb", [128, 1024], F32)

    dv0, act0, mm0 = DV.n, ACT.n, MM.n
    ldx2 = DmaStream(p, "ffn_ldx2", 2)
    stx = DmaStream(p, "ffn_stx", 2)
    NB = 8
    DV_B = 3 * NC + NC + 2 * 4
    MM_B = 8 + 8 + 8
    ACT_B = 4 + NC

    mm_r = lambda bI, tsub: mm0 + MM_B * bI + 2 * (tsub + 1)
    mm_k = lambda bI, dc: mm0 + MM_B * bI + 8 + dc + 1
    mm_v = lambda bI, tsub: mm0 + MM_B * bI + 16 + 2 * (tsub + 1)
    mm_blk = lambda bI: mm0 + MM_B * bI
    act_sig = lambda bI, tsub: act0 + ACT_B * bI + tsub + 1
    act_sq = lambda bI, dc: act0 + ACT_B * bI + 4 + dc + 1
    dv_mix = lambda bI: dv0 + DV_B * bI + 3 * NC
    dv_relu = lambda bI, dc: dv0 + DV_B * bI + 3 * NC + dc + 1
    dv_comb1 = lambda bI, u: dv0 + DV_B * bI + 32 + 2 * u + 1
    dv_comb2 = lambda bI, u: dv0 + DV_B * bI + 32 + 2 * u + 2

    @block.sync
    def _(sync):
        for i in range(NT):
            if i >= 2:
                sync.wait_ge(DV.h, dv_comb2(*divmod(i - 2, 4)))
            ldx2.issue(sync.dma_start(out=SX2[:, (i % 2) * D:(i % 2 + 1) * D],
                                      in_=Xap[i * 128:(i + 1) * 128, :]))

    @block.vector
    def _(v):
        v.wait_ge(CP.h, cp_ready)
        for bI in range(NB):
            b0 = bI * 512
            if bI >= 1:
                v.wait_ge(MM.h, mm_blk(bI - 1) + 16)
            for c in range(NC):
                base = c * CST_FFN
                xT = XLNP[:, base + 1 + b0:base + 513 + b0]
                xP = XLNP[:, base + b0:base + 512 + b0]
                v.wait_ge(DV.h, DV.n)
                DV.inc(v.tensor_tensor(DBUF[:], xT, xP, ALU.subtract))
                v.wait_ge(DV.h, DV.n)
                DV.inc(v.scalar_tensor_tensor(XRB[:, c * 512:(c + 1) * 512],
                                              DBUF[:], MU[:, c:c + 1], xP,
                                              ALU.mult, ALU.add))
                DV.inc(v.scalar_tensor_tensor(XKB[:, c * 512:(c + 1) * 512],
                                              DBUF[:], MU[:, 8 + c:9 + c], xP,
                                              ALU.mult, ALU.add))
            for dc in range(NC):
                v.wait_ge(MM.h, mm_k(bI, dc))
                if dc >= 2:
                    v.wait_ge(ACT.h, act_sq(bI, dc - 2))
                elif bI >= 1:
                    v.wait_ge(ACT.h, act_sq(bI - 1, dc + 6))
                DV.inc(v.tensor_single_scalar(
                    DKR[:, (dc % 2) * 512:(dc % 2 + 1) * 512],
                    PB[:, (dc % 2) * 512:(dc % 2 + 1) * 512], 0.0, ALU.max))
            for u in range(4):
                i = bI * 4 + u
                v.wait_ge(MM.h, mm_v(bI, u))
                ldx2.wait(v, i)
                if i >= 2:
                    stx.wait(v, i - 2)
                ob = (u % 2) * D
                DV.inc(v.tensor_tensor(TOUT[:, ob:ob + D],
                                       PA[:, ob:ob + D],
                                       RB[:, u * D:(u + 1) * D], ALU.mult))
                v.wait_ge(DV.h, DV.n)
                DV.inc(v.tensor_tensor(TOUT[:, ob:ob + D], TOUT[:, ob:ob + D],
                                       SX2[:, (i % 2) * D:(i % 2 + 1) * D],
                                       ALU.add))
    assert DV.n == dv0 + DV_B * NB

    @block.scalar
    def _(sc):
        for bI in range(NB):
            for tsub in range(4):
                sc.wait_ge(MM.h, mm_r(bI, tsub))
                if bI >= 1:
                    sc.wait_ge(DV.h, dv_comb1(bI - 1, tsub))
                ACT.inc(sc.activation(RB[:, tsub * D:(tsub + 1) * D],
                                      PA[:, (tsub % 2) * D:(tsub % 2 + 1) * D],
                                      AF.Sigmoid))
            for dc in range(NC):
                sc.wait_ge(DV.h, dv_relu(bI, dc))
                if bI >= 1:
                    sc.wait_ge(MM.h, mm_blk(bI))
                ACT.inc(sc.activation(KAB[:, dc * 512:(dc + 1) * 512],
                                      DKR[:, (dc % 2) * 512:(dc % 2 + 1) * 512],
                                      AF.Square))
    assert ACT.n == act0 + ACT_B * NB

    @block.tensor
    def _(te):
        te.wait_ge(LDC.h, ldc_all)
        for bI in range(NB):
            te.wait_ge(DV.h, dv_mix(bI))
            for tsub in range(4):
                if tsub >= 2:
                    te.wait_ge(ACT.h, act_sig(bI, tsub - 2))
                elif bI >= 1:
                    te.wait_ge(DV.h, dv_comb1(bI - 1, tsub + 2))
                for dn in range(2):
                    ins = None
                    for c in range(NC):
                        ins = te.matmul(
                            PA[:, (tsub % 2) * D + dn * 512:(tsub % 2) * D + (dn + 1) * 512],
                            XRB[:, c * 512 + tsub * 128:c * 512 + (tsub + 1) * 128],
                            WRT[:, c * D + dn * 512:c * D + (dn + 1) * 512],
                            start=(c == 0), stop=(c == NC - 1))
                    MM.inc(ins)
            for dc in range(NC):
                if dc >= 2:
                    te.wait_ge(DV.h, dv_relu(bI, dc - 2))
                elif bI >= 1:
                    te.wait_ge(DV.h, dv_relu(bI - 1, dc + 6))
                ins = None
                for c in range(NC):
                    ins = te.matmul(
                        PB[:, (dc % 2) * 512:(dc % 2 + 1) * 512],
                        WKT[:, c * D + dc * 128:c * D + (dc + 1) * 128],
                        XKB[:, c * 512:(c + 1) * 512],
                        start=(c == 0), stop=(c == NC - 1))
                MM.inc(ins)
            te.wait_ge(ACT.h, act_sq(bI, NC - 1))
            for tsub in range(4):
                if tsub >= 2:
                    te.wait_ge(DV.h, dv_comb1(bI, tsub - 2))
                else:
                    te.wait_ge(ACT.h, act_sig(bI, tsub + 2))
                for dn in range(2):
                    ins = None
                    for dc in range(NC):
                        ins = te.matmul(
                            PA[:, (tsub % 2) * D + dn * 512:(tsub % 2) * D + (dn + 1) * 512],
                            KAB[:, dc * 512 + tsub * 128:dc * 512 + (tsub + 1) * 128],
                            WVT[:, dc * D + dn * 512:dc * D + (dn + 1) * 512],
                            start=(dc == 0), stop=(dc == NC - 1))
                    MM.inc(ins)
    assert MM.n == mm0 + MM_B * NB

    @block.gpsimd
    def _(gp):
        for i in range(NT):
            bI, u = divmod(i, 4)
            gp.wait_ge(DV.h, dv_comb2(bI, u))
            stx.issue(gp.dma_start(out=XOap[i * 128:(i + 1) * 128, :],
                                   in_=TOUT[:, (u % 2) * D:(u % 2 + 1) * D]))
        for t in range(NT - 2, NT):
            stx.wait(gp, t)


def build_l1ffn():
    p = Prog("l1ffn")
    nc = p.nc
    X = nc.dram_tensor("X", [S, D], F32, kind="ExternalInput").ap()
    WRT = nc.dram_tensor("WRT", [D, D], F32, kind="ExternalInput").ap()
    WKT = nc.dram_tensor("WKT", [D, D], F32, kind="ExternalInput").ap()
    WVT = nc.dram_tensor("WVT", [D, D], F32, kind="ExternalInput").ap()
    MUS = nc.dram_tensor("MUS", [128, 16], F32, kind="ExternalInput").ap()
    XO = nc.dram_tensor("XO", [S, D], F32, kind="ExternalOutput").ap()
    with nc.Block() as block:
        emit_l1ffn(p, block, X, XO, WRT, WKT, WVT, MUS)
    return p


def prep_l1ffn_inputs(x_b, W_r, W_k, W_v, mu_r, mu_k):
    mus = np.concatenate([np.asarray(mu_r).reshape(8, 128).T,
                          np.asarray(mu_k).reshape(8, 128).T], axis=1)
    return {
        "X": np.ascontiguousarray(x_b),
        "WRT": np.ascontiguousarray(np.asarray(W_r).T),
        "WKT": np.ascontiguousarray(np.asarray(W_k).T),
        "WVT": np.ascontiguousarray(np.asarray(W_v).T),
        "MUS": np.ascontiguousarray(mus.astype(np.float32)),
    }




def emit_attn_a(p, block, Xap, XOap, wqt_d, wkt_d, bias_d):
    nc = p.nc
    LDC, LDM, LDI, DV, ACT, PE, ST, CP = (
        p.sem("LDC"), p.sem("LDM"), p.sem("LDI"), p.sem("DV"), p.sem("ACT"),
        p.sem("PE"), p.sem("ST"), p.sem("CP"))

    identb = p.sbuf("identb", [128, 128], BF16)
    idb_d = nc.inline_tensor(np.eye(128).astype(np.float32), "idb_d")
    WQT = p.sbuf("wqt", [128, NC * D], BF16)
    WKT = p.sbuf("wkt", [128, NC * D], BF16)
    BIAS = p.sbuf("biasqk", [128, 16], F32)
    XTOK = p.sbuf("xtok", [128, NT * D], BF16)
    KT = p.sbuf("kt", [128, NC * S], BF16)
    XLT = p.sbuf("xlt", [128, NC * 512], BF16)
    QTI = p.sbuf("qti", [128, NC * 128], BF16)
    PSB = p.sbuf("psb", [128, 2 * 1024], BF16)
    PTS = p.sbuf("pts", [128, 2 * 1024], BF16)
    CS = p.sbuf("cs", [128, D], F32)
    CSROW = p.sbuf("csrow", [1, D], BF16)
    DEN = p.sbuf("den", [128, 8], F32)
    ONE1 = p.sbuf("one1", [128, 1], BF16)
    ONER = p.sbuf("oner", [1, 128], BF16)
    SXA = p.sbuf("sxa", [128, 2 * D], F32)
    SCR8 = p.sbuf("scr8", [128, 2 * D], F32)

    PS_A = p.psum("ps_a", [128, 2048], F32)
    PS_B = p.psum("ps_b", [128, 2048], F32)
    A0 = PS_A[:, 0:1024]
    A1 = PS_A[:, 1024:2048]
    B0 = PS_B[:, 0:1024]
    B1 = PS_B[:, 1024:2048]
    STG = PS_A[:, 0:512].bitcast(BF16)
    SCPAIR = [A0, A1, B0, A0]
    PTRSTG = [PS_A[:, 0:512].bitcast(BF16), PS_A[:, 1024:1536].bitcast(BF16),
              PS_B[:, 0:512].bitcast(BF16), PS_A[:, 0:512].bitcast(BF16)]

    @block.gpsimd
    def _(gp):
        LDI.inc(gp.dma_start(out=identb[:], in_=idb_d.ap()), 16)
        for w_d, w_sb in ((wqt_d, WQT), (wkt_d, WKT)):
            LDC.inc(gp.dma_start(
                out=w_sb[:].rearrange("p (c d) -> p c d", c=NC),
                in_=w_d.rearrange("(c p) d -> p c d", p=128)), 16)
    ldc_all, ldi_all = LDC.n, LDI.n

    @block.sync
    def _(sync):
        LDM.inc(sync.dma_start(out=BIAS[:], in_=bias_d), 16)
    ldm_all = LDM.n

    BIASQ = p.sbuf("biasq", [128, 8], F32)

    @block.vector
    def _(v):
        DV.inc(v.memset(ONE1[:], 1.0))
        DV.inc(v.memset(ONER[:], 1.0))
        v.wait_ge(LDM.h, ldm_all)
        DV.inc(v.tensor_scalar(BIASQ[:], BIAS[:, 0:8], 1.0 / 32.0, None,
                               ALU.mult))

    dv_ln0 = DV.n
    emit_ln(p, block, Xap, xlnt=None, xtok=XTOK, xcst=D, pre="lna",
            sx=SXA, sq=SCR8)
    assert DV.n == dv_ln0 + 7 * NT

    pe_k0, dv_k0, cp_k0 = PE.n, DV.n, CP.n
    NSL = 8

    @block.tensor
    def _(te):
        te.wait_ge(LDI.h, ldi_all)
        te.wait_ge(LDC.h, ldc_all)
        for sl in range(NSL):
            for u in range(4):
                tt = 4 * sl + u
                te.wait_ge(DV.h, dv_ln0 + 7 * tt + 7)
                if tt > 0:
                    te.wait_ge(CP.h, cp_k0 + tt)
                ins = None
                for cc in range(NC):
                    ins = te.transpose(STG[:, cc * 128:(cc + 1) * 128],
                                       XTOK[:, tt * D + cc * 128:tt * D + (cc + 1) * 128],
                                       identb[:])
                PE.inc(ins)
            for ec in range(NC):
                te.wait_ge(CP.h, cp_k0 + 4 * (sl + 1))
                if ec >= 2:
                    te.wait_ge(DV.h, dv_k0 + 8 * sl + (ec - 2) + 1)
                elif sl >= 1:
                    te.wait_ge(DV.h, dv_k0 + 8 * (sl - 1) + (ec + 6) + 1)
                ins = None
                for cc in range(NC):
                    ins = te.matmul(PS_B[:, (ec % 2) * 512:(ec % 2 + 1) * 512],
                                    WKT[:, cc * D + ec * 128:cc * D + (ec + 1) * 128],
                                    XLT[:, cc * 512:(cc + 1) * 512],
                                    start=(cc == 0), stop=(cc == NC - 1))
                PE.inc(ins)
    assert PE.n == pe_k0 + NSL * 12

    @block.vector
    def _(v):
        v.wait_ge(LDM.h, ldm_all)
        for sl in range(NSL):
            for u in range(4):
                v.wait_ge(PE.h, pe_k0 + 12 * sl + u + 1)
                dst = XLT.ap().rearrange("p (c w) -> p c w", c=NC)[
                    :, :, u * 128:(u + 1) * 128]
                src = STG.rearrange("p (c x) -> p c x", c=NC)
                CP.inc(v.tensor_copy(dst, src))
            for ec in range(NC):
                v.wait_ge(PE.h, pe_k0 + 12 * sl + 4 + ec + 1)
                DV.inc(v.tensor_scalar(
                    KT[:, ec * S + sl * 512:ec * S + (sl + 1) * 512],
                    PS_B[:, (ec % 2) * 512:(ec % 2 + 1) * 512],
                    BIAS[:, 8 + ec:9 + ec], None, ALU.add))
    assert CP.n == cp_k0 + 4 * NSL
    assert DV.n == dv_k0 + 8 * NSL

    pe_c0, dv_c0 = PE.n, DV.n

    @block.tensor
    def _(te):
        te.wait_ge(CP.h, cp_k0 + 4 * NSL)
        te.wait_ge(DV.h, dv_k0 + 8 * NSL)
        for tt in range(NT):
            for nh in range(2):
                ins = te.matmul(PS_A[0:1, nh * 512:(nh + 1) * 512],
                                ONE1[:], XTOK[:, tt * D + nh * 512:tt * D + (nh + 1) * 512],
                                start=(tt == 0), stop=(tt == NT - 1))
        PE.inc(ins)

    @block.vector
    def _(v):
        v.wait_ge(PE.h, pe_c0 + 1)
        DV.inc(v.tensor_scalar(CSROW[:], PS_A[0:1, 0:1024], -2.0 / S, None,
                               ALU.mult))

    @block.tensor
    def _(te):
        te.wait_ge(DV.h, dv_c0 + 1)
        ins = None
        for nh in range(2):
            ins = te.matmul(PS_B[:, nh * 512:(nh + 1) * 512], ONER[:],
                            CSROW[0:1, nh * 512:(nh + 1) * 512],
                            start=True, stop=True)
        PE.inc(ins)

    @block.vector
    def _(v):
        v.wait_ge(PE.h, pe_c0 + 2)
        DV.inc(v.tensor_copy(CS[:], B0))

    pe0, dv0, act0 = PE.n, DV.n, ACT.n
    PE_I, DV_I, ACT_I = 25, 13, 4
    ldr = DmaStream(p, "aa_ldr", 2)
    stx = DmaStream(p, "aa_stx", 2)

    peb = lambda it: pe0 + PE_I * it
    dvb = lambda it: dv0 + DV_I * it
    actb = lambda it: act0 + ACT_I * it

    PE_TPX = 1
    PE_Q = lambda ec: 1 + ec + 1
    PE_SC = {0: (10, 11), 1: (12, 13), 2: (16, 17), 3: (20, 21)}
    PE_PTR = {0: 14, 1: 18, 2: 22, 3: 24}
    PE_PV = {0: 15, 1: 19, 2: 23, 3: 25}
    DV_XLTI = 1
    DV_QTI = 2
    DV_PTS = {0: 3, 1: 4, 2: 5, 3: 6}
    DV_E3 = 13

    @block.sync
    def _(sync):
        for it in range(NT):
            if it >= 2:
                sync.wait_ge(DV.h, dvb(it - 2) + DV_E3)
            else:
                sync.wait_ge(DV.h, dv_ln0 + 7 * NT)
            ldr.issue(sync.dma_start(out=SXA[:, (it % 2) * D:(it % 2 + 1) * D],
                                     in_=Xap[it * 128:(it + 1) * 128, :]))

    @block.tensor
    def _(te):
        for it in range(NT):
            if it >= 1:
                te.wait_ge(DV.h, dvb(it - 1) + DV_PTS[3])
            else:
                te.wait_ge(DV.h, dv0)
            ins = None
            for cc in range(NC):
                ins = te.transpose(STG[:, cc * 128:(cc + 1) * 128],
                                   XTOK[:, it * D + cc * 128:it * D + (cc + 1) * 128],
                                   identb[:])
            PE.inc(ins)
            if it >= 1:
                te.wait_ge(DV.h, dvb(it - 1) + DV_PTS[1])
            for ec in range(NC):
                te.wait_ge(DV.h, dvb(it) + DV_XLTI)
                ins = None
                for cc in range(NC):
                    ins = te.matmul(A1[:, ec * 128:(ec + 1) * 128],
                                    WQT[:, cc * D + ec * 128:cc * D + (ec + 1) * 128],
                                    XLT[:, cc * 512:cc * 512 + 128],
                                    start=(cc == 0), stop=(cc == NC - 1))
                PE.inc(ins)

            def sc(g):
                te.wait_ge(DV.h, dvb(it) + DV_QTI)
                if g == 3:
                    te.wait_ge(DV.h, dvb(it) + DV_PTS[0])
                for js in range(2):
                    j0 = g * 1024 + js * 512
                    ins = None
                    for ec in range(NC):
                        ins = te.matmul(
                            SCPAIR[g][:, js * 512:(js + 1) * 512],
                            QTI[:, ec * 128:(ec + 1) * 128],
                            KT[:, ec * S + j0:ec * S + j0 + 512],
                            start=(ec == 0), stop=(ec == NC - 1))
                    PE.inc(ins)

            def ptr(g):
                te.wait_ge(ACT.h, actb(it) + g + 1)
                ins = None
                for l in range(8):
                    ins = te.transpose(
                        PTRSTG[g][:, l * 128:(l + 1) * 128],
                        PSB[:, (g % 2) * 1024 + l * 128:(g % 2) * 1024 + (l + 1) * 128],
                        identb[:])
                PE.inc(ins)

            def pv(g):
                te.wait_ge(DV.h, dvb(it) + DV_PTS[g])
                ins = None
                for l in range(8):
                    jc = g * 8 + l
                    for nh in range(2):
                        ins = te.matmul(
                            B1[:, nh * 512:(nh + 1) * 512],
                            PTS[:, (g % 2) * 1024 + l * 128:(g % 2) * 1024 + (l + 1) * 128],
                            XTOK[:, jc * D + nh * 512:jc * D + (nh + 1) * 512],
                            start=(g == 0 and l == 0), stop=(g == 3 and l == 7),
                            skip_group_check=True)
                PE.inc(ins)

            sc(0); sc(1); ptr(0); pv(0); sc(2); ptr(1); pv(1)
            sc(3); ptr(2); pv(2); ptr(3); pv(3)
    assert PE.n == pe0 + PE_I * NT

    @block.scalar
    def _(sc_):
        for it in range(NT):
            if it >= 1:
                sc_.wait_ge(DV.h, dvb(it - 1) + 8)
            for g in range(4):
                sc_.wait_ge(PE.h, peb(it) + PE_SC[g][1])
                if g >= 2:
                    sc_.wait_ge(PE.h, peb(it) + PE_PTR[g - 2])
                ACT.inc(sc_.activation(PSB[:, (g % 2) * 1024:(g % 2 + 1) * 1024],
                                       SCPAIR[g], AF.Exp,
                                       accum_out=DEN[:, g:g + 1]))
    assert ACT.n == act0 + ACT_I * NT

    @block.vector
    def _(v):
        for it in range(NT):
            v.wait_ge(PE.h, peb(it) + PE_TPX)
            src = STG.rearrange("p (c x) -> p c x", c=NC)
            dst = XLT.ap().rearrange("p (c w) -> p c w", c=NC)[:, :, 0:128]
            DV.inc(v.tensor_copy(dst, src))
            v.wait_ge(PE.h, peb(it) + PE_Q(7))
            DV.inc(v.scalar_tensor_tensor(
                QTI[:].rearrange("p (e x) -> p e x", e=NC),
                A1.rearrange("p (e x) -> p e x", e=NC),
                1.0 / 32.0,
                BIASQ[:].unsqueeze(2).broadcast_to([128, 8, 128]),
                ALU.mult, ALU.add))
            for g in range(4):
                v.wait_ge(PE.h, peb(it) + PE_PTR[g])
                src = PTRSTG[g].rearrange("p (l x) -> p l x", l=8)
                dst = PTS[:, (g % 2) * 1024:(g % 2 + 1) * 1024].rearrange(
                    "p (l x) -> p l x", l=8)
                DV.inc(v.tensor_copy(dst, src))
            v.wait_ge(ACT.h, actb(it) + 4)
            DV.inc(v.tensor_tensor(DEN[:, 4:5], DEN[:, 0:1], DEN[:, 1:2], ALU.add))
            DV.inc(v.tensor_tensor(DEN[:, 5:6], DEN[:, 2:3], DEN[:, 3:4], ALU.add))
            v.wait_ge(DV.h, DV.n)
            DV.inc(v.tensor_tensor(DEN[:, 6:7], DEN[:, 4:5], DEN[:, 5:6], ALU.add))
            v.wait_ge(DV.h, DV.n)
            DV.inc(v.reciprocal(DEN[:, 7:8], DEN[:, 6:7]))
            v.wait_ge(DV.h, DV.n)
            v.wait_ge(PE.h, peb(it) + PE_PV[3])
            if it >= 2:
                stx.wait(v, it - 2)
            ob = (it % 2) * D
            DV.inc(v.scalar_tensor_tensor(SCR8[:, ob:ob + D], B1,
                                          DEN[:, 7:8],
                                          XTOK[:, it * D:(it + 1) * D],
                                          ALU.mult, ALU.add))
            v.wait_ge(DV.h, DV.n)
            DV.inc(v.tensor_tensor(SCR8[:, ob:ob + D], SCR8[:, ob:ob + D],
                                   CS[:], ALU.add))
            ldr.wait(v, it)
            v.wait_ge(DV.h, DV.n)
            DV.inc(v.tensor_tensor(SCR8[:, ob:ob + D], SCR8[:, ob:ob + D],
                                   SXA[:, ob:ob + D], ALU.add))
    assert DV.n == dv0 + DV_I * NT

    @block.gpsimd
    def _(gp):
        for it in range(NT):
            gp.wait_ge(DV.h, dvb(it) + DV_E3)
            stx.issue(gp.dma_start(out=XOap[it * 128:(it + 1) * 128, :],
                                   in_=SCR8[:, (it % 2) * D:(it % 2 + 1) * D]))
        for t in (NT - 2, NT - 1):
            stx.wait(gp, t)


def build_attn_a():
    p = Prog("attna")
    nc = p.nc
    X = nc.dram_tensor("X", [S, D], F32, kind="ExternalInput").ap()
    WQT = nc.dram_tensor("WQT", [D, D], F32, kind="ExternalInput").ap()
    WKT = nc.dram_tensor("WKT", [D, D], F32, kind="ExternalInput").ap()
    BIAS = nc.dram_tensor("BIAS", [128, 16], F32, kind="ExternalInput").ap()
    XO = nc.dram_tensor("XO", [S, D], F32, kind="ExternalOutput").ap()
    with nc.Block() as block:
        emit_attn_a(p, block, X, XO, WQT, WKT, BIAS)
    return p


def prep_attn_a_inputs(x_b, Wq, bq, Wk, bk):
    bias = np.concatenate([np.asarray(bq).reshape(8, 128).T,
                           np.asarray(bk).reshape(8, 128).T], axis=1)
    return {
        "X": np.ascontiguousarray(x_b),
        "WQT": np.ascontiguousarray(np.asarray(Wq).T),
        "WKT": np.ascontiguousarray(np.asarray(Wk).T),
        "BIAS": np.ascontiguousarray(bias.astype(np.float32)),
    }




def emit_attn_b(p, block, Xap, XOap, wqkv_d, wout_d, bout_d, cosf_d, sinf_d,
                scale=0.125):
    nc = p.nc
    LDC, LDM, LDI, DV, ACT, PE, CP = (
        p.sem("LDC"), p.sem("LDM"), p.sem("LDI"), p.sem("DV"), p.sem("ACT"),
        p.sem("PE"), p.sem("CP"))
    GRP = p.sem("GRP")

    identb = p.sbuf("identb", [128, 128], BF16)
    idb_d = nc.inline_tensor(np.eye(128).astype(np.float32), "idb_d")
    XLNT = p.sbuf("xlnt", [128, NC * S], BF16)
    WGB = p.sbuf("wgb", [128, 2 * NC * 384], BF16)
    WOB = p.sbuf("wob", [128, NC * D], BF16)
    COSF = p.sbuf("cosf", [128, 512], F32)
    SINF = p.sbuf("sinf", [128, 512], F32)
    GQ = p.sbuf("gq", [128, NT * 128], BF16)
    GK = p.sbuf("gk", [128, NT * 128], BF16)
    GQR = p.sbuf("gqr", [128, NT * 128], BF16)
    GKR = p.sbuf("gkr", [128, NT * 128], BF16)
    GVE = p.sbuf("gve", [128, NT * 130], BF16)
    QT = p.sbuf("qtg", [128, S], BF16)
    KT = p.sbuf("ktg", [128, S], BF16)
    PTSA = p.sbuf("ptsa", [128, 1536], BF16)
    PTSB = p.sbuf("ptsb", [128, 1536], BF16)
    RT1 = p.sbuf("rt1", [128, 1024], BF16)
    RT2 = p.sbuf("rt2", [128, 1024], BF16)
    DNI = p.sbuf("dni", [1, 1024], F32)
    BCC = p.sbuf("bcc", [128, 1024], F32)
    OUTB = p.sbuf("outb", [128, 1024], BF16)
    ONER = p.sbuf("oner", [1, 128], F32)
    BOUTB = p.sbuf("boutb", [128, D], F32)
    BOUTR = p.sbuf("boutr", [1, D], F32)
    OTI = p.sbuf("oti", [128, 2 * NC * 128], BF16)
    SXA = p.sbuf("sxa", [128, 2 * D], F32)
    SCR8 = p.sbuf("scr8", [128, 2 * D], F32)

    PS_S1 = p.psum("ps_s1", [128, 1536], F32)
    PS_S2 = p.psum("ps_s2", [128, 1536], F32)
    PS_P = p.psum("ps_p", [128, 1024], F32)
    LNPT = PS_S1[:, 0:1024].bitcast(BF16)
    PJ = PS_P
    PVA = PS_P[:, 0:512]
    PVB = PS_P[:, 512:1024]
    TSTG = PS_P[:, 0:512].bitcast(BF16)

    ODR = nc.dram_tensor("odr", [NC, 128, S], BF16, kind="Internal").ap()

    @block.gpsimd
    def _(gp):
        LDI.inc(gp.dma_start(out=identb[:], in_=idb_d.ap()), 16)
        LDC.inc(gp.dma_start(
            out=WOB[:].rearrange("p (c d) -> p c d", c=NC),
            in_=wout_d.rearrange("(c p) d -> p c d", p=128)), 16)
    ldc_all, ldi_all = LDC.n, LDI.n

    @block.sync
    def _(sync):
        LDM.inc(sync.dma_start(out=COSF[:], in_=cosf_d), 16)
        LDM.inc(sync.dma_start(out=SINF[:], in_=sinf_d), 16)
        LDM.inc(sync.dma_start(out=BOUTR[:], in_=bout_d), 16)
    ldm_all = LDM.n

    @block.vector
    def _(v):
        DV.inc(v.memset(ONER[:], 1.0))
        ones_view = GVE.ap().rearrange("p (t h e) -> p t h e", t=NT, h=2)[
            :, :, :, 64:65]
        DV.inc(v.memset(ones_view, 1.0))

    pe_pre0 = PE.n

    @block.tensor
    def _(te):
        te.wait_ge(LDC.h, ldc_all)
        te.wait_ge(LDM.h, ldm_all)
        te.wait_ge(DV.h, 1)
        ins = None
        for nh in range(2):
            ins = te.matmul(PS_S2[:, nh * 512:(nh + 1) * 512], ONER[:],
                            BOUTR[0:1, nh * 512:(nh + 1) * 512],
                            start=True, stop=True)
        PE.inc(ins)

    @block.vector
    def _(v):
        v.wait_ge(PE.h, pe_pre0 + 1)
        DV.inc(v.tensor_copy(BOUTB[:], PS_S2[:, 0:1024]))

    emit_ln(p, block, Xap, xlnt=XLNT, cst=S, off=0, pre="lnb",
            identb=identb, identb_ready=(LDI, ldi_all),
            sx=SXA, sq=SCR8, pt=LNPT)
    cp_ln = CP.n

    wg = DmaStream(p, "bw", 2)

    def _issue_wg(gp, g):
        wg.issue(gp.dma_start(
            out=WGB[:, (g % 2) * NC * 384:(g % 2 + 1) * NC * 384]
                .rearrange("p (c d) -> p c d", c=NC),
            in_=wqkv_d[:, g * 384:(g + 1) * 384]
                .rearrange("(c p) d -> p c d", p=128)))

    @block.gpsimd
    def _(gp):
        _issue_wg(gp, 0)
        _issue_wg(gp, 1)

    KGS = [3] * 10 + [2]
    KGO = [0, 3, 6, 9, 12, 15, 18, 21, 24, 27, 30]
    NKG = 11

    for g in range(NCORE):
        pe0, dv0, act0, cp0 = PE.n, DV.n, ACT.n, CP.n
        wslot = (g % 2) * NC * 384

        @block.tensor
        def _(te):
            te.wait_ge(CP.h, cp_ln)
            wg.wait(te, g)
            if g >= 1:
                te.wait_ge(GRP.h, g)
            for t in range(NT):
                if t >= 2:
                    te.wait_ge(DV.h, dv0 + 3 * (t - 1))
                ins = None
                for cc in range(NC):
                    ins = te.matmul(PJ[:, (t % 2) * 512:(t % 2) * 512 + 384],
                                    XLNT[:, cc * S + t * 128:cc * S + (t + 1) * 128],
                                    WGB[:, wslot + cc * 384:wslot + (cc + 1) * 384],
                                    start=(cc == 0), stop=(cc == NC - 1))
                PE.inc(ins)

        @block.vector
        def _(v):
            v.wait_ge(LDM.h, ldm_all)
            for t in range(NT):
                v.wait_ge(PE.h, pe0 + t + 1)
                base = (t % 2) * 512
                DV.inc(v.tensor_copy(GQ[:, t * 128:(t + 1) * 128],
                                     PJ[:, base:base + 128]))
                DV.inc(v.tensor_copy(GK[:, t * 128:(t + 1) * 128],
                                     PJ[:, base + 128:base + 256]))
                dstv = GVE.ap().rearrange("p (t h e) -> p t h e", t=NT, h=2)[
                    :, t:t + 1, :, 0:64]
                DV.inc(v.tensor_copy(
                    dstv, PJ[:, base + 256:base + 384]
                    .rearrange("p (o h d) -> p o h d", o=1, h=2)))

            def rope(srcT, dstT):
                se = srcT.ap().rearrange("p (t h i two) -> p t h i two",
                                         t=NT, h=2, two=2)[:, :, :, 0:16, 0:1]
                so = srcT.ap().rearrange("p (t h i two) -> p t h i two",
                                         t=NT, h=2, two=2)[:, :, :, 0:16, 1:2]
                de = dstT.ap().rearrange("p (t h i two) -> p t h i two",
                                         t=NT, h=2, two=2)[:, :, :, 0:16, 0:1]
                do = dstT.ap().rearrange("p (t h i two) -> p t h i two",
                                         t=NT, h=2, two=2)[:, :, :, 0:16, 1:2]
                cb = COSF[:].rearrange("p (t i) -> p t i", t=NT) \
                    .unsqueeze(2).broadcast_to([128, NT, 2, 16]).unsqueeze(4)
                sb = SINF[:].rearrange("p (t i) -> p t i", t=NT) \
                    .unsqueeze(2).broadcast_to([128, NT, 2, 16]).unsqueeze(4)
                r1 = RT1[:].rearrange("p (t h i) -> p t h i", t=NT, h=2) \
                    .unsqueeze(4)
                r2 = RT2[:].rearrange("p (t h i) -> p t h i", t=NT, h=2) \
                    .unsqueeze(4)
                DV.inc(v.tensor_tensor(r1, se, cb, ALU.mult))
                DV.inc(v.tensor_tensor(r2, so, sb, ALU.mult))
                v.wait_ge(DV.h, DV.n)
                DV.inc(v.tensor_tensor(de, r1, r2, ALU.subtract))
                v.wait_ge(DV.h, DV.n)
                DV.inc(v.tensor_tensor(r1, so, cb, ALU.mult))
                DV.inc(v.tensor_tensor(r2, se, sb, ALU.mult))
                v.wait_ge(DV.h, DV.n)
                DV.inc(v.tensor_tensor(do, r1, r2, ALU.add))
                v.wait_ge(DV.h, DV.n)
                sp = srcT.ap().rearrange("p (t h d) -> p t h d", t=NT, h=2)[
                    :, :, :, 32:64]
                dp = dstT.ap().rearrange("p (t h d) -> p t h d", t=NT, h=2)[
                    :, :, :, 32:64]
                DV.inc(v.tensor_copy(dp, sp))
            v.wait_ge(DV.h, DV.n)
            rope(GQ, GQR)
            rope(GK, GKR)
        assert DV.n == dv0 + 3 * NT + 14
        dv_rope_end = DV.n
        pe_tr0 = PE.n

        @block.tensor
        def _(te):
            te.wait_ge(DV.h, dv_rope_end)
            for which, src in ((0, GQR), (1, GKR)):
                for b4 in range(4):
                    if which * 4 + b4 > 0:
                        te.wait_ge(CP.h, cp0 + which * 4 + b4)
                    ins = None
                    for u in range(8):
                        t = b4 * 8 + u
                        ins = te.transpose(TSTG[:, u * 128:(u + 1) * 128],
                                           src[:, t * 128:(t + 1) * 128],
                                           identb[:])
                    PE.inc(ins)
        assert PE.n == pe_tr0 + 8

        @block.vector
        def _(v):
            for which in range(2):
                dstT = (QT, KT)[which]
                for b4 in range(4):
                    v.wait_ge(PE.h, pe_tr0 + which * 4 + b4 + 1)
                    CP.inc(v.tensor_copy(dstT[:, b4 * 1024:(b4 + 1) * 1024],
                                         TSTG))
        assert CP.n == cp0 + 8

        pe_i0, act_i0, dv_i0 = PE.n, ACT.n, DV.n
        peit = lambda it: pe_i0 + 24 * it
        actit = lambda it: act_i0 + 22 * it
        dvit = lambda it: dv_i0 + 6 * it

        @block.tensor
        def _(te):
            te.wait_ge(CP.h, cp0 + 8)
            for it in range(8):
                if it >= 1:
                    te.wait_ge(DV.h, dvit(it - 1) + 6)
                for kg in range(NKG):
                    if kg >= 1:
                        te.wait_ge(ACT.h, actit(it) + 2 * kg)
                    i1 = None
                    for j in range(KGS[kg]):
                        jc = KGO[kg] + j
                        te.matmul(PS_S1[:, j * 512:(j + 1) * 512],
                                  KT[0:64, jc * 128:(jc + 1) * 128],
                                  QT[0:64, it * 512:(it + 1) * 512],
                                  start=True, stop=True, tile_position=(0, 0),
                                  skip_group_check=True)
                        i1 = te.matmul(PS_S2[:, j * 512:(j + 1) * 512],
                                       KT[64:128, jc * 128:(jc + 1) * 128],
                                       QT[64:128, it * 512:(it + 1) * 512],
                                       start=True, stop=True,
                                       tile_position=(64, 0),
                                       skip_group_check=True)
                    PE.inc(i1)
                    te.wait_ge(ACT.h, actit(it) + 2 * (kg + 1))
                    ins = None
                    for j in range(KGS[kg]):
                        jc = KGO[kg] + j
                        st = (kg == 0 and j == 0)
                        sp = (kg == NKG - 1 and j == KGS[kg] - 1)
                        te.matmul(PVA[0:65, :], GVE[:, jc * 130:jc * 130 + 65],
                                  PTSA[:, j * 512:(j + 1) * 512],
                                  start=st, stop=sp, skip_group_check=True)
                        ins = te.matmul(PVB[0:65, :],
                                        GVE[:, jc * 130 + 65:jc * 130 + 130],
                                        PTSB[:, j * 512:(j + 1) * 512],
                                        start=st, stop=sp,
                                        skip_group_check=True)
                    PE.inc(ins)
                te.wait_ge(DV.h, dvit(it) + 1)
                PE.inc(te.matmul(PS_S2[0:64, 0:512], ONER[0:1, 0:64],
                                 DNI[0:1, 0:512], start=True, stop=True,
                                 skip_group_check=True))
                te.wait_ge(DV.h, dvit(it) + 2)
                PE.inc(te.matmul(PS_S2[0:64, 512:1024], ONER[0:1, 0:64],
                                 DNI[0:1, 512:1024], start=True, stop=True,
                                 skip_group_check=True))
        assert PE.n == pe_i0 + 24 * 8

        @block.scalar
        def _(sc_):
            for it in range(8):
                for kg in range(NKG):
                    w = KGS[kg] * 512
                    sc_.wait_ge(PE.h, peit(it) + 2 * kg + 1)
                    ACT.inc(sc_.activation(PTSA[:, 0:w], PS_S1[:, 0:w],
                                           AF.Exp, scale=scale))
                    ACT.inc(sc_.activation(PTSB[:, 0:w], PS_S2[:, 0:w],
                                           AF.Exp, scale=scale))
        assert ACT.n == act_i0 + 22 * 8

        od = DmaStream(p, "od%d" % g, 2)

        @block.vector
        def _(v):
            for it in range(8):
                v.wait_ge(PE.h, peit(it) + 22)
                if it >= 1:
                    for tt in (2 * (it - 1), 2 * (it - 1) + 1):
                        od.wait(v, tt)
                DV.inc(v.reciprocal(DNI[0:1, 0:512], PVA[64:65, :]))
                v.wait_ge(PE.h, peit(it) + 23)
                DV.inc(v.reciprocal(DNI[0:1, 512:1024], PVB[64:65, :]))
                v.wait_ge(PE.h, peit(it) + 23)
                DV.inc(v.tensor_copy(BCC[:, 0:512], PS_S2[:, 0:512]))
                v.wait_ge(PE.h, peit(it) + 24)
                DV.inc(v.tensor_copy(BCC[:, 512:1024], PS_S2[:, 512:1024]))
                v.wait_ge(DV.h, DV.n)
                DV.inc(v.tensor_tensor(OUTB[0:64, 0:512], PVA[0:64, :],
                                       BCC[0:64, 0:512], ALU.mult))
                DV.inc(v.tensor_tensor(OUTB[0:64, 512:1024], PVB[0:64, :],
                                       BCC[0:64, 512:1024], ALU.mult))
        assert DV.n == dv_i0 + 6 * 8

        @block.gpsimd
        def _(gp):
            for it in range(8):
                gp.wait_ge(DV.h, dvit(it) + 6)
                od.issue(gp.dma_start(
                    out=ODR[g, 0:64, it * 512:(it + 1) * 512],
                    in_=OUTB[0:64, 0:512]))
                od.issue(gp.dma_start(
                    out=ODR[g, 64:128, it * 512:(it + 1) * 512],
                    in_=OUTB[0:64, 512:1024]))
            if g + 2 < NCORE:
                _issue_wg(gp, g + 2)

        @block.vector
        def _(v):
            for tt in (14, 15):
                od.wait(v, tt)
            GRP.inc(v.engine_nop())

    pe_o0, dv_o0 = PE.n, DV.n
    oti = DmaStream(p, "oti", 2)
    ldr = DmaStream(p, "br", 2)
    stx = DmaStream(p, "bst", 2)

    @block.sync
    def _(sync):
        sync.wait_ge(GRP.h, NCORE)
        for t in range(NT):
            if t >= 2:
                sync.wait_ge(PE.h, pe_o0 + (t - 1))
            oti.issue(sync.dma_start(
                out=OTI[:, (t % 2) * NC * 128:(t % 2 + 1) * NC * 128]
                    .rearrange("p (c x) -> p c x", c=NC),
                in_=ODR[:, :, t * 128:(t + 1) * 128].transpose([1, 0, 2])))
            if t >= 2:
                sync.wait_ge(DV.h, dv_o0 + 2 * (t - 1))
            ldr.issue(sync.dma_start(out=SXA[:, (t % 2) * D:(t % 2 + 1) * D],
                                     in_=Xap[t * 128:(t + 1) * 128, :]))

    @block.tensor
    def _(te):
        for t in range(NT):
            oti.wait(te, t)
            if t >= 2:
                te.wait_ge(DV.h, dv_o0 + 2 * (t - 1))
            pdst = (PS_S1 if t % 2 == 0 else PS_S2)
            ins = None
            for nh in range(2):
                for cc in range(NC):
                    ins = te.matmul(
                        pdst[:, nh * 512:(nh + 1) * 512],
                        OTI[:, (t % 2) * NC * 128 + cc * 128:(t % 2) * NC * 128 + (cc + 1) * 128],
                        WOB[:, cc * D + nh * 512:cc * D + (nh + 1) * 512],
                        start=(cc == 0), stop=(cc == NC - 1),
                        skip_group_check=True)
            PE.inc(ins)
    assert PE.n == pe_o0 + NT

    @block.vector
    def _(v):
        for t in range(NT):
            v.wait_ge(PE.h, pe_o0 + t + 1)
            ldr.wait(v, t)
            if t >= 2:
                stx.wait(v, t - 2)
            psrc = (PS_S1 if t % 2 == 0 else PS_S2)
            ob = (t % 2) * D
            DV.inc(v.tensor_tensor(SCR8[:, ob:ob + D], psrc[:, 0:1024],
                                   BOUTB[:], ALU.add))
            v.wait_ge(DV.h, DV.n)
            DV.inc(v.tensor_tensor(SCR8[:, ob:ob + D], SCR8[:, ob:ob + D],
                                   SXA[:, ob:ob + D], ALU.add))
    assert DV.n == dv_o0 + 2 * NT

    @block.gpsimd
    def _(gp):
        for t in range(NT):
            gp.wait_ge(DV.h, dv_o0 + 2 * t + 2)
            stx.issue(gp.dma_start(out=XOap[t * 128:(t + 1) * 128, :],
                                   in_=SCR8[:, (t % 2) * D:(t % 2 + 1) * D]))
        for t in (NT - 2, NT - 1):
            stx.wait(gp, t)


def build_attn_b():
    p = Prog("attnb")
    nc = p.nc
    X = nc.dram_tensor("X", [S, D], F32, kind="ExternalInput").ap()
    WQKV = nc.dram_tensor("WQKV", [D, 8 * 384], F32, kind="ExternalInput").ap()
    WOUT = nc.dram_tensor("WOUT", [D, D], F32, kind="ExternalInput").ap()
    BOUT = nc.dram_tensor("BOUT", [1, D], F32, kind="ExternalInput").ap()
    COSF = nc.dram_tensor("COSF", [128, 512], F32, kind="ExternalInput").ap()
    SINF = nc.dram_tensor("SINF", [128, 512], F32, kind="ExternalInput").ap()
    XO = nc.dram_tensor("XO", [S, D], F32, kind="ExternalOutput").ap()
    with nc.Block() as block:
        emit_attn_b(p, block, X, XO, WQKV, WOUT, BOUT, COSF, SINF)
    return p


def rope_tables_l0b():
    inv = 1.0 / (10000.0 ** (np.arange(0, 32, 2, dtype=np.float64) / 32.0))
    ang = np.arange(S, dtype=np.float64)[:, None] * inv[None, :]
    cos = np.cos(ang).astype(np.float32)
    sin = np.sin(ang).astype(np.float32)
    cosf = cos.reshape(NT, 128, 16).transpose(1, 0, 2).reshape(128, NT * 16)
    sinf = sin.reshape(NT, 128, 16).transpose(1, 0, 2).reshape(128, NT * 16)
    return np.ascontiguousarray(cosf), np.ascontiguousarray(sinf)


def prep_attn_b_inputs(x_b, Wq, Wkv, Wout, bout):
    Wq = np.asarray(Wq)
    Wkv = np.asarray(Wkv)
    groups = []
    for g in range(8):
        groups.append(Wq[:, g * 128:(g + 1) * 128])
        groups.append(Wkv[:, g * 128:(g + 1) * 128])
        groups.append(Wkv[:, 1024 + g * 128:1024 + (g + 1) * 128])
    wqkv = np.concatenate(groups, axis=1)
    cosf, sinf = rope_tables_l0b()
    return {
        "X": np.ascontiguousarray(x_b),
        "WQKV": np.ascontiguousarray(wqkv),
        "WOUT": np.ascontiguousarray(np.asarray(Wout)),
        "BOUT": np.ascontiguousarray(np.asarray(bout).reshape(1, D)),
        "COSF": cosf, "SINF": sinf,
    }




def emit_attn_l1a(p, block, Xap, XOap, wqt_d, wkt_d, wvt_d, wot_d,
                  bq_d, bk_d, bv_d, bo_d, crow_d, srow_d, scale=0.125):
    nc = p.nc
    LDC, LDM, LDI, DV, ACT, PE, CP = (
        p.sem("LDC"), p.sem("LDM"), p.sem("LDI"), p.sem("DV"), p.sem("ACT"),
        p.sem("PE"), p.sem("CP"))
    HGD = p.sem("HGD")

    identb = p.sbuf("identb", [128, 128], BF16)
    idb_d = nc.inline_tensor(np.eye(128).astype(np.float32), "idb_d")
    XLNT = p.sbuf("xlnt", [128, NC * S], BF16)
    WQT = p.sbuf("wqt", [128, NC * D], BF16)
    WKT = p.sbuf("wkt", [128, NC * D], BF16)
    WVT = p.sbuf("wvt", [128, NC * D], BF16)
    BB = p.sbuf("bb", [128, 4 * D], BF16)
    BR = p.sbuf("br_", [1, D], F32)
    CREPC = p.sbuf("crepc", [128, 32], F32)
    SREPC = p.sbuf("srepc", [128, 32], F32)
    CSR = p.sbuf("csr", [1, 64], F32)
    ONER = p.sbuf("oner", [1, 128], F32)
    PREB = p.sbuf("preb", [128, 2 * D], BF16)
    QR = p.sbuf("qr", [128, 2 * D], BF16)
    KR = p.sbuf("kr", [128, 2 * D], BF16)
    KTT = p.sbuf("ktt", [128, 2048], BF16)
    QD8 = p.sbuf("qd8", [128, 1024], BF16)
    GVE2 = p.sbuf("gve2", [128, 32 * 65], BF16)
    PTSA = p.sbuf("ptsa", [128, 1536], BF16)
    PTSB = p.sbuf("ptsb", [128, 1536], BF16)
    DNI = p.sbuf("dni", [1, 512], F32)
    BCC = p.sbuf("bcc", [128, 512], F32)
    OUTB = p.sbuf("outb", [128, 512], BF16)
    OUTH = p.sbuf("outh", [64, S], BF16)
    OTI = p.sbuf("oti", [128, NC * 128], BF16)
    WOT = p.sbuf("wot", [128, NC * D], BF16)
    SXA = p.sbuf("sxa", [128, 2 * D], F32)
    SCR8 = p.sbuf("scr8", [128, 2 * D], F32)
    QTT = SXA[:].bitcast(BF16)

    PS_S1 = p.psum("ps_s1", [128, 1536], F32)
    PS_S2 = p.psum("ps_s2", [128, 1536], F32)
    PS_P = p.psum("ps_p", [128, 1024], F32)
    LNPT = PS_S1[:, 0:1024].bitcast(BF16)
    PJ = PS_P
    PVH = PS_P[:, 0:512]
    NBC = PS_P[:, 512:1024]
    TSTG = PS_S2[:, 0:512].bitcast(BF16)

    ODR = nc.dram_tensor("odr", [NC, 128, S], BF16, kind="Internal").ap()

    @block.gpsimd
    def _(gp):
        LDI.inc(gp.dma_start(out=identb[:], in_=idb_d.ap()), 16)
        for w_d, w_sb in ((wqt_d, WQT), (wkt_d, WKT), (wvt_d, WVT),
                          (wot_d, WOT)):
            LDC.inc(gp.dma_start(
                out=w_sb[:].rearrange("p (c d) -> p c d", c=NC),
                in_=w_d.rearrange("(c p) d -> p c d", p=128)), 16)
    ldc_all, ldi_all = LDC.n, LDI.n

    brs = DmaStream(p, "brs", 1)
    pe_pre0, dv_pre0 = PE.n, DV.n

    @block.sync
    def _(sync):
        LDM.inc(sync.dma_start(out=CSR[0:1, 0:32], in_=crow_d), 16)
        LDM.inc(sync.dma_start(out=CSR[0:1, 32:64], in_=srow_d), 16)
        for i, b_d in enumerate((bq_d, bk_d, bv_d, bo_d)):
            if i >= 1:
                sync.wait_ge(PE.h, pe_pre0 + i)
            brs.issue(sync.dma_start(out=BR[0:1, :], in_=b_d))
    ldm_all = LDM.n

    @block.vector
    def _(v):
        DV.inc(v.memset(ONER[:], 1.0))
        ones_view = GVE2.ap().rearrange("p (j e) -> p j e", j=32)[:, :, 64:65]
        DV.inc(v.memset(ones_view, 1.0))

    @block.tensor
    def _(te):
        te.wait_ge(LDM.h, ldm_all)
        te.wait_ge(DV.h, dv_pre0 + 1)
        for i in range(4):
            brs.wait(te, i)
            ins = None
            for nh in range(2):
                ins = te.matmul(
                    PS_S2[:, nh * 512:(nh + 1) * 512],
                    ONER[:], BR[0:1, nh * 512:(nh + 1) * 512],
                    start=True, stop=True)
            PE.inc(ins)
            te.wait_ge(DV.h, dv_pre0 + 3 + i)
        PE.inc(te.matmul(PS_P[:, 0:64], ONER[:], CSR[0:1, 0:64],
                         start=True, stop=True))

    @block.vector
    def _(v):
        for i in range(4):
            v.wait_ge(PE.h, pe_pre0 + i + 1)
            DV.inc(v.tensor_copy(BB[:, i * D:(i + 1) * D], PS_S2[:, 0:1024]))
        v.wait_ge(PE.h, pe_pre0 + 5)
        DV.inc(v.tensor_copy(CREPC[:], PS_P[:, 0:32]))
        DV.inc(v.tensor_copy(SREPC[:], PS_P[:, 32:64]))
    dv_pre2 = DV.n

    emit_ln(p, block, Xap, xlnt=XLNT, cst=S, off=0, pre="lnc",
            identb=identb, identb_ready=(LDI, ldi_all),
            sx=SXA, sq=SCR8, pt=LNPT)
    cp_ln = CP.n

    KGS = [3, 3, 3, 3, 3, 1]
    KGO = [0, 3, 6, 9, 12, 15]
    NKG = 6

    for h in range(16):
        pe0, dv0, act0, cp0 = PE.n, DV.n, ACT.n, CP.n
        tok0 = h * 256
        DVSTEP = [1, 2, 9, 10, 17, 18]

        @block.tensor
        def _(te):
            te.wait_ge(CP.h, cp_ln)
            te.wait_ge(LDC.h, ldc_all)
            if h >= 1:
                te.wait_ge(HGD.h, h)
            for wi in range(3):
                wsb = (WQT, WKT, WVT)[wi]
                for tl in range(2):
                    if wi + tl > 0:
                        te.wait_ge(DV.h, dv0 + DVSTEP[2 * wi + tl - 1])
                    ins = None
                    for nh in range(2):
                        for cc in range(NC):
                            ins = te.matmul(
                                PJ[:, nh * 512:(nh + 1) * 512],
                                XLNT[:, cc * S + tok0 + tl * 128:
                                     cc * S + tok0 + (tl + 1) * 128],
                                wsb[:, cc * D + nh * 512:cc * D + (nh + 1) * 512],
                                start=(cc == 0), stop=(cc == NC - 1),
                                skip_group_check=True)
                    PE.inc(ins)

        @block.vector
        def _(v):
            def evac(wi):
                bofs = (0, D, 2 * D)[wi]
                for tl in range(2):
                    v.wait_ge(PE.h, pe0 + 2 * wi + tl + 1)
                    if wi + tl > 0:
                        v.wait_ge(DV.h, DV.n)
                    DV.inc(v.tensor_tensor(PREB[:, tl * D:(tl + 1) * D],
                                           PJ[:, 0:1024],
                                           BB[:, bofs:bofs + D], ALU.add))

            def rope(srcT, dstT):
                sv = lambda two: srcT.ap().rearrange(
                    "p (tl g i two) -> p tl g i two", tl=2, g=16, two=2)[
                    :, :, :, :, two:two + 1]
                dv_ = lambda two: dstT.ap().rearrange(
                    "p (tl g i two) -> p tl g i two", tl=2, g=16, two=2)[
                    :, :, :, :, two:two + 1]
                cb = CREPC[:].unsqueeze(1).unsqueeze(2).broadcast_to(
                    [128, 2, 16, 32]).unsqueeze(4)
                sb = SREPC[:].unsqueeze(1).unsqueeze(2).broadcast_to(
                    [128, 2, 16, 32]).unsqueeze(4)
                r1 = SCR8[:, 0:1024].rearrange("p (tl g i) -> p tl g i",
                                               tl=2, g=16).unsqueeze(4)
                r2 = SCR8[:, 1024:2048].rearrange("p (tl g i) -> p tl g i",
                                                  tl=2, g=16).unsqueeze(4)
                v.wait_ge(DV.h, DV.n)
                DV.inc(v.tensor_tensor(r1, sv(0), cb, ALU.mult))
                DV.inc(v.tensor_tensor(r2, sv(1), sb, ALU.mult))
                v.wait_ge(DV.h, DV.n)
                DV.inc(v.tensor_tensor(dv_(0), r1, r2, ALU.subtract))
                v.wait_ge(DV.h, DV.n)
                DV.inc(v.tensor_tensor(r1, sv(1), cb, ALU.mult))
                DV.inc(v.tensor_tensor(r2, sv(0), sb, ALU.mult))
                v.wait_ge(DV.h, DV.n)
                DV.inc(v.tensor_tensor(dv_(1), r1, r2, ALU.add))
                v.wait_ge(DV.h, DV.n)
            evac(0)
            rope(PREB, QR)
            evac(1)
            rope(PREB, KR)
            evac(2)
            v.wait_ge(DV.h, DV.n)
            for rhalf in range(2):
                dst = GVE2.ap().rearrange("p (g two e) -> p g two e",
                                          g=16, two=2)[:, :, rhalf:rhalf + 1,
                                                       0:64]
                src = PREB[:, rhalf * D:(rhalf + 1) * D].rearrange(
                    "p (g o e) -> p g o e", g=16, o=1)
                DV.inc(v.tensor_copy(dst, src))
        assert DV.n == dv0 + 6 + 12 + 2
        dv_prep_end = DV.n
        pe_tr0 = PE.n

        @block.tensor
        def _(te):
            te.wait_ge(DV.h, dv_prep_end)
            for b8 in range(2):
                if b8 > 0:
                    te.wait_ge(CP.h, cp0 + b8)
                ins = None
                for cc in range(NC):
                    ins = te.transpose(
                        TSTG[:, cc * 128:(cc + 1) * 128],
                        KR[:, b8 * D + cc * 128:b8 * D + (cc + 1) * 128],
                        identb[:])
                PE.inc(ins)
            for qb in range(4):
                te.wait_ge(CP.h, cp0 + 3 + 2 * qb)
                ins = None
                for cc in range(NC):
                    ins = te.transpose(TSTG[:, cc * 128:(cc + 1) * 128],
                                       QD8[:, cc * 128:(cc + 1) * 128],
                                       identb[:])
                PE.inc(ins)

        @block.vector
        def _(v):
            for b8 in range(2):
                v.wait_ge(PE.h, pe_tr0 + b8 + 1)
                dst = KTT.ap().rearrange("p (cc two x) -> p cc two x",
                                         cc=NC, two=2)[:, :, b8:b8 + 1, :]
                src = TSTG.rearrange("p (cc x) -> p cc x", cc=NC).unsqueeze(2)
                CP.inc(v.tensor_copy(dst, src))
            for qb in range(4):
                pi, rhalf = qb // 2, qb % 2
                dstd = QD8[:].rearrange("p (cc two e) -> p cc two e",
                                        cc=NC, two=2)
                srcd = QR[:, rhalf * D:(rhalf + 1) * D].rearrange(
                    "p (gh two e) -> p gh two e", gh=8, two=2)[
                    :, :, pi:pi + 1, :].squeeze(2) \
                    .unsqueeze(2).broadcast_to([128, 8, 2, 64])
                v.wait_ge(PE.h, pe_tr0 + 2 + qb)
                CP.inc(v.tensor_copy(dstd, srcd))
                v.wait_ge(PE.h, pe_tr0 + 2 + qb + 1)
                dst = QTT.rearrange("p (pp jb x) -> p pp jb x",
                                    pp=2, jb=16)[:, pi:pi + 1, :, :] \
                    .rearrange("p pp (ccq rh) x -> p pp ccq rh x", rh=2)[
                    :, :, :, rhalf:rhalf + 1, :]
                src = TSTG.rearrange("p (cc x) -> p cc x", cc=NC) \
                    .unsqueeze(1).unsqueeze(3)
                CP.inc(v.tensor_copy(dst, src))
        assert CP.n == cp0 + 10

        pe_i0, act_i0, dv_i0 = PE.n, ACT.n, DV.n
        peib = lambda k: pe_i0 + 13 * k
        actib = lambda k: act_i0 + 12 * k
        dvib = lambda k: dv_i0 + 5 * k

        @block.tensor
        def _(te):
            te.wait_ge(CP.h, cp0 + 10)
            for k in range(8):
                pi, ib = k // 4, k % 4
                if k >= 1:
                    te.wait_ge(DV.h, dvib(k - 1) + 5)
                for kg in range(NKG):
                    if kg >= 1:
                        te.wait_ge(ACT.h, actib(k) + 2 * kg)
                    i1 = None
                    for j in range(KGS[kg]):
                        jb = KGO[kg] + j
                        qrhs0 = QTT[0:64, pi * 2048 + ib * 512:
                                    pi * 2048 + (ib + 1) * 512]
                        qrhs1 = QTT[64:128, pi * 2048 + ib * 512:
                                    pi * 2048 + (ib + 1) * 512]
                        te.matmul(PS_S1[:, j * 512:(j + 1) * 512],
                                  KTT[0:64, jb * 128:(jb + 1) * 128],
                                  qrhs0,
                                  start=True, stop=True, tile_position=(0, 0),
                                  skip_group_check=True)
                        i1 = te.matmul(PS_S2[:, j * 512:(j + 1) * 512],
                                       KTT[64:128, jb * 128:(jb + 1) * 128],
                                       qrhs1,
                                       start=True, stop=True,
                                       tile_position=(64, 0),
                                       skip_group_check=True)
                    PE.inc(i1)
                    te.wait_ge(ACT.h, actib(k) + 2 * (kg + 1))
                    ins = None
                    for j in range(KGS[kg]):
                        jb = KGO[kg] + j
                        cc, rhalf = jb // 2, jb % 2
                        st = (kg == 0 and j == 0)
                        sp = (kg == NKG - 1 and j == KGS[kg] - 1)
                        jc_e = (2 * cc) * 2 + rhalf
                        jc_o = (2 * cc + 1) * 2 + rhalf
                        te.matmul(PVH[0:65, :],
                                  GVE2[:, jc_e * 65:jc_e * 65 + 65],
                                  PTSA[:, j * 512:(j + 1) * 512],
                                  start=st, stop=False, skip_group_check=True)
                        ins = te.matmul(PVH[0:65, :],
                                        GVE2[:, jc_o * 65:jc_o * 65 + 65],
                                        PTSB[:, j * 512:(j + 1) * 512],
                                        start=False, stop=sp,
                                        skip_group_check=True)
                    PE.inc(ins)
                te.wait_ge(DV.h, dvib(k) + 1)
                PE.inc(te.matmul(NBC[0:64, :], ONER[0:1, 0:64], DNI[0:1, :],
                                 start=True, stop=True, skip_group_check=True))
        assert PE.n == pe_i0 + 13 * 8

        @block.scalar
        def _(sc_):
            for k in range(8):
                for kg in range(NKG):
                    w = KGS[kg] * 512
                    sc_.wait_ge(PE.h, peib(k) + 2 * kg + 1)
                    ACT.inc(sc_.activation(PTSA[:, 0:w], PS_S1[:, 0:w],
                                           AF.Exp, scale=scale))
                    ACT.inc(sc_.activation(PTSB[:, 0:w], PS_S2[:, 0:w],
                                           AF.Exp, scale=scale))
        assert ACT.n == act_i0 + 12 * 8

        @block.vector
        def _(v):
            for k in range(8):
                pi, ib = k // 4, k % 4
                v.wait_ge(PE.h, peib(k) + 12)
                DV.inc(v.reciprocal(DNI[0:1, :], PVH[64:65, :]))
                v.wait_ge(PE.h, peib(k) + 13)
                DV.inc(v.tensor_copy(BCC[:, 0:512], NBC[:, :]))
                v.wait_ge(DV.h, DV.n)
                DV.inc(v.tensor_tensor(OUTB[0:64, :], PVH[0:64, :],
                                       BCC[0:64, 0:512], ALU.mult))
                v.wait_ge(DV.h, DV.n)
                dst2 = OUTH.ap().rearrange(
                    "p (rh rr cl two) -> p rh rr cl two",
                    rh=2, rr=128, two=2)[
                    :, :, :, 2 * ib:2 * ib + 2, pi:pi + 1] \
                    .transpose([0, 3, 1, 2, 4])
                src = OUTB[0:64, :].rearrange(
                    "p (cl rh rr) -> p cl rh rr", cl=2, rh=2).unsqueeze(4)
                DV.inc(v.tensor_copy(dst2, src))
                DV.inc(v.engine_nop())
        assert DV.n == dv_i0 + 5 * 8

        odh = DmaStream(p, "od%d" % h, 1)

        @block.gpsimd
        def _(gp):
            gp.wait_ge(DV.h, dv_i0 + 5 * 8)
            odh.issue(gp.dma_start(
                out=ODR[h // 2, (h % 2) * 64:(h % 2) * 64 + 64, :],
                in_=OUTH[0:64, :]))

        @block.vector
        def _(v):
            odh.wait(v, 0)
            HGD.inc(v.engine_nop())

    pe_o0, dv_o0 = PE.n, DV.n
    oti = DmaStream(p, "oti", 1)
    ldr = DmaStream(p, "ldr", 2)
    stx = DmaStream(p, "stx", 2)

    @block.sync
    def _(sync):
        sync.wait_ge(HGD.h, 16)
        for t in range(NT):
            if t >= 1:
                sync.wait_ge(PE.h, pe_o0 + t)
            oti.issue(sync.dma_start(
                out=OTI[:].rearrange("p (c x) -> p c x", c=NC),
                in_=ODR[:, :, t * 128:(t + 1) * 128].transpose([1, 0, 2])))
            if t >= 2:
                sync.wait_ge(DV.h, dv_o0 + 2 * (t - 1))
            ldr.issue(sync.dma_start(out=SXA[:, (t % 2) * D:(t % 2 + 1) * D],
                                     in_=Xap[t * 128:(t + 1) * 128, :]))

    @block.tensor
    def _(te):
        for t in range(NT):
            oti.wait(te, t)
            if t >= 2:
                te.wait_ge(DV.h, dv_o0 + 2 * (t - 1))
            pdst = (PS_S1 if t % 2 == 0 else PS_S2)
            ins = None
            for nh in range(2):
                for cc in range(NC):
                    ins = te.matmul(
                        pdst[:, nh * 512:(nh + 1) * 512],
                        OTI[:, cc * 128:(cc + 1) * 128],
                        WOT[:, cc * D + nh * 512:cc * D + (nh + 1) * 512],
                        start=(cc == 0), stop=(cc == NC - 1),
                        skip_group_check=True)
            PE.inc(ins)
    assert PE.n == pe_o0 + NT

    @block.vector
    def _(v):
        for t in range(NT):
            v.wait_ge(PE.h, pe_o0 + t + 1)
            ldr.wait(v, t)
            if t >= 2:
                stx.wait(v, t - 2)
            psrc = (PS_S1 if t % 2 == 0 else PS_S2)
            ob = (t % 2) * D
            DV.inc(v.tensor_tensor(SCR8[:, ob:ob + D], psrc[:, 0:1024],
                                   BB[:, 3 * D:4 * D], ALU.add))
            v.wait_ge(DV.h, DV.n)
            DV.inc(v.tensor_tensor(SCR8[:, ob:ob + D], SCR8[:, ob:ob + D],
                                   SXA[:, ob:ob + D], ALU.add))
    assert DV.n == dv_o0 + 2 * NT

    @block.gpsimd
    def _(gp):
        for t in range(NT):
            gp.wait_ge(DV.h, dv_o0 + 2 * t + 2)
            stx.issue(gp.dma_start(out=XOap[t * 128:(t + 1) * 128, :],
                                   in_=SCR8[:, (t % 2) * D:(t % 2 + 1) * D]))
        for t in (NT - 2, NT - 1):
            stx.wait(gp, t)


def build_attn_l1a():
    p = Prog("attnl1a")
    nc = p.nc
    X = nc.dram_tensor("X", [S, D], F32, kind="ExternalInput").ap()
    WQT = nc.dram_tensor("WQT", [D, D], F32, kind="ExternalInput").ap()
    WKT = nc.dram_tensor("WKT", [D, D], F32, kind="ExternalInput").ap()
    WVT = nc.dram_tensor("WVT", [D, D], F32, kind="ExternalInput").ap()
    WOT = nc.dram_tensor("WOT", [D, D], F32, kind="ExternalInput").ap()
    BQ = nc.dram_tensor("BQ", [1, D], F32, kind="ExternalInput").ap()
    BK = nc.dram_tensor("BK", [1, D], F32, kind="ExternalInput").ap()
    BV = nc.dram_tensor("BV", [1, D], F32, kind="ExternalInput").ap()
    BO = nc.dram_tensor("BO", [1, D], F32, kind="ExternalInput").ap()
    CROW = nc.dram_tensor("CROW", [1, 32], F32, kind="ExternalInput").ap()
    SROW = nc.dram_tensor("SROW", [1, 32], F32, kind="ExternalInput").ap()
    XO = nc.dram_tensor("XO", [S, D], F32, kind="ExternalOutput").ap()
    with nc.Block() as block:
        emit_attn_l1a(p, block, X, XO, WQT, WKT, WVT, WOT,
                      BQ, BK, BV, BO, CROW, SROW)
    return p


def prep_attn_l1a_inputs(x_b, wq, bq, wk, bk, wv, bv, wo, bo,
                         rot_cos, rot_sin):
    r = lambda a: np.ascontiguousarray(np.asarray(a))
    return {
        "X": r(x_b),
        "WQT": r(np.asarray(wq).T), "WKT": r(np.asarray(wk).T),
        "WVT": r(np.asarray(wv).T), "WOT": r(np.asarray(wo).T),
        "BQ": r(np.asarray(bq).reshape(1, D)),
        "BK": r(np.asarray(bk).reshape(1, D)),
        "BV": r(np.asarray(bv).reshape(1, D)),
        "BO": r(np.asarray(bo).reshape(1, D)),
        "CROW": r(np.asarray(rot_cos)[S - 1].reshape(1, 32)),
        "SROW": r(np.asarray(rot_sin)[S - 1].reshape(1, 32)),
    }




def emit_attn_l1b(p, block, Xap, XOap, wqkvt_d, wot_d):
    nc = p.nc
    LDC, LDI, DV, ACT, PE, CP = (
        p.sem("LDC"), p.sem("LDI"), p.sem("DV"), p.sem("ACT"),
        p.sem("PE"), p.sem("CP"))
    HGD = p.sem("HGD")

    identb = p.sbuf("identb", [128, 128], BF16)
    idb_d = nc.inline_tensor(np.eye(128).astype(np.float32), "idb_d")
    XLNT = p.sbuf("xlnt", [128, NC * S], BF16)
    WQKV = p.sbuf("wqkv", [128, NC * 384], BF16)
    WOT = p.sbuf("wot", [128, NC * D], BF16)
    QKVR = p.sbuf("qkvr", [128, NT * 192], BF16)
    QSB = p.sbuf("qsb", [128, NT * 64], BF16)
    KEB = p.sbuf("keb", [128, NT * 64], BF16)
    VEB = p.sbuf("veb", [128, NT * 65], BF16)
    QT = p.sbuf("qt", [64, S], BF16)
    SEXT = p.sbuf("sext", [64, 2 * 65], BF16)
    OA = p.sbuf("oa", [65, S], BF16)
    RECB = p.sbuf("recb", [128, NT], F32)
    EXPQ = p.sbuf("expq", [128, NT * 64], F32)
    DNI = p.sbuf("dni", [1, 512], F32)
    BCC = p.sbuf("bcc", [128, 512], F32)
    OUT64 = p.sbuf("out64", [64, S], BF16)
    ONER = p.sbuf("oner", [1, 128], F32)
    OTI = p.sbuf("oti", [128, NC * 128], BF16)
    SXA = p.sbuf("sxa", [128, 2 * D], F32)
    SCR8 = p.sbuf("scr8", [128, 2 * D], F32)

    PS_S1 = p.psum("ps_s1", [128, 1536], F32)
    PS_S2 = p.psum("ps_s2", [128, 1536], F32)
    PS_P = p.psum("ps_p", [128, 1024], F32)
    LNPT = PS_S1[:, 0:1024].bitcast(BF16)
    PJ = PS_P
    CTX = PS_S1[0:64, 1024:1089]
    OAP = PS_S2[0:65, 1024:1536]
    TSTG = PS_S1[:, 0:512].bitcast(BF16)
    NBC = PS_S2[0:64, 0:512]

    ODR = nc.dram_tensor("odr", [NC, 128, S], BF16, kind="Internal").ap()

    @block.gpsimd
    def _(gp):
        LDI.inc(gp.dma_start(out=identb[:], in_=idb_d.ap()), 16)
        LDC.inc(gp.dma_start(
            out=WOT[:].rearrange("p (c d) -> p c d", c=NC),
            in_=wot_d.rearrange("(c p) d -> p c d", p=128)), 16)
    ldc_all, ldi_all = LDC.n, LDI.n

    pe_pre0, dv_pre0 = PE.n, DV.n

    @block.vector
    def _(v):
        DV.inc(v.memset(ONER[:], 1.0))
        ones_view = VEB.ap().rearrange("p (t e) -> p t e", t=NT)[:, :, 64:65]
        DV.inc(v.memset(ones_view, 1.0))

    emit_ln(p, block, Xap, xlnt=XLNT, cst=S, off=0, pre="lnd",
            identb=identb, identb_ready=(LDI, ldi_all),
            sx=SXA, sq=SCR8, pt=LNPT)
    cp_ln = CP.n

    wg = DmaStream(p, "bw", 2)

    def _issue_wg(gp, h):
        wg.issue(gp.dma_start(
            out=WQKV[:, (h % 2) * NC * 192:(h % 2) * NC * 192 + NC * 192]
                .rearrange("p (c d) -> p c d", c=NC),
            in_=wqkvt_d[:, h * 192:(h + 1) * 192]
                .rearrange("(c p) d -> p c d", p=128)))

    @block.gpsimd
    def _(gp):
        _issue_wg(gp, 0)
        _issue_wg(gp, 1)

    for h in range(16):
        pe0, dv0, act0, cp0 = PE.n, DV.n, ACT.n, CP.n
        wslot = (h % 2) * NC * 192

        @block.tensor
        def _(te):
            te.wait_ge(CP.h, cp_ln)
            wg.wait(te, h)
            if h >= 1:
                te.wait_ge(HGD.h, h)
            for t in range(NT):
                if t >= 2:
                    te.wait_ge(DV.h, dv0 + (t - 1))
                ins = None
                for cc in range(NC):
                    ins = te.matmul(
                        PJ[:, (t % 2) * 512:(t % 2) * 512 + 192],
                        XLNT[:, cc * S + t * 128:cc * S + (t + 1) * 128],
                        WQKV[:, wslot + cc * 192:wslot + (cc + 1) * 192],
                        start=(cc == 0), stop=(cc == NC - 1),
                        skip_group_check=True)
                PE.inc(ins)

        @block.vector
        def _(v):
            for t in range(NT):
                v.wait_ge(PE.h, pe0 + t + 1)
                DV.inc(v.tensor_copy(QKVR[:, t * 192:(t + 1) * 192],
                                     PJ[:, (t % 2) * 512:(t % 2) * 512 + 192]))

        @block.scalar
        def _(sc_):
            sc_.wait_ge(DV.h, dv0 + NT)
            ACT.inc(sc_.activation(
                EXPQ[:].rearrange("p (t e) -> p t e", t=NT),
                QKVR[:].rearrange("p (t w) -> p t w", t=NT)[:, :, 0:64],
                AF.Exp))
            ACT.inc(sc_.activation(
                KEB[:].rearrange("p (t e) -> p t e", t=NT),
                QKVR[:].rearrange("p (t w) -> p t w", t=NT)[:, :, 64:128],
                AF.Exp))

        @block.vector
        def _(v):
            v.wait_ge(ACT.h, act0 + 1)
            DV.inc(v.reduce_sum(RECB[:], EXPQ[:].rearrange(
                "p (t e) -> p t e", t=NT), axis=AX.X))
            v.wait_ge(DV.h, DV.n)
            DV.inc(v.reciprocal(RECB[:], RECB[:]))
            v.wait_ge(DV.h, DV.n)
            DV.inc(v.scalar_tensor_tensor(
                QSB[:].rearrange("p (t e) -> p t e", t=NT),
                EXPQ[:].rearrange("p (t e) -> p t e", t=NT),
                0.125,
                RECB[:].unsqueeze(2).broadcast_to([128, NT, 64]),
                ALU.mult, ALU.mult))
            v.wait_ge(DV.h, dv0 + NT)
            DV.inc(v.tensor_copy(
                VEB[:].rearrange("p (t e) -> p t e", t=NT)[:, :, 0:64],
                QKVR[:].rearrange("p (t w) -> p t w", t=NT)[:, :, 128:192]))
        dv_qkv = DV.n

        pe_tr0 = PE.n

        @block.tensor
        def _(te):
            te.wait_ge(DV.h, dv_qkv)
            for b8 in range(4):
                if b8 > 0:
                    te.wait_ge(CP.h, cp0 + b8)
                ins = None
                for u in range(8):
                    t = b8 * 8 + u
                    ins = te.transpose(TSTG[0:64, u * 128:u * 128 + 128],
                                       QSB[:, t * 64:(t + 1) * 64],
                                       identb[:])
                PE.inc(ins)

        @block.vector
        def _(v):
            for b8 in range(4):
                v.wait_ge(PE.h, pe_tr0 + b8 + 1)
                CP.inc(v.tensor_copy(QT[0:64, b8 * 1024:(b8 + 1) * 1024],
                                     TSTG[0:64, 0:1024]))
        assert CP.n == cp0 + 4

        pe_b0, dv_b0 = PE.n, DV.n
        NBK = 64

        @block.tensor
        def _(te):
            te.wait_ge(CP.h, cp0 + 4)
            te.wait_ge(ACT.h, act0 + 2)
            for n in range(NBK):
                t, ph = n // 2, n % 2
                te.wait_ge(DV.h, dv_b0 + n + 1 + (n // 8))
                PE.inc(te.matmul(OAP[0:65, (n % 8) * 64:(n % 8) * 64 + 64],
                                 SEXT[0:64, (n % 2) * 65:(n % 2) * 65 + 65],
                                 QT[0:64, n * 64:(n + 1) * 64],
                                 start=True, stop=True, skip_group_check=True))
                lk = KEB[ph * 64:(ph + 1) * 64, t * 64:(t + 1) * 64]
                rv = VEB[ph * 64:(ph + 1) * 64, t * 65:(t + 1) * 65]
                PE.inc(te.matmul(CTX, lk, rv, start=(n == 0), stop=(n == NBK - 1),
                                 skip_group_check=True))
        assert PE.n == pe_b0 + 2 * NBK

        @block.vector
        def _(v):
            for n in range(NBK):
                if n == 0:
                    DV.inc(v.memset(SEXT[0:64, 0:65], 0.0))
                else:
                    v.wait_ge(PE.h, pe_b0 + 2 * n)
                    if n >= 2:
                        v.wait_ge(PE.h, pe_b0 + 2 * n - 1)
                    DV.inc(v.tensor_copy(SEXT[0:64, (n % 2) * 65:(n % 2) * 65 + 65],
                                         CTX))
                if n % 8 == 7:
                    v.wait_ge(PE.h, pe_b0 + 2 * n + 1)
                    DV.inc(v.tensor_copy(OA[0:65, (n - 7) * 64:(n + 1) * 64],
                                         OAP[0:65, :]))
        assert DV.n == dv_b0 + NBK + 8

        pe_n0, dv_n0 = PE.n, DV.n

        @block.vector
        def _(v):
            for nb in range(8):
                if nb == 0:
                    v.wait_ge(DV.h, dv_b0 + NBK + 8)
                DV.inc(v.tensor_single_scalar(DNI[0:1, :],
                                              OA[64:65, nb * 512:(nb + 1) * 512],
                                              1e-6, ALU.max))
                v.wait_ge(DV.h, DV.n)
                DV.inc(v.reciprocal(DNI[0:1, :], DNI[0:1, :]))
                v.wait_ge(DV.h, DV.n)
                DV.inc(v.engine_nop())
                v.wait_ge(PE.h, pe_n0 + nb + 1)
                DV.inc(v.tensor_copy(BCC[0:64, :], NBC))
                v.wait_ge(DV.h, DV.n)
                DV.inc(v.tensor_tensor(OUT64[0:64, nb * 512:(nb + 1) * 512],
                                       OA[0:64, nb * 512:(nb + 1) * 512],
                                       BCC[0:64, :], ALU.mult))
        assert DV.n == dv_n0 + 5 * 8

        @block.tensor
        def _(te):
            for nb in range(8):
                te.wait_ge(DV.h, dv_n0 + 5 * nb + 2)
                PE.inc(te.matmul(NBC, ONER[0:1, 0:64], DNI[0:1, :],
                                 start=True, stop=True, skip_group_check=True))
                if nb < 7:
                    te.wait_ge(DV.h, dv_n0 + 5 * nb + 4)
        assert PE.n == pe_n0 + 8

        odh = DmaStream(p, "od%d" % h, 1)

        @block.gpsimd
        def _(gp):
            gp.wait_ge(DV.h, dv_n0 + 5 * 8)
            odh.issue(gp.dma_start(
                out=ODR[h // 2, (h % 2) * 64:(h % 2) * 64 + 64, :],
                in_=OUT64[0:64, :]))
            if h + 2 < 16:
                _issue_wg(gp, h + 2)

        @block.vector
        def _(v):
            odh.wait(v, 0)
            HGD.inc(v.engine_nop())

    pe_o0, dv_o0 = PE.n, DV.n
    oti = DmaStream(p, "oti", 1)
    ldr = DmaStream(p, "ldr", 2)
    stx = DmaStream(p, "stx", 2)

    @block.sync
    def _(sync):
        sync.wait_ge(HGD.h, 16)
        for t in range(NT):
            if t >= 1:
                sync.wait_ge(PE.h, pe_o0 + t)
            oti.issue(sync.dma_start(
                out=OTI[:].rearrange("p (c x) -> p c x", c=NC),
                in_=ODR[:, :, t * 128:(t + 1) * 128].transpose([1, 0, 2])))
            if t >= 2:
                sync.wait_ge(DV.h, dv_o0 + 2 * (t - 1))
            ldr.issue(sync.dma_start(out=SXA[:, (t % 2) * D:(t % 2 + 1) * D],
                                     in_=Xap[t * 128:(t + 1) * 128, :]))

    @block.tensor
    def _(te):
        te.wait_ge(LDC.h, ldc_all)
        for t in range(NT):
            oti.wait(te, t)
            if t >= 2:
                te.wait_ge(DV.h, dv_o0 + 2 * (t - 1))
            pdst = (PS_S1 if t % 2 == 0 else PS_S2)
            ins = None
            for nh in range(2):
                for cc in range(NC):
                    ins = te.matmul(
                        pdst[:, nh * 512:(nh + 1) * 512],
                        OTI[:, cc * 128:(cc + 1) * 128],
                        WOT[:, cc * D + nh * 512:cc * D + (nh + 1) * 512],
                        start=(cc == 0), stop=(cc == NC - 1),
                        skip_group_check=True)
            PE.inc(ins)
    assert PE.n == pe_o0 + NT

    @block.vector
    def _(v):
        for t in range(NT):
            v.wait_ge(PE.h, pe_o0 + t + 1)
            ldr.wait(v, t)
            if t >= 2:
                stx.wait(v, t - 2)
            psrc = (PS_S1 if t % 2 == 0 else PS_S2)
            ob = (t % 2) * D
            DV.inc(v.tensor_tensor(SCR8[:, ob:ob + D], psrc[:, 0:1024],
                                   SXA[:, ob:ob + D], ALU.add))
            v.wait_ge(DV.h, DV.n)
            DV.inc(v.engine_nop())
    assert DV.n == dv_o0 + 2 * NT

    @block.gpsimd
    def _(gp):
        for t in range(NT):
            gp.wait_ge(DV.h, dv_o0 + 2 * t + 2)
            stx.issue(gp.dma_start(out=XOap[t * 128:(t + 1) * 128, :],
                                   in_=SCR8[:, (t % 2) * D:(t % 2 + 1) * D]))
        for t in (NT - 2, NT - 1):
            stx.wait(gp, t)


def build_attn_l1b():
    p = Prog("attnl1b")
    nc = p.nc
    X = nc.dram_tensor("X", [S, D], F32, kind="ExternalInput").ap()
    WQKVT = nc.dram_tensor("WQKVT", [D, 16 * 192], F32, kind="ExternalInput").ap()
    WOT = nc.dram_tensor("WOT", [D, D], F32, kind="ExternalInput").ap()
    XO = nc.dram_tensor("XO", [S, D], F32, kind="ExternalOutput").ap()
    with nc.Block() as block:
        emit_attn_l1b(p, block, X, XO, WQKVT, WOT)
    return p


def prep_attn_l1b_inputs(x_b, W_qkv, W_out):
    Wt = np.ascontiguousarray(np.asarray(W_qkv).T)
    groups = []
    for h in range(16):
        groups.append(Wt[:, h * 64:(h + 1) * 64])
        groups.append(Wt[:, 1024 + h * 64:1024 + (h + 1) * 64])
        groups.append(Wt[:, 2048 + h * 64:2048 + (h + 1) * 64])
    wq = np.concatenate(groups, axis=1)
    return {
        "X": np.ascontiguousarray(x_b),
        "WQKVT": np.ascontiguousarray(wq),
        "WOT": np.ascontiguousarray(np.asarray(W_out).T),
    }




MOE_C = 1280
MOE_K = 8192
MOE_IT = 48
HUGEF = float(1 << 20)
MOE_TRUNC = 99


def emit_moe(p, block, Xap, XOap, wr_d, w1_d, b1_d, w2_d, b2_d,
             replica_groups=None):
    nc = p.nc
    LDM, LDI, DV, ACT, PE, CP, GP = (
        p.sem("LDM"), p.sem("LDI"), p.sem("DV"), p.sem("ACT"),
        p.sem("PE"), p.sem("CP"), p.sem("GP"))
    CCS = p.sem("CCS")
    PRE = p.sem("PREF")
    SCT = p.sem("SCT")
    XLD = p.sem("XLD")
    WLD = p.sem("WLD")
    E, C = 8, MOE_C
    if replica_groups is None:
        replica_groups = [list(range(NCORE))]

    identb = p.sbuf("identb", [128, 128], BF16)
    identf = p.sbuf("identf", [128, 128], F32)
    idb_d = nc.inline_tensor(np.eye(128).astype(np.float32), "idb_d")
    XTOKE = p.sbuf("xtoke", [128, NT * 1026], BF16)
    OACC = XTOKE[:, 0:20480].bitcast(F32)
    UN = p.sbuf("union", [128, 41994], BF16)
    SC2 = UN[:, 0:4096].bitcast(F32)
    SLOCF = UN[:, 4096:12288].bitcast(F32)[0:8, :]
    POSB = UN[:, 12288:20480].bitcast(F32)[0:8, :]
    CMPB = UN[:, 20480:24576].bitcast(F32)
    ZER8 = UN[:, 24576:32768].bitcast(F32)[0:8, :]
    XINR = UN[:, 0:5130]
    XINT = UN[:, 5130:15370]
    HT = UN[:, 15370:25610]
    W1C = UN[:, 25610:33802]
    W2Q = UN[:, 33802:41994]
    LSC = p.sbuf("lsc", [128, NT * 8], F32)
    XLTF = p.sbuf("xltf", [128, 1024], F32)
    WRF = p.sbuf("wrf", [128, NC * 8], F32)
    WRBB = p.sbuf("wrbb", [128, NC * 8], BF16)
    DIDXT = p.sbuf("didxt", [128, 256], I32)
    B1SB = p.sbuf("b1sb", [128, 256], F32)
    RSM = p.sbuf("rsm", [128, 24], F32)
    RSC = p.sbuf("rsc", [128, 4], F32)
    GR = p.sbuf("gr", [128, 80], F32)
    GG = p.sbuf("gg", [128, 80], F32)
    GRC = p.sbuf("grc", [128, 20], F32)
    TIDF = p.sbuf("tidf", [128, 10], F32)
    TIDX = p.sbuf("tidx", [128, 10], I32)
    ONER = p.sbuf("oner", [1, 128], F32)
    ONESF = p.sbuf("onesf", [128, 1], F32)
    LO8 = p.sbuf("lo8", [1, 8], F32)
    HI8 = p.sbuf("hi8", [1, 8], F32)
    MID8 = p.sbuf("mid8", [1, 8], F32)
    TT8 = p.sbuf("tt8", [1, 16], F32)
    IND8 = p.sbuf("ind8", [1, 8], F32)
    MIDR = p.sbuf("midr", [128, 8], F32)
    CNTP = p.sbuf("cntp", [128, 8], F32)
    LOCOL = p.sbuf("locol", [8, 1], F32)
    ECC = p.sbuf("ecc", [8, 1], F32)
    ECI = p.sbuf("eci", [8, 1], I32)
    PCI = p.sbuf("pci", [128, 1], I32)
    PCF = p.sbuf("pcf", [128, 2], F32)
    PADR = p.sbuf("padr", [128, 1026], BF16)
    B2B = p.sbuf("b2b", [128, D], F32)
    B2RS = p.sbuf("b2rs", [1, D], F32)
    SXA = p.sbuf("sxa", [128, 2 * D], F32)
    SCR8 = p.sbuf("scr8", [128, 2 * D], F32)
    YCB = p.sbuf("ycb", [128, 2 * D], F32)

    PS_A = p.psum("ps_a", [128, 1536], F32)
    PS_B = p.psum("ps_b", [128, 1536], F32)
    PS_P = p.psum("ps_p", [128, 1024], F32)
    TSTGB = PS_P[:, 0:512].bitcast(BF16)
    TSTGF = PS_A[:, 0:1024]
    PSCR = PS_B[:, 0:8]
    PCNT = PS_B[0:1, 512:520]
    PMIDR = PS_B[:, 1024:1032]
    PLOC = PS_B[0:8, 520:521]
    PW1 = [PS_A[:, 0:1280], PS_B[:, 0:1280]]
    PW2 = [PS_P[:, 0:512], PS_P[:, 512:1024]]
    PGT = PS_B[:, 1280:1288]
    PB2A = PS_A[:, 0:512]
    PB2B = PS_A[:, 512:1024]

    WCD = p.sem("WCD")
    W1B = nc.dram_tensor("w1b", [E, 4, 2, 128, 4096], BF16, kind="Internal").ap()
    W2B = nc.dram_tensor("w2b", [E, 4, 128, 8192], BF16, kind="Internal").ap()
    AGIN = nc.dram_tensor("ag_in", [S, E], F32, kind="Internal").ap()
    AGOUT = nc.dram_tensor("ag_out", [NCORE * S, E], F32, kind="Internal").ap()
    XIN_d = nc.dram_tensor("xin", [E * C, 1026], BF16, kind="Internal").ap()

    @block.gpsimd
    def _(gp):
        LDI.inc(gp.dma_start(out=identb[:], in_=idb_d.ap()), 16)
        LDI.inc(gp.dma_start(
            out=WRBB[:].rearrange("p (c e) -> p c e", c=NC),
            in_=wr_d.rearrange("(c p) e -> p c e", p=128)), 16)
        GP.inc(gp.iota(ECI[:], pattern=[[0, 1]], base=0, channel_multiplier=C))
        GP.inc(gp.iota(PCI[:], pattern=[[0, 1]], base=0, channel_multiplier=1))
        for ce in range(E):
            for cq in range(4):
                WCD.inc(gp.dma_start(out=W1B[ce, cq], in_=w1_d[ce, cq]), 16)
                WCD.inc(gp.dma_start(out=W2B[ce, cq], in_=w2_d[ce, cq]), 16)
    ldi_all = LDI.n
    wcd_all = WCD.n

    @block.sync
    def _(sync):
        LDM.inc(sync.dma_start(out=identf[:], in_=idb_d.ap()), 16)
        LDM.inc(sync.dma_start(
            out=WRF[:].rearrange("p (c e) -> p c e", c=NC),
            in_=wr_d.rearrange("(c p) e -> p c e", p=128)), 16)
        LDM.inc(sync.dma_start(out=B1SB[:], in_=b1_d), 16)
        LDM.inc(sync.dma_start(out=XOap, in_=Xap), 16)
    ldm_all = LDM.n

    @block.vector
    def _(v):
        DV.inc(v.memset(ONER[:], 1.0))
        DV.inc(v.memset(ONESF[:], 1.0))
        DV.inc(v.memset(PADR[:, 0:1024], 0.0))
        DV.inc(v.memset(PADR[:, 1024:1026], 255.0))
        DV.inc(v.memset(LO8[:], 0.0))
        DV.inc(v.memset(HI8[:], 1.0))
        DV.inc(v.memset(MID8[:], 0.5))
        DV.inc(v.memset(ZER8[:], 0.0))
        v.wait_ge(GP.h, 2)
        DV.inc(v.tensor_copy(ECC[:], ECI[:]))
        DV.inc(v.tensor_copy(PCF[:, 0:1], PCI[:]))
        v.wait_ge(DV.h, DV.n)
        DV.inc(v.tensor_scalar(PCF[:, 1:2], PCF[:, 0:1], 128.0, None, ALU.add))
    dv_pads = DV.n

    ln = emit_ln(p, block, Xap, xlnt=None, xtok=XTOKE, xcst=1026, pre="lnm",
                 sx=SXA, sq=SCR8)
    STAT = ln["STAT"]
    dv_ln0 = ln["dv0"]
    dv_ln_end = DV.n

    dv_id0 = DV.n

    @block.vector
    def _(v):
        for t in range(NT):
            v.wait_ge(DV.h, dv_ln0 + 7 * t + 7)
            DV.inc(v.memset(XTOKE[:, t * 1026 + 1024:t * 1026 + 1025],
                            float(t // 2)))
            DV.inc(v.tensor_copy(XTOKE[:, t * 1026 + 1025:t * 1026 + 1026],
                                 PCF[:, (t % 2):(t % 2) + 1]))
    assert DV.n == dv_id0 + 2 * NT
    if MOE_TRUNC < 2:
        return

    mrx = DmaStream(p, "mrx", 2)
    pe_r0, dv_r0, act_r0 = PE.n, DV.n, ACT.n
    DV_RT = 8

    @block.sync
    def _(sync):
        for t in range(NT):
            if t == 0:
                sync.wait_ge(DV.h, dv_ln_end)
            if t >= 2:
                sync.wait_ge(DV.h, dv_r0 + DV_RT * (t - 1))
            mrx.issue(sync.dma_start(out=SXA[:, (t % 2) * D:(t % 2 + 1) * D],
                                     in_=Xap[t * 128:(t + 1) * 128, :]))

    @block.vector
    def _(v):
        v.wait_ge(LDM.h, ldm_all)
        for t in range(NT):
            b = (t % 2) * D
            c6 = 6 * t
            mrx.wait(v, t)
            if t >= 1:
                v.wait_ge(PE.h, pe_r0 + 2 * t)
            DV.inc(v.tensor_scalar(SCR8[:, b:b + D], SXA[:, b:b + D],
                                   STAT[:, c6 + 2:c6 + 3],
                                   STAT[:, c6 + 5:c6 + 6], ALU.add, ALU.mult))
            v.wait_ge(PE.h, pe_r0 + 2 * t + 1)
            DV.inc(v.tensor_copy(XLTF[:].rearrange("p (c x) -> p c x", c=NC),
                                 TSTGF.rearrange("p (c x) -> p c x", c=NC)))
            v.wait_ge(PE.h, pe_r0 + 2 * t + 2)
            DV.inc(v.tensor_copy(RSM[:, 0:8], PSCR))
            v.wait_ge(DV.h, DV.n)
            DV.inc(v.reduce_max(RSC[:, 0:1], RSM[:, 0:8], axis=AX.X))
            v.wait_ge(DV.h, DV.n)
            DV.inc(v.tensor_scalar(RSM[:, 8:16], RSM[:, 0:8], RSC[:, 0:1],
                                   None, ALU.subtract))
            v.wait_ge(ACT.h, act_r0 + t + 1)
            DV.inc(v.tensor_reduce(RSC[:, 1:2], RSM[:, 16:24], axis=AX.X,
                                   op=ALU.add))
            v.wait_ge(DV.h, DV.n)
            DV.inc(v.reciprocal(RSC[:, 2:3], RSC[:, 1:2]))
            v.wait_ge(DV.h, DV.n)
            DV.inc(v.tensor_scalar(LSC[:, t * 8:(t + 1) * 8], RSM[:, 16:24],
                                   RSC[:, 2:3], None, ALU.mult))
    assert DV.n == dv_r0 + DV_RT * NT
    dv_router_end = DV.n

    @block.scalar
    def _(sc_):
        for t in range(NT):
            sc_.wait_ge(DV.h, dv_r0 + DV_RT * t + 5)
            ACT.inc(sc_.activation(RSM[:, 16:24], RSM[:, 8:16], AF.Exp))
    assert ACT.n == act_r0 + NT

    @block.tensor
    def _(te):
        te.wait_ge(LDM.h, ldm_all)
        te.wait_ge(LDI.h, ldi_all)
        for t in range(NT):
            te.wait_ge(DV.h, dv_r0 + DV_RT * t + 1)
            ins = None
            for cc in range(NC):
                ins = te.transpose(TSTGF[:, cc * 128:(cc + 1) * 128],
                                   SCR8[:, (t % 2) * D + cc * 128:
                                        (t % 2) * D + (cc + 1) * 128],
                                   identf[:])
            PE.inc(ins)
            te.wait_ge(DV.h, dv_r0 + DV_RT * t + 2)
            ins = None
            for cc in range(NC):
                ins = te.matmul(PSCR, XLTF[:, cc * 128:(cc + 1) * 128],
                                WRF[:, cc * 8:(cc + 1) * 8],
                                start=(cc == 0), stop=(cc == NC - 1),
                                skip_group_check=True)
            PE.inc(ins)
    assert PE.n == pe_r0 + 2 * NT
    if MOE_TRUNC < 3:
        return

    CCO = p.sem("CCO")

    @block.gpsimd
    def _(gp):
        gp.wait_ge(DV.h, dv_router_end)
        CCS.inc(gp.dma_start(out=AGIN.rearrange("(p t) e -> p t e", p=128),
                             in_=LSC[:].rearrange("p (t e) -> p t e", t=NT)),
                16)
        gp.wait_ge(CCS.h, 16)
        CCO.inc(gp.collective_compute(
            "AllGather", ALU.bypass, replica_groups=replica_groups,
            ins=[AGIN], outs=[AGOUT]), 1)
        gp.wait_ge(CCO.h, 1)
        CCS.inc(gp.dma_start(out=SC2,
                             in_=AGOUT.rearrange("(p t) e -> p (t e)", p=128)),
                16)
    ccs_all = CCS.n
    if MOE_TRUNC < 4:
        return

    pe_b0, dv_b0 = PE.n, DV.n
    DV_BI = 12

    @block.tensor
    def _(te):
        for it in range(MOE_IT):
            if it == 0:
                te.wait_ge(CCS.h, ccs_all)
                te.wait_ge(DV.h, dv_b0)
            else:
                te.wait_ge(DV.h, dv_b0 + DV_BI * it)
            PE.inc(te.matmul(PMIDR, ONER[:], MID8[:], start=True, stop=True,
                             skip_group_check=True))
            te.wait_ge(DV.h, dv_b0 + DV_BI * it + 3)
            PE.inc(te.matmul(PCNT, ONESF[:], CNTP[:], start=True, stop=True,
                             skip_group_check=True))
        te.wait_ge(DV.h, dv_b0 + DV_BI * MOE_IT)
        PE.inc(te.matmul(PLOC, LO8[:], ONESF[0:1, 0:1], start=True, stop=True,
                         skip_group_check=True))
    assert PE.n == pe_b0 + 2 * MOE_IT + 1

    @block.vector
    def _(v):
        for it in range(MOE_IT):
            v.wait_ge(PE.h, pe_b0 + 2 * it + 1)
            DV.inc(v.tensor_copy(MIDR[:], PMIDR))
            v.wait_ge(DV.h, DV.n)
            DV.inc(v.tensor_tensor(
                CMPB.rearrange("p (t e) -> p t e", e=E),
                SC2.rearrange("p (t e) -> p t e", e=E),
                MIDR[:].unsqueeze(1).broadcast_to([128, 256, E]),
                ALU.is_gt))
            v.wait_ge(DV.h, DV.n)
            DV.inc(v.tensor_reduce(
                CNTP[:],
                CMPB.rearrange("p (t e) -> p t e", e=E).transpose([0, 2, 1]),
                axis=AX.X, op=ALU.add))
            v.wait_ge(PE.h, pe_b0 + 2 * it + 2)
            DV.inc(v.tensor_single_scalar(IND8[:], PCNT, float(MOE_K),
                                          ALU.is_ge))
            v.wait_ge(DV.h, DV.n)
            DV.inc(v.tensor_tensor(TT8[0:1, 0:8], MID8[:], LO8[:],
                                   ALU.subtract))
            v.wait_ge(DV.h, DV.n)
            DV.inc(v.tensor_tensor(TT8[0:1, 8:16], TT8[0:1, 0:8], IND8[:],
                                   ALU.mult))
            v.wait_ge(DV.h, DV.n)
            DV.inc(v.tensor_tensor(LO8[:], LO8[:], TT8[0:1, 8:16], ALU.add))
            DV.inc(v.tensor_tensor(TT8[0:1, 0:8], HI8[:], MID8[:],
                                   ALU.subtract))
            v.wait_ge(DV.h, DV.n)
            DV.inc(v.tensor_tensor(TT8[0:1, 8:16], TT8[0:1, 0:8], IND8[:],
                                   ALU.mult))
            v.wait_ge(DV.h, DV.n)
            DV.inc(v.tensor_tensor(HI8[:], MID8[:], TT8[0:1, 8:16], ALU.add))
            v.wait_ge(DV.h, DV.n)
            DV.inc(v.tensor_tensor(MID8[:], LO8[:], HI8[:], ALU.add))
            v.wait_ge(DV.h, DV.n)
            DV.inc(v.tensor_scalar(MID8[:], MID8[:], 0.5, None, ALU.mult))
        v.wait_ge(PE.h, pe_b0 + 2 * MOE_IT + 1)
        DV.inc(v.tensor_copy(LOCOL[:], PLOC))
    assert DV.n == dv_b0 + DV_BI * MOE_IT + 1
    if MOE_TRUNC < 5:
        return

    pe_m0, dv_m0, cp_m0 = PE.n, DV.n, CP.n

    @block.tensor
    def _(te):
        te.wait_ge(DV.h, dv_m0)
        for b8 in range(4):
            if b8 > 0:
                te.wait_ge(CP.h, cp_m0 + b8)
            ins = None
            for u in range(8):
                t = b8 * 8 + u
                ins = te.transpose(TSTGF[0:8, u * 128:(u + 1) * 128],
                                   LSC[:, t * 8:(t + 1) * 8], identf[:])
            PE.inc(ins)
        te.wait_ge(DV.h, dv_m0 + 6)
        for b8 in range(4):
            te.wait_ge(CP.h, cp_m0 + 4 + b8)
            ins = None
            for u in range(8):
                t = b8 * 8 + u
                ins = te.transpose(TSTGF[:, u * 8:(u + 1) * 8],
                                   POSB[0:8, t * 128:(t + 1) * 128],
                                   identf[0:8, 0:8])
            PE.inc(ins)
    assert PE.n == pe_m0 + 8

    @block.vector
    def _(v):
        for b8 in range(4):
            v.wait_ge(PE.h, pe_m0 + b8 + 1)
            CP.inc(v.tensor_copy(SLOCF[0:8, b8 * 1024:(b8 + 1) * 1024],
                                 TSTGF[0:8, 0:1024]))
        v.wait_ge(CP.h, cp_m0 + 4)
        v.wait_ge(DV.h, DV.n)
        DV.inc(v.tensor_scalar(SLOCF[:], SLOCF[:], LOCOL[:], None, ALU.is_gt))
        v.wait_ge(DV.h, DV.n)
        DV.inc(v.tensor_tensor_scan(POSB[:], SLOCF[:], ZER8[:], 0.0,
                                    ALU.add, ALU.add))
        v.wait_ge(DV.h, DV.n)
        DV.inc(v.tensor_tensor(POSB[:], POSB[:], SLOCF[:], ALU.subtract))
        v.wait_ge(DV.h, DV.n)
        DV.inc(v.tensor_scalar(POSB[:], POSB[:], ECC[:], None, ALU.add))
        v.wait_ge(DV.h, DV.n)
        DV.inc(v.tensor_scalar(SLOCF[:], SLOCF[:], -HUGEF, HUGEF,
                               ALU.mult, ALU.add))
        v.wait_ge(DV.h, DV.n)
        DV.inc(v.tensor_tensor(POSB[:], POSB[:], SLOCF[:], ALU.add))
        v.wait_ge(DV.h, DV.n)
        for b8 in range(4):
            v.wait_ge(PE.h, pe_m0 + 4 + b8 + 1)
            CP.inc(v.tensor_copy(DIDXT[:, b8 * 64:(b8 + 1) * 64],
                                 TSTGF[:, 0:64]))
    assert DV.n == dv_m0 + 6
    assert CP.n == cp_m0 + 8
    didxt_done = CP.n
    if MOE_TRUNC < 6:
        return

    @block.sync
    def _(sync):
        sync.wait_ge(DV.h, dv_pads)
        for r in range(E * C // 128):
            PRE.inc(sync.dma_start(out=XIN_d[r * 128:(r + 1) * 128, :],
                                   in_=PADR[:]), 16)
    pre_all = PRE.n

    @block.gpsimd
    def _(gp):
        gp.wait_ge(PRE.h, pre_all)
        gp.wait_ge(CP.h, didxt_done)
        gp.wait_ge(DV.h, dv_id0 + 2 * NT)
        bcr = gp.to_reg(E * C - 1)
        for t in range(NT):
            for e in range(E):
                SCT.inc(gp.indirect_dma_start(
                    XIN_d,
                    IndirectOffsetOnAxis(DIDXT[:, t * 8 + e:t * 8 + e + 1], 0),
                    XTOKE[:, t * 1026:(t + 1) * 1026], None,
                    bounds_check=bcr, oob_is_err=False), 16)
    sct_all = SCT.n
    if MOE_TRUNC < 7:
        return

    NQ, NST = 4, 10
    n_exp = min(E, MOE_TRUNC - 6)
    sad = DmaStream(p, "sad", 1)
    for e in range(n_exp):
        pe0, dv0, act0, cp0 = PE.n, DV.n, ACT.n, CP.n
        xld0, wld0 = XLD.n, WLD.n
        dvS1 = dv0 + 11
        dvS2 = dvS1 + 15
        dv_w2e = lambda q, k: dvS2 + 20 * q + k + 1
        dvS3 = dvS2 + 80
        dv_yc = lambda st: dvS3 + 2 * (st + 1)
        act_gelu = lambda gq: act0 + 1 + gq + 1
        pe_tr = lambda tt: pe0 + tt + 1
        pe_gt = lambda st: pe0 + 10 + st + 1
        pe_b2 = pe0 + 22
        pe_w1 = lambda gq: pe0 + 22 + (gq // 8) * 20 + gq + 1
        pe_w2 = lambda q, k: pe0 + 31 + 28 * q + k

        @block.sync
        def _(sync):
            sync.wait_ge(SCT.h, sct_all)
            if e >= 1:
                sync.wait_ge(PE.h, pe0 - 134 + 22)
            XLD.inc(sync.dma_start(out=B2RS[:], in_=b2_d[e:e + 1, :]), 16)
            for xh in range(2):
                if xh == 1:
                    sync.wait_ge(PE.h, pe_tr(4))
                    sync.wait_ge(DV.h, dv0 + 5)
                elif e >= 1:
                    sync.wait_ge(PE.h, pe0 - 134 + 10)
                    sync.wait_ge(DV.h, dv0 - 115 + 11)
                XLD.inc(sync.dma_start(
                    out=XINR[:].rearrange("p (t w) -> p t w", t=5),
                    in_=XIN_d[e * C + xh * 640:e * C + (xh + 1) * 640, :]
                        .rearrange("(t p) w -> p t w", p=128)), 16)
            if e == 0:
                sync.wait_ge(WCD.h, wcd_all)
            for q in range(NQ):
                if q >= 1:
                    sync.wait_ge(PE.h, pe_w2(q - 1, 19))
                elif e >= 1:
                    sync.wait_ge(PE.h, pe0)
                WLD.inc(sync.dma_start(out=W2Q[:], in_=W2B[e, q]), 16)
                for dhb in range(2):
                    gdb = q * 2 + dhb
                    if gdb >= 2:
                        sync.wait_ge(ACT.h, act_gelu(4 * (gdb - 1) - 1))
                    elif e >= 1:
                        sync.wait_ge(PE.h, pe0)
                    WLD.inc(sync.dma_start(
                        out=W1C[:, (dhb % 2) * 4096:(dhb % 2) * 4096 + 4096],
                        in_=W1B[e, q, dhb]), 16)

        @block.tensor
        def _(te):
            if e >= 1:
                te.wait_ge(DV.h, dv0)
            else:
                te.wait_ge(CP.h, didxt_done)
            for tt in range(10):
                xh, tl = tt // 5, tt % 5
                te.wait_ge(XLD.h, xld0 + 32 + 16 * xh)
                if tt > 0:
                    te.wait_ge(CP.h, cp0 + tt)
                ins = None
                for cc in range(NC):
                    ins = te.transpose(
                        TSTGB[:, cc * 128:(cc + 1) * 128],
                        XINR[:, tl * 1026 + cc * 128:tl * 1026 + (cc + 1) * 128],
                        identb[:])
                PE.inc(ins)
            te.wait_ge(CP.h, cp0 + 10)
            for st in range(NST):
                if st >= 1:
                    te.wait_ge(DV.h, dvS1 + st)
                ins = None
                for cc in range(NC):
                    ins = te.matmul(PGT,
                                    XINT[:, cc * 1280 + st * 128:
                                         cc * 1280 + (st + 1) * 128],
                                    WRBB[:, cc * 8:(cc + 1) * 8],
                                    start=(cc == 0), stop=(cc == NC - 1),
                                    skip_group_check=True)
                PE.inc(ins)
            te.wait_ge(XLD.h, xld0 + 48)
            te.wait_ge(DV.h, dvS1 + 10)
            PE.inc(te.matmul(PB2A, ONER[:], B2RS[0:1, 0:512], start=True,
                             stop=True, skip_group_check=True))
            PE.inc(te.matmul(PB2B, ONER[:], B2RS[0:1, 512:1024], start=True,
                             stop=True, skip_group_check=True))
            te.wait_ge(DV.h, dvS2)
            for q in range(NQ):
                for dhb in range(2):
                    gdb = q * 2 + dhb
                    te.wait_ge(WLD.h, wld0 + 48 * (q + 1))
                    for dhc in range(4):
                        gq = gdb * 4 + dhc
                        pw = PW1[gq % 2]
                        if gq >= 2:
                            te.wait_ge(ACT.h, act_gelu(gq - 2))
                        ins = None
                        for n3, (n0, nw) in enumerate(((0, 512), (512, 512),
                                                       (1024, 256))):
                            for cc in range(NC):
                                ins = te.matmul(
                                    pw[:, n0:n0 + nw],
                                    W1C[:, (dhb % 2) * 4096 + cc * 512 +
                                        dhc * 128:(dhb % 2) * 4096 + cc * 512 +
                                        (dhc + 1) * 128],
                                    XINT[:, cc * 1280 + n0:cc * 1280 + n0 + nw],
                                    start=(cc == 0),
                                    stop=(cc == NC - 1),
                                    skip_group_check=True)
                        PE.inc(ins)
                te.wait_ge(WLD.h, wld0 + 48 * (q + 1))
                te.wait_ge(ACT.h, act_gelu(q * 8 + 7))
                for st in range(NST):
                    for k2 in range(2):
                        k = st * 2 + k2
                        if q * 20 + k >= 2:
                            te.wait_ge(DV.h, dv_w2e(q, k - 2) if k >= 2
                                       else dv_w2e(q - 1, k + 18))
                        ins = None
                        for qd in range(8):
                            ins = te.matmul(
                                PW2[k % 2],
                                HT[:, qd * 1280 + st * 128:
                                   qd * 1280 + (st + 1) * 128],
                                W2Q[:, qd * 1024 + k2 * 512:
                                    qd * 1024 + (k2 + 1) * 512],
                                start=(qd == 0), stop=(qd == 7),
                                skip_group_check=True)
                        PE.inc(ins)
        assert PE.n == pe0 + 134

        @block.scalar
        def _(sc_):
            sc_.wait_ge(DV.h, dvS1 + 10)
            ACT.inc(sc_.activation(GG[:], GR[:], AF.Exp))
            for q in range(NQ):
                for dhb in range(2):
                    for dhc in range(4):
                        gq = (q * 2 + dhb) * 4 + dhc
                        sc_.wait_ge(PE.h, pe_w1(gq))
                        bcol = e * 32 + q * 8 + dhb * 4 + dhc
                        ACT.inc(sc_.activation(
                            HT[:].rearrange("p (qd w) -> p qd w", qd=8)[
                                :, (dhb * 4 + dhc):(dhb * 4 + dhc) + 1, :],
                            PW1[gq % 2][:, 0:1280].unsqueeze(1),
                            AF.Gelu, bias=B1SB[:, bcol:bcol + 1]))
        assert ACT.n == act0 + 33

        @block.vector
        def _(v):
            for tt in range(10):
                xh, tl = tt // 5, tt % 5
                v.wait_ge(PE.h, pe_tr(tt))
                dst = XINT.rearrange("p (c w) -> p c w", c=NC)[
                    :, :, tt * 128:(tt + 1) * 128]
                src = TSTGB.rearrange("p (c x) -> p c x", cc=NC) if False \
                    else TSTGB.rearrange("p (c x) -> p c x", c=NC)
                CP.inc(v.tensor_copy(dst, src))
                DV.inc(v.scalar_tensor_tensor(
                    TIDF[:, tt:tt + 1],
                    XINR[:, tl * 1026 + 1024:tl * 1026 + 1025], 256.0,
                    XINR[:, tl * 1026 + 1025:tl * 1026 + 1026],
                    ALU.mult, ALU.add))
            v.wait_ge(DV.h, DV.n)
            if e >= 1:
                v.wait_ge(sad.sems[0].h, 16 * NST * e)
            DV.inc(v.tensor_copy(TIDX[:], TIDF[:]))
            assert DV.n == dvS1
            for st in range(NST):
                v.wait_ge(PE.h, pe_gt(st))
                DV.inc(v.tensor_copy(GR[:, st * 8:(st + 1) * 8], PGT))
            v.wait_ge(ACT.h, act0 + 1)
            DV.inc(v.reduce_sum(GRC[:, 0:10],
                                GG[:].rearrange("p (s w) -> p s w", s=NST),
                                axis=AX.X))
            v.wait_ge(DV.h, DV.n)
            DV.inc(v.reciprocal(GRC[:, 10:20], GRC[:, 0:10]))
            v.wait_ge(DV.h, DV.n)
            DV.inc(v.scalar_tensor_tensor(
                GG[:].rearrange("p (s w) -> p s w", s=NST),
                GG[:].rearrange("p (s w) -> p s w", s=NST), 1.0,
                GRC[:, 10:20].unsqueeze(2).broadcast_to([128, NST, E]),
                ALU.mult, ALU.mult))
            v.wait_ge(PE.h, pe_b2)
            DV.inc(v.tensor_copy(B2B[:, 0:512], PB2A))
            DV.inc(v.tensor_copy(B2B[:, 512:1024], PB2B))
            assert DV.n == dvS2
            for q in range(NQ):
                for st in range(NST):
                    for k2 in range(2):
                        k = st * 2 + k2
                        v.wait_ge(PE.h, pe_w2(q, k))
                        dst = OACC[:, st * 1024 + k2 * 512:
                                   st * 1024 + (k2 + 1) * 512]
                        if q == 0:
                            DV.inc(v.tensor_copy(dst, PW2[k % 2]))
                        else:
                            DV.inc(v.tensor_tensor(dst, dst, PW2[k % 2],
                                                   ALU.add))
            assert DV.n == dvS3
            for st in range(NST):
                if st >= 2:
                    sad.wait(v, e * NST + st - 2) if False else None
                ob = (st % 2) * D
                if st >= 2:
                    s2 = sad.sems[0]
                    v.wait_ge(s2.h, 16 * (e * NST + st - 1))
                DV.inc(v.tensor_tensor(YCB[:, ob:ob + D],
                                       OACC[:, st * 1024:(st + 1) * 1024],
                                       B2B[:], ALU.add))
                v.wait_ge(DV.h, DV.n)
                DV.inc(v.tensor_scalar(YCB[:, ob:ob + D], YCB[:, ob:ob + D],
                                       GG[:, st * 8 + e:st * 8 + e + 1],
                                       None, ALU.mult))
            assert DV.n == dvS3 + 20

        @block.gpsimd
        def _(gp):
            gp.wait_ge(LDM.h, ldm_all)
            bcr2 = gp.to_reg(S - 1)
            for st in range(NST):
                gp.wait_ge(DV.h, dv_yc(st))
                i = e * NST + st
                if i >= 1:
                    gp.wait_ge(sad.sems[0].h, 16 * i)
                sad.issue(gp.indirect_dma_start(
                    XOap,
                    IndirectOffsetOnAxis(TIDX[:, st:st + 1], 0),
                    YCB[:, (st % 2) * D:(st % 2 + 1) * D], None,
                    bounds_check=bcr2, oob_is_err=False,
                    compute_op=ALU.add))

    @block.gpsimd
    def _(gp):
        gp.wait_ge(sad.sems[0].h, 16 * n_exp * NST)


def build_moe():
    p = Prog("moe")
    nc = p.nc
    X = nc.dram_tensor("X", [S, D], F32, kind="ExternalInput").ap()
    WR = nc.dram_tensor("WR", [D, 8], F32, kind="ExternalInput").ap()
    W1 = nc.dram_tensor("W1", [8, 4, 2, 128, 4096], F32,
                        kind="ExternalInput").ap()
    B1 = nc.dram_tensor("B1", [128, 256], F32, kind="ExternalInput").ap()
    W2 = nc.dram_tensor("W2", [8, 4, 128, 8192], F32,
                        kind="ExternalInput").ap()
    B2 = nc.dram_tensor("B2", [8, D], F32, kind="ExternalInput").ap()
    XO = nc.dram_tensor("XO", [S, D], F32, kind="ExternalOutput").ap()
    with nc.Block() as block:
        emit_moe(p, block, X, XO, WR, W1, B1, W2, B2)
    return p


def prep_moe_inputs(x_b, Wr, W1, b1, W2, b2):
    r = lambda a: np.ascontiguousarray(np.asarray(a))
    b1l = np.asarray(b1).reshape(8, 32, 128).transpose(2, 0, 1).reshape(128, 256)
    W1p = np.asarray(W1).reshape(8, 8, 128, 4, 2, 512).transpose(
        0, 3, 4, 2, 1, 5).reshape(8, 4, 2, 128, 4096)
    W2p = np.asarray(W2).reshape(8, 4, 8, 128, 1024).transpose(
        0, 1, 3, 2, 4).reshape(8, 4, 128, 8192)
    return {
        "X": r(x_b), "WR": r(Wr), "W1": r(W1p),
        "B1": r(b1l.astype(np.float32)),
        "W2": r(W2p), "B2": r(np.asarray(b2).reshape(8, D)),
    }


_PROGS = {}


def _progs():
    if not _PROGS:
        _PROGS["a"] = build_attn_a()
        _PROGS["moe"] = build_moe()
        _PROGS["b"] = build_attn_b()
        _PROGS["l1a"] = build_attn_l1a()
        _PROGS["l1f"] = build_l1ffn()
        _PROGS["l1b"] = build_attn_l1b()
    return _PROGS


def kernel(**inputs):
    from concourse.bass_utils import run_bass_kernel_spmd
    g = {k: np.asarray(v) for k, v in inputs.items()}
    P = _progs()
    X = [np.ascontiguousarray(g["L0_attn_a_x"][b]) for b in range(NCORE)]

    def launch(p, prep):
        ims = [prep(b) for b in range(NCORE)]
        res = run_bass_kernel_spmd(p.nc, ims, core_ids=list(range(NCORE)))
        return [np.asarray(res.results[c]["XO"]) for c in range(NCORE)]

    moe_prep = lambda b: prep_moe_inputs(
        X[b], g["L0_ffn_Wr"], g["L0_ffn_W1"], g["L0_ffn_b1"],
        g["L0_ffn_W2"], g["L0_ffn_b2"])
    l1f_prep = lambda b: prep_l1ffn_inputs(
        X[b], g["L1_ffn_W_r"], g["L1_ffn_W_k"], g["L1_ffn_W_v"],
        g["L1_ffn_mu_r"], g["L1_ffn_mu_k"])

    X = launch(P["a"], lambda b: prep_attn_a_inputs(
        X[b], g["L0_attn_a_Wq"], g["L0_attn_a_bq"],
        g["L0_attn_a_Wk"], g["L0_attn_a_bk"]))
    X = launch(P["moe"], moe_prep)
    X = launch(P["b"], lambda b: prep_attn_b_inputs(
        X[b], g["L0_attn_b_Wq"], g["L0_attn_b_Wkv"],
        g["L0_attn_b_Wout"], g["L0_attn_b_bout"]))
    X = launch(P["moe"], moe_prep)
    X = launch(P["l1a"], lambda b: prep_attn_l1a_inputs(
        X[b], g["L1_attn_a_wq"], g["L1_attn_a_bq"], g["L1_attn_a_wk"],
        g["L1_attn_a_bk"], g["L1_attn_a_wv"], g["L1_attn_a_bv"],
        g["L1_attn_a_wo"], g["L1_attn_a_bo"],
        g["L1_attn_a_rot_cos"], g["L1_attn_a_rot_sin"]))
    X = launch(P["l1f"], l1f_prep)
    X = launch(P["l1b"], lambda b: prep_attn_l1b_inputs(
        X[b], g["L1_attn_b_W_qkv"], g["L1_attn_b_W_out"]))
    X = launch(P["l1f"], l1f_prep)
    return np.stack(X).astype(np.float32)


# version 02 (final)
# speedup vs baseline: 1.0183x; 1.0183x over previous
import math
import numpy as np
import concourse.bass as bass
import concourse.mybir as mybir
from concourse.bass import IndirectOffsetOnAxis
from concourse.bass_utils import run_bass_kernel_spmd

F32 = mybir.dt.float32
BF16 = mybir.dt.bfloat16
I32 = mybir.dt.int32
AF = mybir.ActivationFunctionType
ALU = mybir.AluOpType
AX = mybir.AxisListType

S = 4096
D = 1024
NT = S // 128
NC = D // 128
NCORE = 8
EPS = 1e-5


class Sem:
    def __init__(self, nc, name):
        self.h = nc.alloc_semaphore(name)
        self.n = 0

    def inc(self, instr, k=1):
        instr.then_inc(self.h, k)
        self.n += k
        return instr


AR_BYTES = 212736


class SbufView:
    __slots__ = ("_ap",)

    def __init__(self, ap):
        object.__setattr__(self, "_ap", ap)

    def ap(self):
        return self._ap

    def __getitem__(self, k):
        return self._ap[k]

    def __getattr__(self, k):
        return getattr(self._ap, k)


class Prog:
    def __init__(self, name):
        self.nc = bass.Bass("TRN2", target_bir_lowering=False, name=name)
        self.sems = {}
        self.arena = None
        self.parena = None
        self.aoff = 0
        self.poff = 0
        self._stage = 0

    def sem(self, name):
        if name not in self.sems:
            self.sems[name] = Sem(self.nc, name)
        return self.sems[name]

    def enable_arena(self):
        self.arena = self.nc.alloc_sbuf_tensor(
            "ARENA", [128, AR_BYTES // 2], BF16)
        self.parena = self.nc.alloc_psum_tensor("PARENA", [128, 4096], F32)

    def stage_reset(self):
        self.aoff = 0
        self.poff = 0
        self._stage += 1

    def uname(self, name):
        return name if self.arena is None else f"{name}_st{self._stage}"

    def sbuf(self, name, shape, dtype):
        if self.arena is None:
            return self.nc.alloc_sbuf_tensor(name, shape, dtype)
        p_, n = shape
        nb = n * (2 if dtype == BF16 else 4)
        off = (self.aoff + 63) & ~63
        self.aoff = off + nb
        assert self.aoff <= AR_BYTES, f"sbuf overflow at {name}: {self.aoff}"
        v = self.arena[0:p_, off // 2:(off + nb) // 2]
        return SbufView(v if dtype == BF16 else v.bitcast(dtype))

    def psum(self, name, shape, dtype=F32):
        if self.parena is None:
            return self.nc.alloc_psum_tensor(name, shape, dtype)
        p_, n = shape
        nf = n // 2 if dtype == BF16 else n
        off = (self.poff + 511) & ~511
        self.poff = off + nf
        assert self.poff <= 4096, f"psum overflow at {name}: {self.poff}"
        v = self.parena[0:p_, off:off + nf]
        return v if dtype == F32 else v.bitcast(dtype)


class DmaStream:

    def __init__(self, p, name, nslots):
        self.sems = [p.sem(f"{name}_s{i}") for i in range(nslots)]
        self.k = nslots
        self.base = [s.n for s in self.sems]
        self.count = 0

    def issue(self, instr):
        s = self.sems[self.count % self.k]
        s.inc(instr, 16)
        self.count += 1
        return instr

    def wait(self, eng, t):
        s = self.sems[t % self.k]
        eng.wait_ge(s.h, self.base[t % self.k] + 16 * (t // self.k + 1))


def emit_ln(p, block, Xap, xlnt=None, cst=0, off=0, xtok=None, xcst=D,
            pre="ln", identb=None, identb_ready=None, stat=None,
            sx=None, sq=None, pt=None):
    DV, ACT, PE, CP = (p.sem("DV"), p.sem("ACT"), p.sem("PE"), p.sem("CP"))
    SX = sx if sx is not None else p.sbuf(pre + "_sx", [128, 2 * D], F32)
    SQ = sq if sq is not None else p.sbuf(pre + "_sq", [128, 2 * D], F32)
    STAT = stat if stat is not None else p.sbuf(pre + "_stat", [128, 6 * NT], F32)
    XLB = p.sbuf(pre + "_xlb", [128, 2 * D], BF16) if (xtok is None and xlnt is not None) else None
    if xlnt is not None:
        PT = pt if pt is not None else p.psum(pre + "_pt", [128, 2048], BF16)

    dv0, act0, pe0, cp0 = DV.n, ACT.n, PE.n, CP.n
    do_t = xlnt is not None
    ldx = DmaStream(p, pre + "_ldx", 2)

    @block.sync
    def _(sync):
        for t in range(NT):
            if t >= 2:
                sync.wait_ge(DV.h, dv0 + 7 * (t - 1))
                sync.wait_ge(ACT.h, act0 + 2 * (t - 1))
                if xtok is None and xlnt is not None and t >= 2:
                    sync.wait_ge(PE.h, pe0 + t - 1)
            ldx.issue(sync.dma_start(out=SX[:, (t % 2) * D:(t % 2 + 1) * D],
                                     in_=Xap[t * 128:(t + 1) * 128, :]))

    @block.scalar
    def _(sc):
        for t in range(NT):
            b = t % 2
            ldx.wait(sc, t)
            if t >= 2:
                sc.wait_ge(DV.h, dv0 + 7 * (t - 1))
            ACT.inc(sc.activation(SQ[:, b * D:(b + 1) * D],
                                  SX[:, b * D:(b + 1) * D], AF.Square))
            sc.wait_ge(DV.h, dv0 + 7 * t + 5)
            ACT.inc(sc.activation(STAT[:, 6 * t + 4:6 * t + 5],
                                  STAT[:, 6 * t + 3:6 * t + 4], AF.Sqrt))
    assert ACT.n == act0 + 2 * NT

    def _ln_copy(v, j):
        v.wait_ge(PE.h, pe0 + j + 1)
        src = PT[:, (j % 2) * 1024:(j % 2) * 1024 + 1024].rearrange(
            "p (c x) -> p c x", c=NC)
        dst = xlnt.ap().rearrange("p (c w) -> p c w", c=NC)[
            :, :, off + j * 128:off + j * 128 + 128]
        CP.inc(v.tensor_copy(dst, src))

    @block.vector
    def _(v):
        for t in range(NT):
            b = t % 2
            c = 6 * t
            sl = lambda j: STAT[:, c + j:c + j + 1]
            ldx.wait(v, t)
            DV.inc(v.reduce_sum(sl(0), SX[:, b * D:(b + 1) * D], axis=AX.X))
            v.wait_ge(ACT.h, act0 + 2 * t + 1)
            DV.inc(v.tensor_reduce(sl(1), SQ[:, b * D:(b + 1) * D],
                                   axis=AX.X, op=ALU.add))
            v.wait_ge(DV.h, DV.n)
            DV.inc(v.tensor_scalar(sl(2), sl(0), -1.0 / D, None, ALU.mult))
            v.wait_ge(DV.h, DV.n)
            DV.inc(v.tensor_scalar(sl(5), sl(2), sl(2), EPS, ALU.mult,
                                   ALU.subtract))
            v.wait_ge(DV.h, DV.n)
            DV.inc(v.tensor_scalar(sl(3), sl(1), 1.0 / D, sl(5), ALU.mult,
                                   ALU.subtract))
            v.wait_ge(ACT.h, act0 + 2 * t + 2)
            DV.inc(v.reciprocal(sl(5), sl(4)))
            v.wait_ge(DV.h, DV.n)
            if xtok is not None:
                dest = xtok[:, t * xcst:t * xcst + D]
            else:
                dest = XLB[:, (t % 2) * D:(t % 2 + 1) * D]
            DV.inc(v.tensor_scalar(dest, SX[:, b * D:(b + 1) * D], sl(2), sl(5),
                                   ALU.add, ALU.mult))
            if do_t and t >= 1:
                _ln_copy(v, t - 1)
        if do_t:
            _ln_copy(v, NT - 1)
    assert DV.n == dv0 + 7 * NT

    if do_t:
        @block.tensor
        def _(te):
            te.wait_ge(identb_ready[0].h, identb_ready[1])
            for t in range(NT):
                te.wait_ge(DV.h, dv0 + 7 * t + 7)
                if t >= 2:
                    te.wait_ge(CP.h, cp0 + t - 1)
                src = (xtok[:, t * xcst:t * xcst + D] if xtok is not None
                       else XLB[:, (t % 2) * D:(t % 2 + 1) * D])
                ins = None
                for cc in range(NC):
                    ins = te.transpose(
                        PT[:, (t % 2) * 1024 + cc * 128:(t % 2) * 1024 + (cc + 1) * 128],
                        src[:, cc * 128:(cc + 1) * 128], identb[:])
                PE.inc(ins)
        assert PE.n == pe0 + NT

        assert CP.n == cp0 + NT
    return dict(dv0=dv0, act0=act0, pe0=pe0, cp0=cp0, STAT=STAT)


CST_FFN = 4104


def emit_l1ffn(p, block, Xap, XOap, wrt_d, wkt_d, wvt_d, mus_d):
    nc = p.nc
    LDC, LDM, DV, ACT, PE, ST, MM, CP = (
        p.sem("LDC"), p.sem("LDM"), p.sem("DV"), p.sem("ACT"),
        p.sem("PE"), p.sem("ST"), p.sem("MM"), p.sem("CP"))

    identb = p.sbuf("identb", [128, 128], BF16)
    idb_d = nc.inline_tensor(np.eye(128).astype(np.float32), p.uname("idb_d"))
    XLNP = p.sbuf("xlnp", [128, NC * CST_FFN], BF16)
    WRT = p.sbuf("wrt", [128, NC * D], BF16)
    WKT = p.sbuf("wkt", [128, NC * D], BF16)
    WVT = p.sbuf("wvt", [128, NC * D], BF16)
    MU = p.sbuf("mu", [128, 16], F32)

    LDI = p.sem("LDI")

    @block.gpsimd
    def _(gp):
        LDI.inc(gp.dma_start(out=identb[:], in_=idb_d.ap()), 16)
        for w_d, w_sb in ((wrt_d, WRT), (wkt_d, WKT), (wvt_d, WVT)):
            LDC.inc(gp.dma_start(
                out=w_sb[:].rearrange("p (c d) -> p c d", c=NC),
                in_=w_d.rearrange("(c p) d -> p c d", p=128)), 16)
    ldi_all = LDI.n
    ldc_all = LDC.n

    @block.sync
    def _(sync):
        LDM.inc(sync.dma_start(out=MU[:], in_=mus_d), 16)
    ldm_all = LDM.n

    @block.vector
    def _(v):
        v.wait_ge(LDM.h, ldm_all)
        DV.inc(v.memset(
            XLNP.ap().rearrange("p (c w) -> p c w", c=NC)[:, :, 0:1], 0.0))

    emit_ln(p, block, Xap, xlnt=XLNP, cst=CST_FFN, off=1,
            pre="ln", identb=identb, identb_ready=(LDI, ldi_all))
    cp_ready = CP.n

    XRB = p.sbuf("xrb", [128, NC * 512], BF16)
    XKB = p.sbuf("xkb", [128, NC * 512], BF16)
    DBUF = p.sbuf("dbuf", [128, 512], BF16)
    KAB = p.sbuf("kab", [128, NC * 512], BF16)
    DKR = p.sbuf("dkr", [128, 2 * 512], BF16)
    RB = p.sbuf("rb", [128, 4 * D], F32)
    TOUT = p.sbuf("tout", [128, 2 * D], F32)
    SX2 = p.sbuf("sx2", [128, 2 * D], F32)
    PA = p.psum("pa", [128, 2048], F32)
    PB = p.psum("pb", [128, 1024], F32)

    dv0, act0, mm0 = DV.n, ACT.n, MM.n
    ldx2 = DmaStream(p, "ffn_ldx2", 2)
    stx = DmaStream(p, "ffn_stx", 2)
    NB = 8
    DV_B = 3 * NC + NC + 2 * 4
    MM_B = 8 + 8 + 8
    ACT_B = 4 + NC

    mm_r = lambda bI, tsub: mm0 + MM_B * bI + 2 * (tsub + 1)
    mm_k = lambda bI, dc: mm0 + MM_B * bI + 8 + dc + 1
    mm_v = lambda bI, tsub: mm0 + MM_B * bI + 16 + 2 * (tsub + 1)
    mm_blk = lambda bI: mm0 + MM_B * bI
    act_sig = lambda bI, tsub: act0 + ACT_B * bI + tsub + 1
    act_sq = lambda bI, dc: act0 + ACT_B * bI + 4 + dc + 1
    dv_mix = lambda bI: dv0 + DV_B * bI + 3 * NC
    dv_relu = lambda bI, dc: dv0 + DV_B * bI + 3 * NC + dc + 1
    dv_comb1 = lambda bI, u: dv0 + DV_B * bI + 32 + 2 * u + 1
    dv_comb2 = lambda bI, u: dv0 + DV_B * bI + 32 + 2 * u + 2

    @block.sync
    def _(sync):
        for i in range(NT):
            if i >= 2:
                sync.wait_ge(DV.h, dv_comb2(*divmod(i - 2, 4)))
            ldx2.issue(sync.dma_start(out=SX2[:, (i % 2) * D:(i % 2 + 1) * D],
                                      in_=Xap[i * 128:(i + 1) * 128, :]))

    @block.vector
    def _(v):
        v.wait_ge(CP.h, cp_ready)
        for bI in range(NB):
            b0 = bI * 512
            if bI >= 1:
                v.wait_ge(MM.h, mm_blk(bI - 1) + 16)
            for c in range(NC):
                base = c * CST_FFN
                xT = XLNP[:, base + 1 + b0:base + 513 + b0]
                xP = XLNP[:, base + b0:base + 512 + b0]
                v.wait_ge(DV.h, DV.n)
                DV.inc(v.tensor_tensor(DBUF[:], xT, xP, ALU.subtract))
                v.wait_ge(DV.h, DV.n)
                DV.inc(v.scalar_tensor_tensor(XRB[:, c * 512:(c + 1) * 512],
                                              DBUF[:], MU[:, c:c + 1], xP,
                                              ALU.mult, ALU.add))
                DV.inc(v.scalar_tensor_tensor(XKB[:, c * 512:(c + 1) * 512],
                                              DBUF[:], MU[:, 8 + c:9 + c], xP,
                                              ALU.mult, ALU.add))
            for dc in range(NC):
                v.wait_ge(MM.h, mm_k(bI, dc))
                if dc >= 2:
                    v.wait_ge(ACT.h, act_sq(bI, dc - 2))
                elif bI >= 1:
                    v.wait_ge(ACT.h, act_sq(bI - 1, dc + 6))
                DV.inc(v.tensor_single_scalar(
                    DKR[:, (dc % 2) * 512:(dc % 2 + 1) * 512],
                    PB[:, (dc % 2) * 512:(dc % 2 + 1) * 512], 0.0, ALU.max))
            for u in range(4):
                i = bI * 4 + u
                v.wait_ge(MM.h, mm_v(bI, u))
                ldx2.wait(v, i)
                if i >= 2:
                    stx.wait(v, i - 2)
                ob = (u % 2) * D
                DV.inc(v.tensor_tensor(TOUT[:, ob:ob + D],
                                       PA[:, ob:ob + D],
                                       RB[:, u * D:(u + 1) * D], ALU.mult))
                v.wait_ge(DV.h, DV.n)
                DV.inc(v.tensor_tensor(TOUT[:, ob:ob + D], TOUT[:, ob:ob + D],
                                       SX2[:, (i % 2) * D:(i % 2 + 1) * D],
                                       ALU.add))
    assert DV.n == dv0 + DV_B * NB

    @block.scalar
    def _(sc):
        for bI in range(NB):
            for tsub in range(4):
                sc.wait_ge(MM.h, mm_r(bI, tsub))
                if bI >= 1:
                    sc.wait_ge(DV.h, dv_comb1(bI - 1, tsub))
                ACT.inc(sc.activation(RB[:, tsub * D:(tsub + 1) * D],
                                      PA[:, (tsub % 2) * D:(tsub % 2 + 1) * D],
                                      AF.Sigmoid))
            for dc in range(NC):
                sc.wait_ge(DV.h, dv_relu(bI, dc))
                if bI >= 1:
                    sc.wait_ge(MM.h, mm_blk(bI))
                ACT.inc(sc.activation(KAB[:, dc * 512:(dc + 1) * 512],
                                      DKR[:, (dc % 2) * 512:(dc % 2 + 1) * 512],
                                      AF.Square))
    assert ACT.n == act0 + ACT_B * NB

    @block.tensor
    def _(te):
        te.wait_ge(LDC.h, ldc_all)
        for bI in range(NB):
            te.wait_ge(DV.h, dv_mix(bI))
            for tsub in range(4):
                if tsub >= 2:
                    te.wait_ge(ACT.h, act_sig(bI, tsub - 2))
                elif bI >= 1:
                    te.wait_ge(DV.h, dv_comb1(bI - 1, tsub + 2))
                for dn in range(2):
                    ins = None
                    for c in range(NC):
                        ins = te.matmul(
                            PA[:, (tsub % 2) * D + dn * 512:(tsub % 2) * D + (dn + 1) * 512],
                            XRB[:, c * 512 + tsub * 128:c * 512 + (tsub + 1) * 128],
                            WRT[:, c * D + dn * 512:c * D + (dn + 1) * 512],
                            start=(c == 0), stop=(c == NC - 1))
                    MM.inc(ins)
            for dc in range(NC):
                if dc >= 2:
                    te.wait_ge(DV.h, dv_relu(bI, dc - 2))
                elif bI >= 1:
                    te.wait_ge(DV.h, dv_relu(bI - 1, dc + 6))
                ins = None
                for c in range(NC):
                    ins = te.matmul(
                        PB[:, (dc % 2) * 512:(dc % 2 + 1) * 512],
                        WKT[:, c * D + dc * 128:c * D + (dc + 1) * 128],
                        XKB[:, c * 512:(c + 1) * 512],
                        start=(c == 0), stop=(c == NC - 1))
                MM.inc(ins)
            te.wait_ge(ACT.h, act_sq(bI, NC - 1))
            for tsub in range(4):
                if tsub >= 2:
                    te.wait_ge(DV.h, dv_comb1(bI, tsub - 2))
                else:
                    te.wait_ge(ACT.h, act_sig(bI, tsub + 2))
                for dn in range(2):
                    ins = None
                    for dc in range(NC):
                        ins = te.matmul(
                            PA[:, (tsub % 2) * D + dn * 512:(tsub % 2) * D + (dn + 1) * 512],
                            KAB[:, dc * 512 + tsub * 128:dc * 512 + (tsub + 1) * 128],
                            WVT[:, dc * D + dn * 512:dc * D + (dn + 1) * 512],
                            start=(dc == 0), stop=(dc == NC - 1))
                    MM.inc(ins)
    assert MM.n == mm0 + MM_B * NB

    @block.gpsimd
    def _(gp):
        for i in range(NT):
            bI, u = divmod(i, 4)
            gp.wait_ge(DV.h, dv_comb2(bI, u))
            stx.issue(gp.dma_start(out=XOap[i * 128:(i + 1) * 128, :],
                                   in_=TOUT[:, (u % 2) * D:(u % 2 + 1) * D]))
        for t in range(NT - 2, NT):
            stx.wait(gp, t)


def build_l1ffn():
    p = Prog("l1ffn")
    nc = p.nc
    X = nc.dram_tensor("X", [S, D], F32, kind="ExternalInput").ap()
    WRT = nc.dram_tensor("WRT", [D, D], F32, kind="ExternalInput").ap()
    WKT = nc.dram_tensor("WKT", [D, D], F32, kind="ExternalInput").ap()
    WVT = nc.dram_tensor("WVT", [D, D], F32, kind="ExternalInput").ap()
    MUS = nc.dram_tensor("MUS", [128, 16], F32, kind="ExternalInput").ap()
    XO = nc.dram_tensor("XO", [S, D], F32, kind="ExternalOutput").ap()
    with nc.Block() as block:
        emit_l1ffn(p, block, X, XO, WRT, WKT, WVT, MUS)
    return p


def prep_l1ffn_inputs(x_b, W_r, W_k, W_v, mu_r, mu_k):
    mus = np.concatenate([np.asarray(mu_r).reshape(8, 128).T,
                          np.asarray(mu_k).reshape(8, 128).T], axis=1)
    return {
        "X": np.ascontiguousarray(x_b),
        "WRT": np.ascontiguousarray(np.asarray(W_r).T),
        "WKT": np.ascontiguousarray(np.asarray(W_k).T),
        "WVT": np.ascontiguousarray(np.asarray(W_v).T),
        "MUS": np.ascontiguousarray(mus.astype(np.float32)),
    }




def emit_attn_a(p, block, Xap, XOap, wqt_d, wkt_d, bias_d):
    nc = p.nc
    LDC, LDM, LDI, DV, ACT, PE, ST, CP = (
        p.sem("LDC"), p.sem("LDM"), p.sem("LDI"), p.sem("DV"), p.sem("ACT"),
        p.sem("PE"), p.sem("ST"), p.sem("CP"))

    identb = p.sbuf("identb", [128, 128], BF16)
    idb_d = nc.inline_tensor(np.eye(128).astype(np.float32), p.uname("idb_d"))
    WQT = p.sbuf("wqt", [128, NC * D], BF16)
    WKT = p.sbuf("wkt", [128, NC * D], BF16)
    BIAS = p.sbuf("biasqk", [128, 16], F32)
    XTOK = p.sbuf("xtok", [128, NT * D], BF16)
    KT = p.sbuf("kt", [128, NC * S], BF16)
    XLT = p.sbuf("xlt", [128, NC * 512], BF16)
    QTI = p.sbuf("qti", [128, NC * 128], BF16)
    PSB = p.sbuf("psb", [128, 2 * 1024], BF16)
    PTS = p.sbuf("pts", [128, 2 * 1024], BF16)
    CS = p.sbuf("cs", [128, D], F32)
    CSROW = p.sbuf("csrow", [1, D], BF16)
    DEN = p.sbuf("den", [128, 8], F32)
    ONE1 = p.sbuf("one1", [128, 1], BF16)
    ONER = p.sbuf("oner", [1, 128], BF16)
    SXA = p.sbuf("sxa", [128, 2 * D], F32)
    SCR8 = p.sbuf("scr8", [128, 2 * D], F32)

    PS_A = p.psum("ps_a", [128, 2048], F32)
    PS_B = p.psum("ps_b", [128, 2048], F32)
    A0 = PS_A[:, 0:1024]
    A1 = PS_A[:, 1024:2048]
    B0 = PS_B[:, 0:1024]
    B1 = PS_B[:, 1024:2048]
    STG = PS_A[:, 0:512].bitcast(BF16)
    SCPAIR = [A0, A1, B0, A0]
    PTRSTG = [PS_A[:, 0:512].bitcast(BF16), PS_A[:, 1024:1536].bitcast(BF16),
              PS_B[:, 0:512].bitcast(BF16), PS_A[:, 0:512].bitcast(BF16)]

    @block.gpsimd
    def _(gp):
        LDI.inc(gp.dma_start(out=identb[:], in_=idb_d.ap()), 16)
        for w_d, w_sb in ((wqt_d, WQT), (wkt_d, WKT)):
            LDC.inc(gp.dma_start(
                out=w_sb[:].rearrange("p (c d) -> p c d", c=NC),
                in_=w_d.rearrange("(c p) d -> p c d", p=128)), 16)
    ldc_all, ldi_all = LDC.n, LDI.n

    @block.sync
    def _(sync):
        LDM.inc(sync.dma_start(out=BIAS[:], in_=bias_d), 16)
    ldm_all = LDM.n

    BIASQ = p.sbuf("biasq", [128, 8], F32)

    @block.vector
    def _(v):
        DV.inc(v.memset(ONE1[:], 1.0))
        DV.inc(v.memset(ONER[:], 1.0))
        v.wait_ge(LDM.h, ldm_all)
        DV.inc(v.tensor_scalar(BIASQ[:], BIAS[:, 0:8], 1.0 / 32.0, None,
                               ALU.mult))

    dv_ln0 = DV.n
    emit_ln(p, block, Xap, xlnt=None, xtok=XTOK, xcst=D, pre="lna",
            sx=SXA, sq=SCR8)
    assert DV.n == dv_ln0 + 7 * NT

    pe_k0, dv_k0, cp_k0 = PE.n, DV.n, CP.n
    NSL = 8

    @block.tensor
    def _(te):
        te.wait_ge(LDI.h, ldi_all)
        te.wait_ge(LDC.h, ldc_all)
        for sl in range(NSL):
            for u in range(4):
                tt = 4 * sl + u
                te.wait_ge(DV.h, dv_ln0 + 7 * tt + 7)
                if tt > 0:
                    te.wait_ge(CP.h, cp_k0 + tt)
                ins = None
                for cc in range(NC):
                    ins = te.transpose(STG[:, cc * 128:(cc + 1) * 128],
                                       XTOK[:, tt * D + cc * 128:tt * D + (cc + 1) * 128],
                                       identb[:])
                PE.inc(ins)
            for ec in range(NC):
                te.wait_ge(CP.h, cp_k0 + 4 * (sl + 1))
                if ec >= 2:
                    te.wait_ge(DV.h, dv_k0 + 8 * sl + (ec - 2) + 1)
                elif sl >= 1:
                    te.wait_ge(DV.h, dv_k0 + 8 * (sl - 1) + (ec + 6) + 1)
                ins = None
                for cc in range(NC):
                    ins = te.matmul(PS_B[:, (ec % 2) * 512:(ec % 2 + 1) * 512],
                                    WKT[:, cc * D + ec * 128:cc * D + (ec + 1) * 128],
                                    XLT[:, cc * 512:(cc + 1) * 512],
                                    start=(cc == 0), stop=(cc == NC - 1))
                PE.inc(ins)
    assert PE.n == pe_k0 + NSL * 12

    @block.vector
    def _(v):
        v.wait_ge(LDM.h, ldm_all)
        for sl in range(NSL):
            for u in range(4):
                v.wait_ge(PE.h, pe_k0 + 12 * sl + u + 1)
                dst = XLT.ap().rearrange("p (c w) -> p c w", c=NC)[
                    :, :, u * 128:(u + 1) * 128]
                src = STG.rearrange("p (c x) -> p c x", c=NC)
                CP.inc(v.tensor_copy(dst, src))
            for ec in range(NC):
                v.wait_ge(PE.h, pe_k0 + 12 * sl + 4 + ec + 1)
                DV.inc(v.tensor_scalar(
                    KT[:, ec * S + sl * 512:ec * S + (sl + 1) * 512],
                    PS_B[:, (ec % 2) * 512:(ec % 2 + 1) * 512],
                    BIAS[:, 8 + ec:9 + ec], None, ALU.add))
    assert CP.n == cp_k0 + 4 * NSL
    assert DV.n == dv_k0 + 8 * NSL

    pe_c0, dv_c0 = PE.n, DV.n

    @block.tensor
    def _(te):
        te.wait_ge(CP.h, cp_k0 + 4 * NSL)
        te.wait_ge(DV.h, dv_k0 + 8 * NSL)
        for tt in range(NT):
            for nh in range(2):
                ins = te.matmul(PS_A[0:1, nh * 512:(nh + 1) * 512],
                                ONE1[:], XTOK[:, tt * D + nh * 512:tt * D + (nh + 1) * 512],
                                start=(tt == 0), stop=(tt == NT - 1))
        PE.inc(ins)

    @block.vector
    def _(v):
        v.wait_ge(PE.h, pe_c0 + 1)
        DV.inc(v.tensor_scalar(CSROW[:], PS_A[0:1, 0:1024], -2.0 / S, None,
                               ALU.mult))

    @block.tensor
    def _(te):
        te.wait_ge(DV.h, dv_c0 + 1)
        ins = None
        for nh in range(2):
            ins = te.matmul(PS_B[:, nh * 512:(nh + 1) * 512], ONER[:],
                            CSROW[0:1, nh * 512:(nh + 1) * 512],
                            start=True, stop=True)
        PE.inc(ins)

    @block.vector
    def _(v):
        v.wait_ge(PE.h, pe_c0 + 2)
        DV.inc(v.tensor_copy(CS[:], B0))

    pe0, dv0, act0 = PE.n, DV.n, ACT.n
    PE_I, DV_I, ACT_I = 25, 13, 4
    ldr = DmaStream(p, "aa_ldr", 2)
    stx = DmaStream(p, "aa_stx", 2)

    peb = lambda it: pe0 + PE_I * it
    dvb = lambda it: dv0 + DV_I * it
    actb = lambda it: act0 + ACT_I * it

    PE_TPX = 1
    PE_Q = lambda ec: 1 + ec + 1
    PE_SC = {0: (10, 11), 1: (12, 13), 2: (16, 17), 3: (20, 21)}
    PE_PTR = {0: 14, 1: 18, 2: 22, 3: 24}
    PE_PV = {0: 15, 1: 19, 2: 23, 3: 25}
    DV_XLTI = 1
    DV_QTI = 2
    DV_PTS = {0: 3, 1: 4, 2: 5, 3: 6}
    DV_E3 = 13

    @block.sync
    def _(sync):
        for it in range(NT):
            if it >= 2:
                sync.wait_ge(DV.h, dvb(it - 2) + DV_E3)
            else:
                sync.wait_ge(DV.h, dv_ln0 + 7 * NT)
            ldr.issue(sync.dma_start(out=SXA[:, (it % 2) * D:(it % 2 + 1) * D],
                                     in_=Xap[it * 128:(it + 1) * 128, :]))

    @block.tensor
    def _(te):
        for it in range(NT):
            if it >= 1:
                te.wait_ge(DV.h, dvb(it - 1) + DV_PTS[3])
            else:
                te.wait_ge(DV.h, dv0)
            ins = None
            for cc in range(NC):
                ins = te.transpose(STG[:, cc * 128:(cc + 1) * 128],
                                   XTOK[:, it * D + cc * 128:it * D + (cc + 1) * 128],
                                   identb[:])
            PE.inc(ins)
            if it >= 1:
                te.wait_ge(DV.h, dvb(it - 1) + DV_PTS[1])
            for ec in range(NC):
                te.wait_ge(DV.h, dvb(it) + DV_XLTI)
                ins = None
                for cc in range(NC):
                    ins = te.matmul(A1[:, ec * 128:(ec + 1) * 128],
                                    WQT[:, cc * D + ec * 128:cc * D + (ec + 1) * 128],
                                    XLT[:, cc * 512:cc * 512 + 128],
                                    start=(cc == 0), stop=(cc == NC - 1))
                PE.inc(ins)

            def sc(g):
                te.wait_ge(DV.h, dvb(it) + DV_QTI)
                if g == 3:
                    te.wait_ge(DV.h, dvb(it) + DV_PTS[0])
                for js in range(2):
                    j0 = g * 1024 + js * 512
                    ins = None
                    for ec in range(NC):
                        ins = te.matmul(
                            SCPAIR[g][:, js * 512:(js + 1) * 512],
                            QTI[:, ec * 128:(ec + 1) * 128],
                            KT[:, ec * S + j0:ec * S + j0 + 512],
                            start=(ec == 0), stop=(ec == NC - 1))
                    PE.inc(ins)

            def ptr(g):
                te.wait_ge(ACT.h, actb(it) + g + 1)
                ins = None
                for l in range(8):
                    ins = te.transpose(
                        PTRSTG[g][:, l * 128:(l + 1) * 128],
                        PSB[:, (g % 2) * 1024 + l * 128:(g % 2) * 1024 + (l + 1) * 128],
                        identb[:])
                PE.inc(ins)

            def pv(g):
                te.wait_ge(DV.h, dvb(it) + DV_PTS[g])
                ins = None
                for l in range(8):
                    jc = g * 8 + l
                    for nh in range(2):
                        ins = te.matmul(
                            B1[:, nh * 512:(nh + 1) * 512],
                            PTS[:, (g % 2) * 1024 + l * 128:(g % 2) * 1024 + (l + 1) * 128],
                            XTOK[:, jc * D + nh * 512:jc * D + (nh + 1) * 512],
                            start=(g == 0 and l == 0), stop=(g == 3 and l == 7),
                            skip_group_check=True)
                PE.inc(ins)

            sc(0); sc(1); ptr(0); pv(0); sc(2); ptr(1); pv(1)
            sc(3); ptr(2); pv(2); ptr(3); pv(3)
    assert PE.n == pe0 + PE_I * NT

    @block.scalar
    def _(sc_):
        for it in range(NT):
            if it >= 1:
                sc_.wait_ge(DV.h, dvb(it - 1) + 8)
            for g in range(4):
                sc_.wait_ge(PE.h, peb(it) + PE_SC[g][1])
                if g >= 2:
                    sc_.wait_ge(PE.h, peb(it) + PE_PTR[g - 2])
                ACT.inc(sc_.activation(PSB[:, (g % 2) * 1024:(g % 2 + 1) * 1024],
                                       SCPAIR[g], AF.Exp,
                                       accum_out=DEN[:, g:g + 1]))
    assert ACT.n == act0 + ACT_I * NT

    @block.vector
    def _(v):
        for it in range(NT):
            v.wait_ge(PE.h, peb(it) + PE_TPX)
            src = STG.rearrange("p (c x) -> p c x", c=NC)
            dst = XLT.ap().rearrange("p (c w) -> p c w", c=NC)[:, :, 0:128]
            DV.inc(v.tensor_copy(dst, src))
            v.wait_ge(PE.h, peb(it) + PE_Q(7))
            DV.inc(v.scalar_tensor_tensor(
                QTI[:].rearrange("p (e x) -> p e x", e=NC),
                A1.rearrange("p (e x) -> p e x", e=NC),
                1.0 / 32.0,
                BIASQ[:].unsqueeze(2).broadcast_to([128, 8, 128]),
                ALU.mult, ALU.add))
            for g in range(4):
                v.wait_ge(PE.h, peb(it) + PE_PTR[g])
                src = PTRSTG[g].rearrange("p (l x) -> p l x", l=8)
                dst = PTS[:, (g % 2) * 1024:(g % 2 + 1) * 1024].rearrange(
                    "p (l x) -> p l x", l=8)
                DV.inc(v.tensor_copy(dst, src))
            v.wait_ge(ACT.h, actb(it) + 4)
            DV.inc(v.tensor_tensor(DEN[:, 4:5], DEN[:, 0:1], DEN[:, 1:2], ALU.add))
            DV.inc(v.tensor_tensor(DEN[:, 5:6], DEN[:, 2:3], DEN[:, 3:4], ALU.add))
            v.wait_ge(DV.h, DV.n)
            DV.inc(v.tensor_tensor(DEN[:, 6:7], DEN[:, 4:5], DEN[:, 5:6], ALU.add))
            v.wait_ge(DV.h, DV.n)
            DV.inc(v.reciprocal(DEN[:, 7:8], DEN[:, 6:7]))
            v.wait_ge(DV.h, DV.n)
            v.wait_ge(PE.h, peb(it) + PE_PV[3])
            if it >= 2:
                stx.wait(v, it - 2)
            ob = (it % 2) * D
            DV.inc(v.scalar_tensor_tensor(SCR8[:, ob:ob + D], B1,
                                          DEN[:, 7:8],
                                          XTOK[:, it * D:(it + 1) * D],
                                          ALU.mult, ALU.add))
            v.wait_ge(DV.h, DV.n)
            DV.inc(v.tensor_tensor(SCR8[:, ob:ob + D], SCR8[:, ob:ob + D],
                                   CS[:], ALU.add))
            ldr.wait(v, it)
            v.wait_ge(DV.h, DV.n)
            DV.inc(v.tensor_tensor(SCR8[:, ob:ob + D], SCR8[:, ob:ob + D],
                                   SXA[:, ob:ob + D], ALU.add))
    assert DV.n == dv0 + DV_I * NT

    @block.gpsimd
    def _(gp):
        for it in range(NT):
            gp.wait_ge(DV.h, dvb(it) + DV_E3)
            stx.issue(gp.dma_start(out=XOap[it * 128:(it + 1) * 128, :],
                                   in_=SCR8[:, (it % 2) * D:(it % 2 + 1) * D]))
        for t in (NT - 2, NT - 1):
            stx.wait(gp, t)


def build_attn_a():
    p = Prog("attna")
    nc = p.nc
    X = nc.dram_tensor("X", [S, D], F32, kind="ExternalInput").ap()
    WQT = nc.dram_tensor("WQT", [D, D], F32, kind="ExternalInput").ap()
    WKT = nc.dram_tensor("WKT", [D, D], F32, kind="ExternalInput").ap()
    BIAS = nc.dram_tensor("BIAS", [128, 16], F32, kind="ExternalInput").ap()
    XO = nc.dram_tensor("XO", [S, D], F32, kind="ExternalOutput").ap()
    with nc.Block() as block:
        emit_attn_a(p, block, X, XO, WQT, WKT, BIAS)
    return p


def prep_attn_a_inputs(x_b, Wq, bq, Wk, bk):
    bias = np.concatenate([np.asarray(bq).reshape(8, 128).T,
                           np.asarray(bk).reshape(8, 128).T], axis=1)
    return {
        "X": np.ascontiguousarray(x_b),
        "WQT": np.ascontiguousarray(np.asarray(Wq).T),
        "WKT": np.ascontiguousarray(np.asarray(Wk).T),
        "BIAS": np.ascontiguousarray(bias.astype(np.float32)),
    }




def emit_attn_b(p, block, Xap, XOap, wqkv_d, wout_d, bout_d, cosf_d, sinf_d,
                scale=0.125):
    nc = p.nc
    LDC, LDM, LDI, DV, ACT, PE, CP = (
        p.sem("LDC"), p.sem("LDM"), p.sem("LDI"), p.sem("DV"), p.sem("ACT"),
        p.sem("PE"), p.sem("CP"))
    GRP = p.sem("GRP")

    identb = p.sbuf("identb", [128, 128], BF16)
    idb_d = nc.inline_tensor(np.eye(128).astype(np.float32), p.uname("idb_d"))
    XLNT = p.sbuf("xlnt", [128, NC * S], BF16)
    WGB = p.sbuf("wgb", [128, 2 * NC * 384], BF16)
    WOB = p.sbuf("wob", [128, NC * D], BF16)
    COSF = p.sbuf("cosf", [128, 512], F32)
    SINF = p.sbuf("sinf", [128, 512], F32)
    GQ = p.sbuf("gq", [128, NT * 128], BF16)
    GK = p.sbuf("gk", [128, NT * 128], BF16)
    GQR = p.sbuf("gqr", [128, NT * 128], BF16)
    GKR = p.sbuf("gkr", [128, NT * 128], BF16)
    GVE = p.sbuf("gve", [128, NT * 130], BF16)
    QT = p.sbuf("qtg", [128, S], BF16)
    KT = p.sbuf("ktg", [128, S], BF16)
    PTSA = p.sbuf("ptsa", [128, 1536], BF16)
    PTSB = p.sbuf("ptsb", [128, 1536], BF16)
    RT1 = p.sbuf("rt1", [128, 1024], BF16)
    RT2 = p.sbuf("rt2", [128, 1024], BF16)
    DNI = p.sbuf("dni", [1, 1024], F32)
    BCC = p.sbuf("bcc", [128, 1024], F32)
    OUTB = p.sbuf("outb", [128, 1024], BF16)
    ONER = p.sbuf("oner", [1, 128], F32)
    BOUTB = p.sbuf("boutb", [128, D], F32)
    BOUTR = p.sbuf("boutr", [1, D], F32)
    OTI = p.sbuf("oti", [128, 2 * NC * 128], BF16)
    SXA = p.sbuf("sxa", [128, 2 * D], F32)
    SCR8 = p.sbuf("scr8", [128, 2 * D], F32)

    PS_S1 = p.psum("ps_s1", [128, 1536], F32)
    PS_S2 = p.psum("ps_s2", [128, 1536], F32)
    PS_P = p.psum("ps_p", [128, 1024], F32)
    LNPT = PS_S1[:, 0:1024].bitcast(BF16)
    PJ = PS_P
    PVA = PS_P[:, 0:512]
    PVB = PS_P[:, 512:1024]
    TSTG = PS_P[:, 0:512].bitcast(BF16)

    ODR = nc.dram_tensor(p.uname("odr"), [NC, 128, S], BF16, kind="Internal").ap()

    @block.gpsimd
    def _(gp):
        LDI.inc(gp.dma_start(out=identb[:], in_=idb_d.ap()), 16)
        LDC.inc(gp.dma_start(
            out=WOB[:].rearrange("p (c d) -> p c d", c=NC),
            in_=wout_d.rearrange("(c p) d -> p c d", p=128)), 16)
    ldc_all, ldi_all = LDC.n, LDI.n

    @block.sync
    def _(sync):
        LDM.inc(sync.dma_start(out=COSF[:], in_=cosf_d), 16)
        LDM.inc(sync.dma_start(out=SINF[:], in_=sinf_d), 16)
        LDM.inc(sync.dma_start(out=BOUTR[:], in_=bout_d), 16)
    ldm_all = LDM.n

    dv_pre0 = DV.n

    @block.vector
    def _(v):
        DV.inc(v.memset(ONER[:], 1.0))
        ones_view = GVE.ap().rearrange("p (t h e) -> p t h e", t=NT, h=2)[
            :, :, :, 64:65]
        DV.inc(v.memset(ones_view, 1.0))

    pe_pre0 = PE.n

    @block.tensor
    def _(te):
        te.wait_ge(LDC.h, ldc_all)
        te.wait_ge(LDM.h, ldm_all)
        te.wait_ge(DV.h, dv_pre0 + 1)
        ins = None
        for nh in range(2):
            ins = te.matmul(PS_S2[:, nh * 512:(nh + 1) * 512], ONER[:],
                            BOUTR[0:1, nh * 512:(nh + 1) * 512],
                            start=True, stop=True)
        PE.inc(ins)

    @block.vector
    def _(v):
        v.wait_ge(PE.h, pe_pre0 + 1)
        DV.inc(v.tensor_copy(BOUTB[:], PS_S2[:, 0:1024]))

    emit_ln(p, block, Xap, xlnt=XLNT, cst=S, off=0, pre="lnb",
            identb=identb, identb_ready=(LDI, ldi_all),
            sx=SXA, sq=SCR8, pt=LNPT)
    cp_ln = CP.n

    wg = DmaStream(p, "bw", 2)

    def _issue_wg(gp, g):
        wg.issue(gp.dma_start(
            out=WGB[:, (g % 2) * NC * 384:(g % 2 + 1) * NC * 384]
                .rearrange("p (c d) -> p c d", c=NC),
            in_=wqkv_d[:, g * 384:(g + 1) * 384]
                .rearrange("(c p) d -> p c d", p=128)))

    @block.gpsimd
    def _(gp):
        _issue_wg(gp, 0)
        _issue_wg(gp, 1)

    KGS = [3] * 10 + [2]
    KGO = [0, 3, 6, 9, 12, 15, 18, 21, 24, 27, 30]
    NKG = 11

    for g in range(NCORE):
        pe0, dv0, act0, cp0 = PE.n, DV.n, ACT.n, CP.n
        wslot = (g % 2) * NC * 384

        @block.tensor
        def _(te):
            te.wait_ge(CP.h, cp_ln)
            wg.wait(te, g)
            if g >= 1:
                te.wait_ge(GRP.h, g)
            for t in range(NT):
                if t >= 2:
                    te.wait_ge(DV.h, dv0 + 3 * (t - 1))
                ins = None
                for cc in range(NC):
                    ins = te.matmul(PJ[:, (t % 2) * 512:(t % 2) * 512 + 384],
                                    XLNT[:, cc * S + t * 128:cc * S + (t + 1) * 128],
                                    WGB[:, wslot + cc * 384:wslot + (cc + 1) * 384],
                                    start=(cc == 0), stop=(cc == NC - 1))
                PE.inc(ins)

        @block.vector
        def _(v):
            v.wait_ge(LDM.h, ldm_all)
            for t in range(NT):
                v.wait_ge(PE.h, pe0 + t + 1)
                base = (t % 2) * 512
                DV.inc(v.tensor_copy(GQ[:, t * 128:(t + 1) * 128],
                                     PJ[:, base:base + 128]))
                DV.inc(v.tensor_copy(GK[:, t * 128:(t + 1) * 128],
                                     PJ[:, base + 128:base + 256]))
                dstv = GVE.ap().rearrange("p (t h e) -> p t h e", t=NT, h=2)[
                    :, t:t + 1, :, 0:64]
                DV.inc(v.tensor_copy(
                    dstv, PJ[:, base + 256:base + 384]
                    .rearrange("p (o h d) -> p o h d", o=1, h=2)))

            def rope(srcT, dstT):
                se = srcT.ap().rearrange("p (t h i two) -> p t h i two",
                                         t=NT, h=2, two=2)[:, :, :, 0:16, 0:1]
                so = srcT.ap().rearrange("p (t h i two) -> p t h i two",
                                         t=NT, h=2, two=2)[:, :, :, 0:16, 1:2]
                de = dstT.ap().rearrange("p (t h i two) -> p t h i two",
                                         t=NT, h=2, two=2)[:, :, :, 0:16, 0:1]
                do = dstT.ap().rearrange("p (t h i two) -> p t h i two",
                                         t=NT, h=2, two=2)[:, :, :, 0:16, 1:2]
                cb = COSF[:].rearrange("p (t i) -> p t i", t=NT) \
                    .unsqueeze(2).broadcast_to([128, NT, 2, 16]).unsqueeze(4)
                sb = SINF[:].rearrange("p (t i) -> p t i", t=NT) \
                    .unsqueeze(2).broadcast_to([128, NT, 2, 16]).unsqueeze(4)
                r1 = RT1[:].rearrange("p (t h i) -> p t h i", t=NT, h=2) \
                    .unsqueeze(4)
                r2 = RT2[:].rearrange("p (t h i) -> p t h i", t=NT, h=2) \
                    .unsqueeze(4)
                DV.inc(v.tensor_tensor(r1, se, cb, ALU.mult))
                DV.inc(v.tensor_tensor(r2, so, sb, ALU.mult))
                v.wait_ge(DV.h, DV.n)
                DV.inc(v.tensor_tensor(de, r1, r2, ALU.subtract))
                v.wait_ge(DV.h, DV.n)
                DV.inc(v.tensor_tensor(r1, so, cb, ALU.mult))
                DV.inc(v.tensor_tensor(r2, se, sb, ALU.mult))
                v.wait_ge(DV.h, DV.n)
                DV.inc(v.tensor_tensor(do, r1, r2, ALU.add))
                v.wait_ge(DV.h, DV.n)
                sp = srcT.ap().rearrange("p (t h d) -> p t h d", t=NT, h=2)[
                    :, :, :, 32:64]
                dp = dstT.ap().rearrange("p (t h d) -> p t h d", t=NT, h=2)[
                    :, :, :, 32:64]
                DV.inc(v.tensor_copy(dp, sp))
            v.wait_ge(DV.h, DV.n)
            rope(GQ, GQR)
            rope(GK, GKR)
        assert DV.n == dv0 + 3 * NT + 14
        dv_rope_end = DV.n
        pe_tr0 = PE.n

        @block.tensor
        def _(te):
            te.wait_ge(DV.h, dv_rope_end)
            for which, src in ((0, GQR), (1, GKR)):
                for b4 in range(4):
                    if which * 4 + b4 > 0:
                        te.wait_ge(CP.h, cp0 + which * 4 + b4)
                    ins = None
                    for u in range(8):
                        t = b4 * 8 + u
                        ins = te.transpose(TSTG[:, u * 128:(u + 1) * 128],
                                           src[:, t * 128:(t + 1) * 128],
                                           identb[:])
                    PE.inc(ins)
        assert PE.n == pe_tr0 + 8

        @block.vector
        def _(v):
            for which in range(2):
                dstT = (QT, KT)[which]
                for b4 in range(4):
                    v.wait_ge(PE.h, pe_tr0 + which * 4 + b4 + 1)
                    CP.inc(v.tensor_copy(dstT[:, b4 * 1024:(b4 + 1) * 1024],
                                         TSTG))
        assert CP.n == cp0 + 8

        pe_i0, act_i0, dv_i0 = PE.n, ACT.n, DV.n
        peit = lambda it: pe_i0 + 24 * it
        actit = lambda it: act_i0 + 22 * it
        dvit = lambda it: dv_i0 + 6 * it

        @block.tensor
        def _(te):
            te.wait_ge(CP.h, cp0 + 8)
            for it in range(8):
                if it >= 1:
                    te.wait_ge(DV.h, dvit(it - 1) + 6)
                for kg in range(NKG):
                    if kg >= 1:
                        te.wait_ge(ACT.h, actit(it) + 2 * kg)
                    i1 = None
                    for j in range(KGS[kg]):
                        jc = KGO[kg] + j
                        te.matmul(PS_S1[:, j * 512:(j + 1) * 512],
                                  KT[0:64, jc * 128:(jc + 1) * 128],
                                  QT[0:64, it * 512:(it + 1) * 512],
                                  start=True, stop=True, tile_position=(0, 0),
                                  skip_group_check=True)
                        i1 = te.matmul(PS_S2[:, j * 512:(j + 1) * 512],
                                       KT[64:128, jc * 128:(jc + 1) * 128],
                                       QT[64:128, it * 512:(it + 1) * 512],
                                       start=True, stop=True,
                                       tile_position=(64, 0),
                                       skip_group_check=True)
                    PE.inc(i1)
                    te.wait_ge(ACT.h, actit(it) + 2 * (kg + 1))
                    ins = None
                    for j in range(KGS[kg]):
                        jc = KGO[kg] + j
                        st = (kg == 0 and j == 0)
                        sp = (kg == NKG - 1 and j == KGS[kg] - 1)
                        te.matmul(PVA[0:65, :], GVE[:, jc * 130:jc * 130 + 65],
                                  PTSA[:, j * 512:(j + 1) * 512],
                                  start=st, stop=sp, skip_group_check=True)
                        ins = te.matmul(PVB[0:65, :],
                                        GVE[:, jc * 130 + 65:jc * 130 + 130],
                                        PTSB[:, j * 512:(j + 1) * 512],
                                        start=st, stop=sp,
                                        skip_group_check=True)
                    PE.inc(ins)
                te.wait_ge(DV.h, dvit(it) + 1)
                PE.inc(te.matmul(PS_S2[0:64, 0:512], ONER[0:1, 0:64],
                                 DNI[0:1, 0:512], start=True, stop=True,
                                 skip_group_check=True))
                te.wait_ge(DV.h, dvit(it) + 2)
                PE.inc(te.matmul(PS_S2[0:64, 512:1024], ONER[0:1, 0:64],
                                 DNI[0:1, 512:1024], start=True, stop=True,
                                 skip_group_check=True))
        assert PE.n == pe_i0 + 24 * 8

        @block.scalar
        def _(sc_):
            for it in range(8):
                for kg in range(NKG):
                    w = KGS[kg] * 512
                    sc_.wait_ge(PE.h, peit(it) + 2 * kg + 1)
                    ACT.inc(sc_.activation(PTSA[:, 0:w], PS_S1[:, 0:w],
                                           AF.Exp, scale=scale))
                    ACT.inc(sc_.activation(PTSB[:, 0:w], PS_S2[:, 0:w],
                                           AF.Exp, scale=scale))
        assert ACT.n == act_i0 + 22 * 8

        od = DmaStream(p, "od%d" % g, 2)

        @block.vector
        def _(v):
            for it in range(8):
                v.wait_ge(PE.h, peit(it) + 22)
                if it >= 1:
                    for tt in (2 * (it - 1), 2 * (it - 1) + 1):
                        od.wait(v, tt)
                DV.inc(v.reciprocal(DNI[0:1, 0:512], PVA[64:65, :]))
                v.wait_ge(PE.h, peit(it) + 23)
                DV.inc(v.reciprocal(DNI[0:1, 512:1024], PVB[64:65, :]))
                v.wait_ge(PE.h, peit(it) + 23)
                DV.inc(v.tensor_copy(BCC[:, 0:512], PS_S2[:, 0:512]))
                v.wait_ge(PE.h, peit(it) + 24)
                DV.inc(v.tensor_copy(BCC[:, 512:1024], PS_S2[:, 512:1024]))
                v.wait_ge(DV.h, DV.n)
                DV.inc(v.tensor_tensor(OUTB[0:64, 0:512], PVA[0:64, :],
                                       BCC[0:64, 0:512], ALU.mult))
                DV.inc(v.tensor_tensor(OUTB[0:64, 512:1024], PVB[0:64, :],
                                       BCC[0:64, 512:1024], ALU.mult))
        assert DV.n == dv_i0 + 6 * 8

        @block.gpsimd
        def _(gp):
            for it in range(8):
                gp.wait_ge(DV.h, dvit(it) + 6)
                od.issue(gp.dma_start(
                    out=ODR[g, 0:64, it * 512:(it + 1) * 512],
                    in_=OUTB[0:64, 0:512]))
                od.issue(gp.dma_start(
                    out=ODR[g, 64:128, it * 512:(it + 1) * 512],
                    in_=OUTB[0:64, 512:1024]))
            if g + 2 < NCORE:
                _issue_wg(gp, g + 2)

        @block.vector
        def _(v):
            for tt in (14, 15):
                od.wait(v, tt)
            GRP.inc(v.engine_nop())

    pe_o0, dv_o0 = PE.n, DV.n
    oti = DmaStream(p, "oti", 2)
    ldr = DmaStream(p, "br", 2)
    stx = DmaStream(p, "bst", 2)

    @block.sync
    def _(sync):
        sync.wait_ge(GRP.h, NCORE)
        for t in range(NT):
            if t >= 2:
                sync.wait_ge(PE.h, pe_o0 + (t - 1))
            oti.issue(sync.dma_start(
                out=OTI[:, (t % 2) * NC * 128:(t % 2 + 1) * NC * 128]
                    .rearrange("p (c x) -> p c x", c=NC),
                in_=ODR[:, :, t * 128:(t + 1) * 128].transpose([1, 0, 2])))
            if t >= 2:
                sync.wait_ge(DV.h, dv_o0 + 2 * (t - 1))
            ldr.issue(sync.dma_start(out=SXA[:, (t % 2) * D:(t % 2 + 1) * D],
                                     in_=Xap[t * 128:(t + 1) * 128, :]))

    @block.tensor
    def _(te):
        for t in range(NT):
            oti.wait(te, t)
            if t >= 2:
                te.wait_ge(DV.h, dv_o0 + 2 * (t - 1))
            pdst = (PS_S1 if t % 2 == 0 else PS_S2)
            ins = None
            for nh in range(2):
                for cc in range(NC):
                    ins = te.matmul(
                        pdst[:, nh * 512:(nh + 1) * 512],
                        OTI[:, (t % 2) * NC * 128 + cc * 128:(t % 2) * NC * 128 + (cc + 1) * 128],
                        WOB[:, cc * D + nh * 512:cc * D + (nh + 1) * 512],
                        start=(cc == 0), stop=(cc == NC - 1),
                        skip_group_check=True)
            PE.inc(ins)
    assert PE.n == pe_o0 + NT

    @block.vector
    def _(v):
        for t in range(NT):
            v.wait_ge(PE.h, pe_o0 + t + 1)
            ldr.wait(v, t)
            if t >= 2:
                stx.wait(v, t - 2)
            psrc = (PS_S1 if t % 2 == 0 else PS_S2)
            ob = (t % 2) * D
            DV.inc(v.tensor_tensor(SCR8[:, ob:ob + D], psrc[:, 0:1024],
                                   BOUTB[:], ALU.add))
            v.wait_ge(DV.h, DV.n)
            DV.inc(v.tensor_tensor(SCR8[:, ob:ob + D], SCR8[:, ob:ob + D],
                                   SXA[:, ob:ob + D], ALU.add))
    assert DV.n == dv_o0 + 2 * NT

    @block.gpsimd
    def _(gp):
        for t in range(NT):
            gp.wait_ge(DV.h, dv_o0 + 2 * t + 2)
            stx.issue(gp.dma_start(out=XOap[t * 128:(t + 1) * 128, :],
                                   in_=SCR8[:, (t % 2) * D:(t % 2 + 1) * D]))
        for t in (NT - 2, NT - 1):
            stx.wait(gp, t)


def build_attn_b():
    p = Prog("attnb")
    nc = p.nc
    X = nc.dram_tensor("X", [S, D], F32, kind="ExternalInput").ap()
    WQKV = nc.dram_tensor("WQKV", [D, 8 * 384], F32, kind="ExternalInput").ap()
    WOUT = nc.dram_tensor("WOUT", [D, D], F32, kind="ExternalInput").ap()
    BOUT = nc.dram_tensor("BOUT", [1, D], F32, kind="ExternalInput").ap()
    COSF = nc.dram_tensor("COSF", [128, 512], F32, kind="ExternalInput").ap()
    SINF = nc.dram_tensor("SINF", [128, 512], F32, kind="ExternalInput").ap()
    XO = nc.dram_tensor("XO", [S, D], F32, kind="ExternalOutput").ap()
    with nc.Block() as block:
        emit_attn_b(p, block, X, XO, WQKV, WOUT, BOUT, COSF, SINF)
    return p


def rope_tables_l0b():
    inv = 1.0 / (10000.0 ** (np.arange(0, 32, 2, dtype=np.float64) / 32.0))
    ang = np.arange(S, dtype=np.float64)[:, None] * inv[None, :]
    cos = np.cos(ang).astype(np.float32)
    sin = np.sin(ang).astype(np.float32)
    cosf = cos.reshape(NT, 128, 16).transpose(1, 0, 2).reshape(128, NT * 16)
    sinf = sin.reshape(NT, 128, 16).transpose(1, 0, 2).reshape(128, NT * 16)
    return np.ascontiguousarray(cosf), np.ascontiguousarray(sinf)


def prep_attn_b_inputs(x_b, Wq, Wkv, Wout, bout):
    Wq = np.asarray(Wq)
    Wkv = np.asarray(Wkv)
    groups = []
    for g in range(8):
        groups.append(Wq[:, g * 128:(g + 1) * 128])
        groups.append(Wkv[:, g * 128:(g + 1) * 128])
        groups.append(Wkv[:, 1024 + g * 128:1024 + (g + 1) * 128])
    wqkv = np.concatenate(groups, axis=1)
    cosf, sinf = rope_tables_l0b()
    return {
        "X": np.ascontiguousarray(x_b),
        "WQKV": np.ascontiguousarray(wqkv),
        "WOUT": np.ascontiguousarray(np.asarray(Wout)),
        "BOUT": np.ascontiguousarray(np.asarray(bout).reshape(1, D)),
        "COSF": cosf, "SINF": sinf,
    }




def emit_attn_l1a(p, block, Xap, XOap, wqt_d, wkt_d, wvt_d, wot_d,
                  bq_d, bk_d, bv_d, bo_d, crow_d, srow_d, scale=0.125):
    nc = p.nc
    LDC, LDM, LDI, DV, ACT, PE, CP = (
        p.sem("LDC"), p.sem("LDM"), p.sem("LDI"), p.sem("DV"), p.sem("ACT"),
        p.sem("PE"), p.sem("CP"))
    HGD = p.sem("HGD")
    hgd0 = HGD.n

    identb = p.sbuf("identb", [128, 128], BF16)
    idb_d = nc.inline_tensor(np.eye(128).astype(np.float32), p.uname("idb_d"))
    XLNT = p.sbuf("xlnt", [128, NC * S], BF16)
    WQT = p.sbuf("wqt", [128, NC * D], BF16)
    WKT = p.sbuf("wkt", [128, NC * D], BF16)
    WVT = p.sbuf("wvt", [128, NC * D], BF16)
    BB = p.sbuf("bb", [128, 4 * D], BF16)
    BR = p.sbuf("br_", [1, D], F32)
    CREPC = p.sbuf("crepc", [128, 32], F32)
    SREPC = p.sbuf("srepc", [128, 32], F32)
    CSR = p.sbuf("csr", [1, 64], F32)
    ONER = p.sbuf("oner", [1, 128], F32)
    PREB = p.sbuf("preb", [128, 2 * D], BF16)
    QR = p.sbuf("qr", [128, 2 * D], BF16)
    KR = p.sbuf("kr", [128, 2 * D], BF16)
    KTT = p.sbuf("ktt", [128, 2048], BF16)
    QD8 = p.sbuf("qd8", [128, 1024], BF16)
    GVE2 = p.sbuf("gve2", [128, 32 * 65], BF16)
    PTSA = p.sbuf("ptsa", [128, 1536], BF16)
    PTSB = p.sbuf("ptsb", [128, 1536], BF16)
    DNI = p.sbuf("dni", [1, 512], F32)
    BCC = p.sbuf("bcc", [128, 512], F32)
    OUTB = p.sbuf("outb", [128, 512], BF16)
    OUTH = p.sbuf("outh", [64, S], BF16)
    OTI = p.sbuf("oti", [128, NC * 128], BF16)
    WOT = p.sbuf("wot", [128, NC * D], BF16)
    SXA = p.sbuf("sxa", [128, 2 * D], F32)
    SCR8 = p.sbuf("scr8", [128, 2 * D], F32)
    QTT = SXA[:].bitcast(BF16)

    PS_S1 = p.psum("ps_s1", [128, 1536], F32)
    PS_S2 = p.psum("ps_s2", [128, 1536], F32)
    PS_P = p.psum("ps_p", [128, 1024], F32)
    LNPT = PS_S1[:, 0:1024].bitcast(BF16)
    PJ = PS_P
    PVH = PS_P[:, 0:512]
    NBC = PS_P[:, 512:1024]
    TSTG = PS_S2[:, 0:512].bitcast(BF16)

    ODR = nc.dram_tensor(p.uname("odr"), [NC, 128, S], BF16, kind="Internal").ap()

    @block.gpsimd
    def _(gp):
        LDI.inc(gp.dma_start(out=identb[:], in_=idb_d.ap()), 16)
        for w_d, w_sb in ((wqt_d, WQT), (wkt_d, WKT), (wvt_d, WVT),
                          (wot_d, WOT)):
            LDC.inc(gp.dma_start(
                out=w_sb[:].rearrange("p (c d) -> p c d", c=NC),
                in_=w_d.rearrange("(c p) d -> p c d", p=128)), 16)
    ldc_all, ldi_all = LDC.n, LDI.n

    brs = DmaStream(p, "brs", 1)
    pe_pre0, dv_pre0 = PE.n, DV.n

    @block.sync
    def _(sync):
        LDM.inc(sync.dma_start(out=CSR[0:1, 0:32], in_=crow_d), 16)
        LDM.inc(sync.dma_start(out=CSR[0:1, 32:64], in_=srow_d), 16)
        for i, b_d in enumerate((bq_d, bk_d, bv_d, bo_d)):
            if i >= 1:
                sync.wait_ge(PE.h, pe_pre0 + i)
            brs.issue(sync.dma_start(out=BR[0:1, :], in_=b_d))
    ldm_all = LDM.n

    @block.vector
    def _(v):
        DV.inc(v.memset(ONER[:], 1.0))
        ones_view = GVE2.ap().rearrange("p (j e) -> p j e", j=32)[:, :, 64:65]
        DV.inc(v.memset(ones_view, 1.0))

    @block.tensor
    def _(te):
        te.wait_ge(LDM.h, ldm_all)
        te.wait_ge(DV.h, dv_pre0 + 1)
        for i in range(4):
            brs.wait(te, i)
            ins = None
            for nh in range(2):
                ins = te.matmul(
                    PS_S2[:, nh * 512:(nh + 1) * 512],
                    ONER[:], BR[0:1, nh * 512:(nh + 1) * 512],
                    start=True, stop=True)
            PE.inc(ins)
            te.wait_ge(DV.h, dv_pre0 + 3 + i)
        PE.inc(te.matmul(PS_P[:, 0:64], ONER[:], CSR[0:1, 0:64],
                         start=True, stop=True))

    @block.vector
    def _(v):
        for i in range(4):
            v.wait_ge(PE.h, pe_pre0 + i + 1)
            DV.inc(v.tensor_copy(BB[:, i * D:(i + 1) * D], PS_S2[:, 0:1024]))
        v.wait_ge(PE.h, pe_pre0 + 5)
        DV.inc(v.tensor_copy(CREPC[:], PS_P[:, 0:32]))
        DV.inc(v.tensor_copy(SREPC[:], PS_P[:, 32:64]))
    dv_pre2 = DV.n

    emit_ln(p, block, Xap, xlnt=XLNT, cst=S, off=0, pre="lnc",
            identb=identb, identb_ready=(LDI, ldi_all),
            sx=SXA, sq=SCR8, pt=LNPT)
    cp_ln = CP.n

    KGS = [3, 3, 3, 3, 3, 1]
    KGO = [0, 3, 6, 9, 12, 15]
    NKG = 6

    for h in range(16):
        pe0, dv0, act0, cp0 = PE.n, DV.n, ACT.n, CP.n
        tok0 = h * 256
        DVSTEP = [1, 2, 9, 10, 17, 18]

        @block.tensor
        def _(te):
            te.wait_ge(CP.h, cp_ln)
            te.wait_ge(LDC.h, ldc_all)
            if h >= 1:
                te.wait_ge(HGD.h, hgd0 + h)
            for wi in range(3):
                wsb = (WQT, WKT, WVT)[wi]
                for tl in range(2):
                    if wi + tl > 0:
                        te.wait_ge(DV.h, dv0 + DVSTEP[2 * wi + tl - 1])
                    ins = None
                    for nh in range(2):
                        for cc in range(NC):
                            ins = te.matmul(
                                PJ[:, nh * 512:(nh + 1) * 512],
                                XLNT[:, cc * S + tok0 + tl * 128:
                                     cc * S + tok0 + (tl + 1) * 128],
                                wsb[:, cc * D + nh * 512:cc * D + (nh + 1) * 512],
                                start=(cc == 0), stop=(cc == NC - 1),
                                skip_group_check=True)
                    PE.inc(ins)

        @block.vector
        def _(v):
            def evac(wi):
                bofs = (0, D, 2 * D)[wi]
                for tl in range(2):
                    v.wait_ge(PE.h, pe0 + 2 * wi + tl + 1)
                    if wi + tl > 0:
                        v.wait_ge(DV.h, DV.n)
                    DV.inc(v.tensor_tensor(PREB[:, tl * D:(tl + 1) * D],
                                           PJ[:, 0:1024],
                                           BB[:, bofs:bofs + D], ALU.add))

            def rope(srcT, dstT):
                sv = lambda two: srcT.ap().rearrange(
                    "p (tl g i two) -> p tl g i two", tl=2, g=16, two=2)[
                    :, :, :, :, two:two + 1]
                dv_ = lambda two: dstT.ap().rearrange(
                    "p (tl g i two) -> p tl g i two", tl=2, g=16, two=2)[
                    :, :, :, :, two:two + 1]
                cb = CREPC[:].unsqueeze(1).unsqueeze(2).broadcast_to(
                    [128, 2, 16, 32]).unsqueeze(4)
                sb = SREPC[:].unsqueeze(1).unsqueeze(2).broadcast_to(
                    [128, 2, 16, 32]).unsqueeze(4)
                r1 = SCR8[:, 0:1024].rearrange("p (tl g i) -> p tl g i",
                                               tl=2, g=16).unsqueeze(4)
                r2 = SCR8[:, 1024:2048].rearrange("p (tl g i) -> p tl g i",
                                                  tl=2, g=16).unsqueeze(4)
                v.wait_ge(DV.h, DV.n)
                DV.inc(v.tensor_tensor(r1, sv(0), cb, ALU.mult))
                DV.inc(v.tensor_tensor(r2, sv(1), sb, ALU.mult))
                v.wait_ge(DV.h, DV.n)
                DV.inc(v.tensor_tensor(dv_(0), r1, r2, ALU.subtract))
                v.wait_ge(DV.h, DV.n)
                DV.inc(v.tensor_tensor(r1, sv(1), cb, ALU.mult))
                DV.inc(v.tensor_tensor(r2, sv(0), sb, ALU.mult))
                v.wait_ge(DV.h, DV.n)
                DV.inc(v.tensor_tensor(dv_(1), r1, r2, ALU.add))
                v.wait_ge(DV.h, DV.n)
            evac(0)
            rope(PREB, QR)
            evac(1)
            rope(PREB, KR)
            evac(2)
            v.wait_ge(DV.h, DV.n)
            for rhalf in range(2):
                dst = GVE2.ap().rearrange("p (g two e) -> p g two e",
                                          g=16, two=2)[:, :, rhalf:rhalf + 1,
                                                       0:64]
                src = PREB[:, rhalf * D:(rhalf + 1) * D].rearrange(
                    "p (g o e) -> p g o e", g=16, o=1)
                DV.inc(v.tensor_copy(dst, src))
        assert DV.n == dv0 + 6 + 12 + 2
        dv_prep_end = DV.n
        pe_tr0 = PE.n

        @block.tensor
        def _(te):
            te.wait_ge(DV.h, dv_prep_end)
            for b8 in range(2):
                if b8 > 0:
                    te.wait_ge(CP.h, cp0 + b8)
                ins = None
                for cc in range(NC):
                    ins = te.transpose(
                        TSTG[:, cc * 128:(cc + 1) * 128],
                        KR[:, b8 * D + cc * 128:b8 * D + (cc + 1) * 128],
                        identb[:])
                PE.inc(ins)
            for qb in range(4):
                te.wait_ge(CP.h, cp0 + 3 + 2 * qb)
                ins = None
                for cc in range(NC):
                    ins = te.transpose(TSTG[:, cc * 128:(cc + 1) * 128],
                                       QD8[:, cc * 128:(cc + 1) * 128],
                                       identb[:])
                PE.inc(ins)

        @block.vector
        def _(v):
            for b8 in range(2):
                v.wait_ge(PE.h, pe_tr0 + b8 + 1)
                dst = KTT.ap().rearrange("p (cc two x) -> p cc two x",
                                         cc=NC, two=2)[:, :, b8:b8 + 1, :]
                src = TSTG.rearrange("p (cc x) -> p cc x", cc=NC).unsqueeze(2)
                CP.inc(v.tensor_copy(dst, src))
            for qb in range(4):
                pi, rhalf = qb // 2, qb % 2
                dstd = QD8[:].rearrange("p (cc two e) -> p cc two e",
                                        cc=NC, two=2)
                srcd = QR[:, rhalf * D:(rhalf + 1) * D].rearrange(
                    "p (gh two e) -> p gh two e", gh=8, two=2)[
                    :, :, pi:pi + 1, :].squeeze(2) \
                    .unsqueeze(2).broadcast_to([128, 8, 2, 64])
                v.wait_ge(PE.h, pe_tr0 + 2 + qb)
                CP.inc(v.tensor_copy(dstd, srcd))
                v.wait_ge(PE.h, pe_tr0 + 2 + qb + 1)
                dst = QTT.rearrange("p (pp jb x) -> p pp jb x",
                                    pp=2, jb=16)[:, pi:pi + 1, :, :] \
                    .rearrange("p pp (ccq rh) x -> p pp ccq rh x", rh=2)[
                    :, :, :, rhalf:rhalf + 1, :]
                src = TSTG.rearrange("p (cc x) -> p cc x", cc=NC) \
                    .unsqueeze(1).unsqueeze(3)
                CP.inc(v.tensor_copy(dst, src))
        assert CP.n == cp0 + 10

        pe_i0, act_i0, dv_i0 = PE.n, ACT.n, DV.n
        peib = lambda k: pe_i0 + 13 * k
        actib = lambda k: act_i0 + 12 * k
        dvib = lambda k: dv_i0 + 5 * k

        @block.tensor
        def _(te):
            te.wait_ge(CP.h, cp0 + 10)
            for k in range(8):
                pi, ib = k // 4, k % 4
                if k >= 1:
                    te.wait_ge(DV.h, dvib(k - 1) + 5)
                for kg in range(NKG):
                    if kg >= 1:
                        te.wait_ge(ACT.h, actib(k) + 2 * kg)
                    i1 = None
                    for j in range(KGS[kg]):
                        jb = KGO[kg] + j
                        qrhs0 = QTT[0:64, pi * 2048 + ib * 512:
                                    pi * 2048 + (ib + 1) * 512]
                        qrhs1 = QTT[64:128, pi * 2048 + ib * 512:
                                    pi * 2048 + (ib + 1) * 512]
                        te.matmul(PS_S1[:, j * 512:(j + 1) * 512],
                                  KTT[0:64, jb * 128:(jb + 1) * 128],
                                  qrhs0,
                                  start=True, stop=True, tile_position=(0, 0),
                                  skip_group_check=True)
                        i1 = te.matmul(PS_S2[:, j * 512:(j + 1) * 512],
                                       KTT[64:128, jb * 128:(jb + 1) * 128],
                                       qrhs1,
                                       start=True, stop=True,
                                       tile_position=(64, 0),
                                       skip_group_check=True)
                    PE.inc(i1)
                    te.wait_ge(ACT.h, actib(k) + 2 * (kg + 1))
                    ins = None
                    for j in range(KGS[kg]):
                        jb = KGO[kg] + j
                        cc, rhalf = jb // 2, jb % 2
                        st = (kg == 0 and j == 0)
                        sp = (kg == NKG - 1 and j == KGS[kg] - 1)
                        jc_e = (2 * cc) * 2 + rhalf
                        jc_o = (2 * cc + 1) * 2 + rhalf
                        te.matmul(PVH[0:65, :],
                                  GVE2[:, jc_e * 65:jc_e * 65 + 65],
                                  PTSA[:, j * 512:(j + 1) * 512],
                                  start=st, stop=False, skip_group_check=True)
                        ins = te.matmul(PVH[0:65, :],
                                        GVE2[:, jc_o * 65:jc_o * 65 + 65],
                                        PTSB[:, j * 512:(j + 1) * 512],
                                        start=False, stop=sp,
                                        skip_group_check=True)
                    PE.inc(ins)
                te.wait_ge(DV.h, dvib(k) + 1)
                PE.inc(te.matmul(NBC[0:64, :], ONER[0:1, 0:64], DNI[0:1, :],
                                 start=True, stop=True, skip_group_check=True))
        assert PE.n == pe_i0 + 13 * 8

        @block.scalar
        def _(sc_):
            for k in range(8):
                for kg in range(NKG):
                    w = KGS[kg] * 512
                    sc_.wait_ge(PE.h, peib(k) + 2 * kg + 1)
                    ACT.inc(sc_.activation(PTSA[:, 0:w], PS_S1[:, 0:w],
                                           AF.Exp, scale=scale))
                    ACT.inc(sc_.activation(PTSB[:, 0:w], PS_S2[:, 0:w],
                                           AF.Exp, scale=scale))
        assert ACT.n == act_i0 + 12 * 8

        @block.vector
        def _(v):
            for k in range(8):
                pi, ib = k // 4, k % 4
                v.wait_ge(PE.h, peib(k) + 12)
                DV.inc(v.reciprocal(DNI[0:1, :], PVH[64:65, :]))
                v.wait_ge(PE.h, peib(k) + 13)
                DV.inc(v.tensor_copy(BCC[:, 0:512], NBC[:, :]))
                v.wait_ge(DV.h, DV.n)
                DV.inc(v.tensor_tensor(OUTB[0:64, :], PVH[0:64, :],
                                       BCC[0:64, 0:512], ALU.mult))
                v.wait_ge(DV.h, DV.n)
                dst2 = OUTH.ap().rearrange(
                    "p (rh rr cl two) -> p rh rr cl two",
                    rh=2, rr=128, two=2)[
                    :, :, :, 2 * ib:2 * ib + 2, pi:pi + 1] \
                    .transpose([0, 3, 1, 2, 4])
                src = OUTB[0:64, :].rearrange(
                    "p (cl rh rr) -> p cl rh rr", cl=2, rh=2).unsqueeze(4)
                DV.inc(v.tensor_copy(dst2, src))
                DV.inc(v.engine_nop())
        assert DV.n == dv_i0 + 5 * 8

        odh = DmaStream(p, "od%d" % h, 1)

        @block.gpsimd
        def _(gp):
            gp.wait_ge(DV.h, dv_i0 + 5 * 8)
            odh.issue(gp.dma_start(
                out=ODR[h // 2, (h % 2) * 64:(h % 2) * 64 + 64, :],
                in_=OUTH[0:64, :]))

        @block.vector
        def _(v):
            odh.wait(v, 0)
            HGD.inc(v.engine_nop())

    pe_o0, dv_o0 = PE.n, DV.n
    oti = DmaStream(p, "oti", 1)
    ldr = DmaStream(p, "ldr", 2)
    stx = DmaStream(p, "stx", 2)

    @block.sync
    def _(sync):
        sync.wait_ge(HGD.h, hgd0 + 16)
        for t in range(NT):
            if t >= 1:
                sync.wait_ge(PE.h, pe_o0 + t)
            oti.issue(sync.dma_start(
                out=OTI[:].rearrange("p (c x) -> p c x", c=NC),
                in_=ODR[:, :, t * 128:(t + 1) * 128].transpose([1, 0, 2])))
            if t >= 2:
                sync.wait_ge(DV.h, dv_o0 + 2 * (t - 1))
            ldr.issue(sync.dma_start(out=SXA[:, (t % 2) * D:(t % 2 + 1) * D],
                                     in_=Xap[t * 128:(t + 1) * 128, :]))

    @block.tensor
    def _(te):
        for t in range(NT):
            oti.wait(te, t)
            if t >= 2:
                te.wait_ge(DV.h, dv_o0 + 2 * (t - 1))
            pdst = (PS_S1 if t % 2 == 0 else PS_S2)
            ins = None
            for nh in range(2):
                for cc in range(NC):
                    ins = te.matmul(
                        pdst[:, nh * 512:(nh + 1) * 512],
                        OTI[:, cc * 128:(cc + 1) * 128],
                        WOT[:, cc * D + nh * 512:cc * D + (nh + 1) * 512],
                        start=(cc == 0), stop=(cc == NC - 1),
                        skip_group_check=True)
            PE.inc(ins)
    assert PE.n == pe_o0 + NT

    @block.vector
    def _(v):
        for t in range(NT):
            v.wait_ge(PE.h, pe_o0 + t + 1)
            ldr.wait(v, t)
            if t >= 2:
                stx.wait(v, t - 2)
            psrc = (PS_S1 if t % 2 == 0 else PS_S2)
            ob = (t % 2) * D
            DV.inc(v.tensor_tensor(SCR8[:, ob:ob + D], psrc[:, 0:1024],
                                   BB[:, 3 * D:4 * D], ALU.add))
            v.wait_ge(DV.h, DV.n)
            DV.inc(v.tensor_tensor(SCR8[:, ob:ob + D], SCR8[:, ob:ob + D],
                                   SXA[:, ob:ob + D], ALU.add))
    assert DV.n == dv_o0 + 2 * NT

    @block.gpsimd
    def _(gp):
        for t in range(NT):
            gp.wait_ge(DV.h, dv_o0 + 2 * t + 2)
            stx.issue(gp.dma_start(out=XOap[t * 128:(t + 1) * 128, :],
                                   in_=SCR8[:, (t % 2) * D:(t % 2 + 1) * D]))
        for t in (NT - 2, NT - 1):
            stx.wait(gp, t)


def build_attn_l1a():
    p = Prog("attnl1a")
    nc = p.nc
    X = nc.dram_tensor("X", [S, D], F32, kind="ExternalInput").ap()
    WQT = nc.dram_tensor("WQT", [D, D], F32, kind="ExternalInput").ap()
    WKT = nc.dram_tensor("WKT", [D, D], F32, kind="ExternalInput").ap()
    WVT = nc.dram_tensor("WVT", [D, D], F32, kind="ExternalInput").ap()
    WOT = nc.dram_tensor("WOT", [D, D], F32, kind="ExternalInput").ap()
    BQ = nc.dram_tensor("BQ", [1, D], F32, kind="ExternalInput").ap()
    BK = nc.dram_tensor("BK", [1, D], F32, kind="ExternalInput").ap()
    BV = nc.dram_tensor("BV", [1, D], F32, kind="ExternalInput").ap()
    BO = nc.dram_tensor("BO", [1, D], F32, kind="ExternalInput").ap()
    CROW = nc.dram_tensor("CROW", [1, 32], F32, kind="ExternalInput").ap()
    SROW = nc.dram_tensor("SROW", [1, 32], F32, kind="ExternalInput").ap()
    XO = nc.dram_tensor("XO", [S, D], F32, kind="ExternalOutput").ap()
    with nc.Block() as block:
        emit_attn_l1a(p, block, X, XO, WQT, WKT, WVT, WOT,
                      BQ, BK, BV, BO, CROW, SROW)
    return p


def prep_attn_l1a_inputs(x_b, wq, bq, wk, bk, wv, bv, wo, bo,
                         rot_cos, rot_sin):
    r = lambda a: np.ascontiguousarray(np.asarray(a))
    return {
        "X": r(x_b),
        "WQT": r(np.asarray(wq).T), "WKT": r(np.asarray(wk).T),
        "WVT": r(np.asarray(wv).T), "WOT": r(np.asarray(wo).T),
        "BQ": r(np.asarray(bq).reshape(1, D)),
        "BK": r(np.asarray(bk).reshape(1, D)),
        "BV": r(np.asarray(bv).reshape(1, D)),
        "BO": r(np.asarray(bo).reshape(1, D)),
        "CROW": r(np.asarray(rot_cos)[S - 1].reshape(1, 32)),
        "SROW": r(np.asarray(rot_sin)[S - 1].reshape(1, 32)),
    }




def emit_attn_l1b(p, block, Xap, XOap, wqkvt_d, wot_d):
    nc = p.nc
    LDC, LDI, DV, ACT, PE, CP = (
        p.sem("LDC"), p.sem("LDI"), p.sem("DV"), p.sem("ACT"),
        p.sem("PE"), p.sem("CP"))
    HGD = p.sem("HGD")
    hgd0 = HGD.n

    identb = p.sbuf("identb", [128, 128], BF16)
    idb_d = nc.inline_tensor(np.eye(128).astype(np.float32), p.uname("idb_d"))
    XLNT = p.sbuf("xlnt", [128, NC * S], BF16)
    WQKV = p.sbuf("wqkv", [128, NC * 384], BF16)
    WOT = p.sbuf("wot", [128, NC * D], BF16)
    QKVR = p.sbuf("qkvr", [128, NT * 192], BF16)
    QSB = p.sbuf("qsb", [128, NT * 64], BF16)
    KEB = p.sbuf("keb", [128, NT * 64], BF16)
    VEB = p.sbuf("veb", [128, NT * 65], BF16)
    QT = p.sbuf("qt", [64, S], BF16)
    SEXT = p.sbuf("sext", [64, 2 * 65], BF16)
    OA = p.sbuf("oa", [65, S], BF16)
    RECB = p.sbuf("recb", [128, NT], F32)
    EXPQ = p.sbuf("expq", [128, NT * 64], F32)
    DNI = p.sbuf("dni", [1, 512], F32)
    BCC = p.sbuf("bcc", [128, 512], F32)
    OUT64 = p.sbuf("out64", [64, S], BF16)
    ONER = p.sbuf("oner", [1, 128], F32)
    OTI = p.sbuf("oti", [128, NC * 128], BF16)
    SXA = p.sbuf("sxa", [128, 2 * D], F32)
    SCR8 = p.sbuf("scr8", [128, 2 * D], F32)

    PS_S1 = p.psum("ps_s1", [128, 1536], F32)
    PS_S2 = p.psum("ps_s2", [128, 1536], F32)
    PS_P = p.psum("ps_p", [128, 1024], F32)
    LNPT = PS_S1[:, 0:1024].bitcast(BF16)
    PJ = PS_P
    CTX = PS_S1[0:64, 1024:1089]
    OAP = PS_S2[0:65, 1024:1536]
    TSTG = PS_S1[:, 0:512].bitcast(BF16)
    NBC = PS_S2[0:64, 0:512]

    ODR = nc.dram_tensor(p.uname("odr"), [NC, 128, S], BF16, kind="Internal").ap()

    @block.gpsimd
    def _(gp):
        LDI.inc(gp.dma_start(out=identb[:], in_=idb_d.ap()), 16)
        LDC.inc(gp.dma_start(
            out=WOT[:].rearrange("p (c d) -> p c d", c=NC),
            in_=wot_d.rearrange("(c p) d -> p c d", p=128)), 16)
    ldc_all, ldi_all = LDC.n, LDI.n

    pe_pre0, dv_pre0 = PE.n, DV.n

    @block.vector
    def _(v):
        DV.inc(v.memset(ONER[:], 1.0))
        ones_view = VEB.ap().rearrange("p (t e) -> p t e", t=NT)[:, :, 64:65]
        DV.inc(v.memset(ones_view, 1.0))

    emit_ln(p, block, Xap, xlnt=XLNT, cst=S, off=0, pre="lnd",
            identb=identb, identb_ready=(LDI, ldi_all),
            sx=SXA, sq=SCR8, pt=LNPT)
    cp_ln = CP.n

    wg = DmaStream(p, "bw", 2)

    def _issue_wg(gp, h):
        wg.issue(gp.dma_start(
            out=WQKV[:, (h % 2) * NC * 192:(h % 2) * NC * 192 + NC * 192]
                .rearrange("p (c d) -> p c d", c=NC),
            in_=wqkvt_d[:, h * 192:(h + 1) * 192]
                .rearrange("(c p) d -> p c d", p=128)))

    @block.gpsimd
    def _(gp):
        _issue_wg(gp, 0)
        _issue_wg(gp, 1)

    for h in range(16):
        pe0, dv0, act0, cp0 = PE.n, DV.n, ACT.n, CP.n
        wslot = (h % 2) * NC * 192

        @block.tensor
        def _(te):
            te.wait_ge(CP.h, cp_ln)
            wg.wait(te, h)
            if h >= 1:
                te.wait_ge(HGD.h, hgd0 + h)
            for t in range(NT):
                if t >= 2:
                    te.wait_ge(DV.h, dv0 + (t - 1))
                ins = None
                for cc in range(NC):
                    ins = te.matmul(
                        PJ[:, (t % 2) * 512:(t % 2) * 512 + 192],
                        XLNT[:, cc * S + t * 128:cc * S + (t + 1) * 128],
                        WQKV[:, wslot + cc * 192:wslot + (cc + 1) * 192],
                        start=(cc == 0), stop=(cc == NC - 1),
                        skip_group_check=True)
                PE.inc(ins)

        @block.vector
        def _(v):
            for t in range(NT):
                v.wait_ge(PE.h, pe0 + t + 1)
                DV.inc(v.tensor_copy(QKVR[:, t * 192:(t + 1) * 192],
                                     PJ[:, (t % 2) * 512:(t % 2) * 512 + 192]))

        @block.scalar
        def _(sc_):
            sc_.wait_ge(DV.h, dv0 + NT)
            ACT.inc(sc_.activation(
                EXPQ[:].rearrange("p (t e) -> p t e", t=NT),
                QKVR[:].rearrange("p (t w) -> p t w", t=NT)[:, :, 0:64],
                AF.Exp))
            ACT.inc(sc_.activation(
                KEB[:].rearrange("p (t e) -> p t e", t=NT),
                QKVR[:].rearrange("p (t w) -> p t w", t=NT)[:, :, 64:128],
                AF.Exp))

        @block.vector
        def _(v):
            v.wait_ge(ACT.h, act0 + 1)
            DV.inc(v.reduce_sum(RECB[:], EXPQ[:].rearrange(
                "p (t e) -> p t e", t=NT), axis=AX.X))
            v.wait_ge(DV.h, DV.n)
            DV.inc(v.reciprocal(RECB[:], RECB[:]))
            v.wait_ge(DV.h, DV.n)
            DV.inc(v.scalar_tensor_tensor(
                QSB[:].rearrange("p (t e) -> p t e", t=NT),
                EXPQ[:].rearrange("p (t e) -> p t e", t=NT),
                0.125,
                RECB[:].unsqueeze(2).broadcast_to([128, NT, 64]),
                ALU.mult, ALU.mult))
            v.wait_ge(DV.h, dv0 + NT)
            DV.inc(v.tensor_copy(
                VEB[:].rearrange("p (t e) -> p t e", t=NT)[:, :, 0:64],
                QKVR[:].rearrange("p (t w) -> p t w", t=NT)[:, :, 128:192]))
        dv_qkv = DV.n

        pe_tr0 = PE.n

        @block.tensor
        def _(te):
            te.wait_ge(DV.h, dv_qkv)
            for b8 in range(4):
                if b8 > 0:
                    te.wait_ge(CP.h, cp0 + b8)
                ins = None
                for u in range(8):
                    t = b8 * 8 + u
                    ins = te.transpose(TSTG[0:64, u * 128:u * 128 + 128],
                                       QSB[:, t * 64:(t + 1) * 64],
                                       identb[:])
                PE.inc(ins)

        @block.vector
        def _(v):
            for b8 in range(4):
                v.wait_ge(PE.h, pe_tr0 + b8 + 1)
                CP.inc(v.tensor_copy(QT[0:64, b8 * 1024:(b8 + 1) * 1024],
                                     TSTG[0:64, 0:1024]))
        assert CP.n == cp0 + 4

        pe_b0, dv_b0 = PE.n, DV.n
        NBK = 64

        @block.tensor
        def _(te):
            te.wait_ge(CP.h, cp0 + 4)
            te.wait_ge(ACT.h, act0 + 2)
            for n in range(NBK):
                t, ph = n // 2, n % 2
                te.wait_ge(DV.h, dv_b0 + n + 1 + (n // 8))
                PE.inc(te.matmul(OAP[0:65, (n % 8) * 64:(n % 8) * 64 + 64],
                                 SEXT[0:64, (n % 2) * 65:(n % 2) * 65 + 65],
                                 QT[0:64, n * 64:(n + 1) * 64],
                                 start=True, stop=True, skip_group_check=True))
                lk = KEB[ph * 64:(ph + 1) * 64, t * 64:(t + 1) * 64]
                rv = VEB[ph * 64:(ph + 1) * 64, t * 65:(t + 1) * 65]
                PE.inc(te.matmul(CTX, lk, rv, start=(n == 0), stop=(n == NBK - 1),
                                 skip_group_check=True))
        assert PE.n == pe_b0 + 2 * NBK

        @block.vector
        def _(v):
            for n in range(NBK):
                if n == 0:
                    DV.inc(v.memset(SEXT[0:64, 0:65], 0.0))
                else:
                    v.wait_ge(PE.h, pe_b0 + 2 * n)
                    if n >= 2:
                        v.wait_ge(PE.h, pe_b0 + 2 * n - 1)
                    DV.inc(v.tensor_copy(SEXT[0:64, (n % 2) * 65:(n % 2) * 65 + 65],
                                         CTX))
                if n % 8 == 7:
                    v.wait_ge(PE.h, pe_b0 + 2 * n + 1)
                    DV.inc(v.tensor_copy(OA[0:65, (n - 7) * 64:(n + 1) * 64],
                                         OAP[0:65, :]))
        assert DV.n == dv_b0 + NBK + 8

        pe_n0, dv_n0 = PE.n, DV.n

        @block.vector
        def _(v):
            for nb in range(8):
                if nb == 0:
                    v.wait_ge(DV.h, dv_b0 + NBK + 8)
                DV.inc(v.tensor_single_scalar(DNI[0:1, :],
                                              OA[64:65, nb * 512:(nb + 1) * 512],
                                              1e-6, ALU.max))
                v.wait_ge(DV.h, DV.n)
                DV.inc(v.reciprocal(DNI[0:1, :], DNI[0:1, :]))
                v.wait_ge(DV.h, DV.n)
                DV.inc(v.engine_nop())
                v.wait_ge(PE.h, pe_n0 + nb + 1)
                DV.inc(v.tensor_copy(BCC[0:64, :], NBC))
                v.wait_ge(DV.h, DV.n)
                DV.inc(v.tensor_tensor(OUT64[0:64, nb * 512:(nb + 1) * 512],
                                       OA[0:64, nb * 512:(nb + 1) * 512],
                                       BCC[0:64, :], ALU.mult))
        assert DV.n == dv_n0 + 5 * 8

        @block.tensor
        def _(te):
            for nb in range(8):
                te.wait_ge(DV.h, dv_n0 + 5 * nb + 2)
                PE.inc(te.matmul(NBC, ONER[0:1, 0:64], DNI[0:1, :],
                                 start=True, stop=True, skip_group_check=True))
                if nb < 7:
                    te.wait_ge(DV.h, dv_n0 + 5 * nb + 4)
        assert PE.n == pe_n0 + 8

        odh = DmaStream(p, "od%d" % h, 1)

        @block.gpsimd
        def _(gp):
            gp.wait_ge(DV.h, dv_n0 + 5 * 8)
            odh.issue(gp.dma_start(
                out=ODR[h // 2, (h % 2) * 64:(h % 2) * 64 + 64, :],
                in_=OUT64[0:64, :]))
            if h + 2 < 16:
                _issue_wg(gp, h + 2)

        @block.vector
        def _(v):
            odh.wait(v, 0)
            HGD.inc(v.engine_nop())

    pe_o0, dv_o0 = PE.n, DV.n
    oti = DmaStream(p, "oti", 1)
    ldr = DmaStream(p, "ldr", 2)
    stx = DmaStream(p, "stx", 2)

    @block.sync
    def _(sync):
        sync.wait_ge(HGD.h, hgd0 + 16)
        for t in range(NT):
            if t >= 1:
                sync.wait_ge(PE.h, pe_o0 + t)
            oti.issue(sync.dma_start(
                out=OTI[:].rearrange("p (c x) -> p c x", c=NC),
                in_=ODR[:, :, t * 128:(t + 1) * 128].transpose([1, 0, 2])))
            if t >= 2:
                sync.wait_ge(DV.h, dv_o0 + 2 * (t - 1))
            ldr.issue(sync.dma_start(out=SXA[:, (t % 2) * D:(t % 2 + 1) * D],
                                     in_=Xap[t * 128:(t + 1) * 128, :]))

    @block.tensor
    def _(te):
        te.wait_ge(LDC.h, ldc_all)
        for t in range(NT):
            oti.wait(te, t)
            if t >= 2:
                te.wait_ge(DV.h, dv_o0 + 2 * (t - 1))
            pdst = (PS_S1 if t % 2 == 0 else PS_S2)
            ins = None
            for nh in range(2):
                for cc in range(NC):
                    ins = te.matmul(
                        pdst[:, nh * 512:(nh + 1) * 512],
                        OTI[:, cc * 128:(cc + 1) * 128],
                        WOT[:, cc * D + nh * 512:cc * D + (nh + 1) * 512],
                        start=(cc == 0), stop=(cc == NC - 1),
                        skip_group_check=True)
            PE.inc(ins)
    assert PE.n == pe_o0 + NT

    @block.vector
    def _(v):
        for t in range(NT):
            v.wait_ge(PE.h, pe_o0 + t + 1)
            ldr.wait(v, t)
            if t >= 2:
                stx.wait(v, t - 2)
            psrc = (PS_S1 if t % 2 == 0 else PS_S2)
            ob = (t % 2) * D
            DV.inc(v.tensor_tensor(SCR8[:, ob:ob + D], psrc[:, 0:1024],
                                   SXA[:, ob:ob + D], ALU.add))
            v.wait_ge(DV.h, DV.n)
            DV.inc(v.engine_nop())
    assert DV.n == dv_o0 + 2 * NT

    @block.gpsimd
    def _(gp):
        for t in range(NT):
            gp.wait_ge(DV.h, dv_o0 + 2 * t + 2)
            stx.issue(gp.dma_start(out=XOap[t * 128:(t + 1) * 128, :],
                                   in_=SCR8[:, (t % 2) * D:(t % 2 + 1) * D]))
        for t in (NT - 2, NT - 1):
            stx.wait(gp, t)


def build_attn_l1b():
    p = Prog("attnl1b")
    nc = p.nc
    X = nc.dram_tensor("X", [S, D], F32, kind="ExternalInput").ap()
    WQKVT = nc.dram_tensor("WQKVT", [D, 16 * 192], F32, kind="ExternalInput").ap()
    WOT = nc.dram_tensor("WOT", [D, D], F32, kind="ExternalInput").ap()
    XO = nc.dram_tensor("XO", [S, D], F32, kind="ExternalOutput").ap()
    with nc.Block() as block:
        emit_attn_l1b(p, block, X, XO, WQKVT, WOT)
    return p


def prep_attn_l1b_inputs(x_b, W_qkv, W_out):
    Wt = np.ascontiguousarray(np.asarray(W_qkv).T)
    groups = []
    for h in range(16):
        groups.append(Wt[:, h * 64:(h + 1) * 64])
        groups.append(Wt[:, 1024 + h * 64:1024 + (h + 1) * 64])
        groups.append(Wt[:, 2048 + h * 64:2048 + (h + 1) * 64])
    wq = np.concatenate(groups, axis=1)
    return {
        "X": np.ascontiguousarray(x_b),
        "WQKVT": np.ascontiguousarray(wq),
        "WOT": np.ascontiguousarray(np.asarray(W_out).T),
    }




MOE_C = 1280
MOE_K = 8192
MOE_IT = 48
HUGEF = float(1 << 20)
MOE_TRUNC = 99


def emit_moe(p, block, Xap, XOap, wr_d, w1_d, b1_d, w2_d, b2_d,
             replica_groups=None, precast=None):
    nc = p.nc
    LDM, LDI, DV, ACT, PE, CP, GP = (
        p.sem("LDM"), p.sem("LDI"), p.sem("DV"), p.sem("ACT"),
        p.sem("PE"), p.sem("CP"), p.sem("GP"))
    CCS = p.sem("CCS")
    PRE = p.sem("PREF")
    SCT = p.sem("SCT")
    XLD = p.sem("XLD")
    WLD = p.sem("WLD")
    E, C = 8, MOE_C
    gp0_base = p.sem("GP").n
    if replica_groups is None:
        replica_groups = [list(range(NCORE))]

    identb = p.sbuf("identb", [128, 128], BF16)
    identf = p.sbuf("identf", [128, 128], F32)
    idb_d = nc.inline_tensor(np.eye(128).astype(np.float32), p.uname("idb_d"))
    XTOKE = p.sbuf("xtoke", [128, NT * 1026], BF16)
    OACC = XTOKE[:, 0:20480].bitcast(F32)
    UN = p.sbuf("union", [128, 41994], BF16)
    SC2 = UN[:, 0:4096].bitcast(F32)
    SLOCF = UN[:, 4096:12288].bitcast(F32)[0:8, :]
    POSB = UN[:, 12288:20480].bitcast(F32)[0:8, :]
    CMPB = UN[:, 20480:24576].bitcast(F32)
    ZER8 = UN[:, 24576:32768].bitcast(F32)[0:8, :]
    XINR = UN[:, 0:5130]
    XINT = UN[:, 5130:15370]
    HT = UN[:, 15370:25610]
    W1C = UN[:, 25610:33802]
    W2Q = UN[:, 33802:41994]
    LSC = p.sbuf("lsc", [128, NT * 8], F32)
    XLTF = p.sbuf("xltf", [128, 1024], F32)
    WRF = p.sbuf("wrf", [128, NC * 8], F32)
    WRBB = p.sbuf("wrbb", [128, NC * 8], BF16)
    DIDXT = p.sbuf("didxt", [128, 256], I32)
    B1SB = p.sbuf("b1sb", [128, 256], F32)
    RSM = p.sbuf("rsm", [128, 24], F32)
    RSC = p.sbuf("rsc", [128, 4], F32)
    GR = p.sbuf("gr", [128, 80], F32)
    GG = p.sbuf("gg", [128, 80], F32)
    GRC = p.sbuf("grc", [128, 20], F32)
    TIDF = p.sbuf("tidf", [128, 10], F32)
    TIDX = p.sbuf("tidx", [128, 10], I32)
    ONER = p.sbuf("oner", [1, 128], F32)
    ONESF = p.sbuf("onesf", [128, 1], F32)
    LO8 = p.sbuf("lo8", [1, 8], F32)
    HI8 = p.sbuf("hi8", [1, 8], F32)
    MID8 = p.sbuf("mid8", [1, 8], F32)
    TT8 = p.sbuf("tt8", [1, 16], F32)
    IND8 = p.sbuf("ind8", [1, 8], F32)
    MIDR = p.sbuf("midr", [128, 8], F32)
    CNTP = p.sbuf("cntp", [128, 8], F32)
    LOCOL = p.sbuf("locol", [8, 1], F32)
    ECC = p.sbuf("ecc", [8, 1], F32)
    ECI = p.sbuf("eci", [8, 1], I32)
    PCI = p.sbuf("pci", [128, 1], I32)
    PCF = p.sbuf("pcf", [128, 2], F32)
    PADR = p.sbuf("padr", [128, 1026], BF16)
    B2B = p.sbuf("b2b", [128, D], F32)
    B2RS = p.sbuf("b2rs", [1, D], F32)
    SXA = p.sbuf("sxa", [128, 2 * D], F32)
    SCR8 = p.sbuf("scr8", [128, 2 * D], F32)
    YCB = p.sbuf("ycb", [128, 2 * D], F32)

    PS_A = p.psum("ps_a", [128, 1536], F32)
    PS_B = p.psum("ps_b", [128, 1536], F32)
    PS_P = p.psum("ps_p", [128, 1024], F32)
    TSTGB = PS_P[:, 0:512].bitcast(BF16)
    TSTGF = PS_A[:, 0:1024]
    PSCR = PS_B[:, 0:8]
    PCNT = PS_B[0:1, 512:520]
    PMIDR = PS_B[:, 1024:1032]
    PLOC = PS_B[0:8, 520:521]
    PW1 = [PS_A[:, 0:1280], PS_B[:, 0:1280]]
    PW2 = [PS_P[:, 0:512], PS_P[:, 512:1024]]
    PGT = PS_B[:, 1280:1288]
    PB2A = PS_A[:, 0:512]
    PB2B = PS_A[:, 512:1024]

    WCD = p.sem("WCD")
    if precast is None:
        W1B = nc.dram_tensor(p.uname("w1b"), [E, 4, 2, 128, 4096], BF16,
                             kind="Internal").ap()
        W2B = nc.dram_tensor(p.uname("w2b"), [E, 4, 128, 8192], BF16,
                             kind="Internal").ap()
    else:
        W1B, W2B = precast
    AGIN = nc.dram_tensor(p.uname("ag_in"), [S, E], F32, kind="Internal").ap()
    AGOUT = nc.dram_tensor(p.uname("ag_out"), [NCORE * S, E], F32, kind="Internal").ap()
    XIN_d = nc.dram_tensor(p.uname("xin"), [E * C, 1026], BF16, kind="Internal").ap()

    @block.gpsimd
    def _(gp):
        LDI.inc(gp.dma_start(out=identb[:], in_=idb_d.ap()), 16)
        LDI.inc(gp.dma_start(
            out=WRBB[:].rearrange("p (c e) -> p c e", c=NC),
            in_=wr_d.rearrange("(c p) e -> p c e", p=128)), 16)
        GP.inc(gp.iota(ECI[:], pattern=[[0, 1]], base=0, channel_multiplier=C))
        GP.inc(gp.iota(PCI[:], pattern=[[0, 1]], base=0, channel_multiplier=1))
        if precast is None:
            for ce in range(E):
                for cq in range(4):
                    WCD.inc(gp.dma_start(out=W1B[ce, cq], in_=w1_d[ce, cq]),
                            16)
                    WCD.inc(gp.dma_start(out=W2B[ce, cq], in_=w2_d[ce, cq]),
                            16)
    ldi_all = LDI.n
    wcd_all = WCD.n

    @block.sync
    def _(sync):
        LDM.inc(sync.dma_start(out=identf[:], in_=idb_d.ap()), 16)
        LDM.inc(sync.dma_start(
            out=WRF[:].rearrange("p (c e) -> p c e", c=NC),
            in_=wr_d.rearrange("(c p) e -> p c e", p=128)), 16)
        LDM.inc(sync.dma_start(out=B1SB[:], in_=b1_d), 16)
        LDM.inc(sync.dma_start(out=XOap, in_=Xap), 16)
    ldm_all = LDM.n

    @block.vector
    def _(v):
        DV.inc(v.memset(ONER[:], 1.0))
        DV.inc(v.memset(ONESF[:], 1.0))
        DV.inc(v.memset(PADR[:, 0:1024], 0.0))
        DV.inc(v.memset(PADR[:, 1024:1026], 255.0))
        DV.inc(v.memset(LO8[:], 0.0))
        DV.inc(v.memset(HI8[:], 1.0))
        DV.inc(v.memset(MID8[:], 0.5))
        DV.inc(v.memset(ZER8[:], 0.0))
        v.wait_ge(GP.h, gp0_base + 2)
        DV.inc(v.tensor_copy(ECC[:], ECI[:]))
        DV.inc(v.tensor_copy(PCF[:, 0:1], PCI[:]))
        v.wait_ge(DV.h, DV.n)
        DV.inc(v.tensor_scalar(PCF[:, 1:2], PCF[:, 0:1], 128.0, None, ALU.add))
    dv_pads = DV.n

    ln = emit_ln(p, block, Xap, xlnt=None, xtok=XTOKE, xcst=1026, pre="lnm",
                 sx=SXA, sq=SCR8)
    STAT = ln["STAT"]
    dv_ln0 = ln["dv0"]
    dv_ln_end = DV.n

    dv_id0 = DV.n

    @block.vector
    def _(v):
        for t in range(NT):
            v.wait_ge(DV.h, dv_ln0 + 7 * t + 7)
            DV.inc(v.memset(XTOKE[:, t * 1026 + 1024:t * 1026 + 1025],
                            float(t // 2)))
            DV.inc(v.tensor_copy(XTOKE[:, t * 1026 + 1025:t * 1026 + 1026],
                                 PCF[:, (t % 2):(t % 2) + 1]))
    assert DV.n == dv_id0 + 2 * NT
    if MOE_TRUNC < 2:
        return

    mrx = DmaStream(p, "mrx", 2)
    pe_r0, dv_r0, act_r0 = PE.n, DV.n, ACT.n
    DV_RT = 8

    @block.sync
    def _(sync):
        for t in range(NT):
            if t == 0:
                sync.wait_ge(DV.h, dv_ln_end)
            if t >= 2:
                sync.wait_ge(DV.h, dv_r0 + DV_RT * (t - 1))
            mrx.issue(sync.dma_start(out=SXA[:, (t % 2) * D:(t % 2 + 1) * D],
                                     in_=Xap[t * 128:(t + 1) * 128, :]))

    @block.vector
    def _(v):
        v.wait_ge(LDM.h, ldm_all)
        for t in range(NT):
            b = (t % 2) * D
            c6 = 6 * t
            mrx.wait(v, t)
            if t >= 1:
                v.wait_ge(PE.h, pe_r0 + 2 * t)
            DV.inc(v.tensor_scalar(SCR8[:, b:b + D], SXA[:, b:b + D],
                                   STAT[:, c6 + 2:c6 + 3],
                                   STAT[:, c6 + 5:c6 + 6], ALU.add, ALU.mult))
            v.wait_ge(PE.h, pe_r0 + 2 * t + 1)
            DV.inc(v.tensor_copy(XLTF[:].rearrange("p (c x) -> p c x", c=NC),
                                 TSTGF.rearrange("p (c x) -> p c x", c=NC)))
            v.wait_ge(PE.h, pe_r0 + 2 * t + 2)
            DV.inc(v.tensor_copy(RSM[:, 0:8], PSCR))
            v.wait_ge(DV.h, DV.n)
            DV.inc(v.reduce_max(RSC[:, 0:1], RSM[:, 0:8], axis=AX.X))
            v.wait_ge(DV.h, DV.n)
            DV.inc(v.tensor_scalar(RSM[:, 8:16], RSM[:, 0:8], RSC[:, 0:1],
                                   None, ALU.subtract))
            v.wait_ge(ACT.h, act_r0 + t + 1)
            DV.inc(v.tensor_reduce(RSC[:, 1:2], RSM[:, 16:24], axis=AX.X,
                                   op=ALU.add))
            v.wait_ge(DV.h, DV.n)
            DV.inc(v.reciprocal(RSC[:, 2:3], RSC[:, 1:2]))
            v.wait_ge(DV.h, DV.n)
            DV.inc(v.tensor_scalar(LSC[:, t * 8:(t + 1) * 8], RSM[:, 16:24],
                                   RSC[:, 2:3], None, ALU.mult))
    assert DV.n == dv_r0 + DV_RT * NT
    dv_router_end = DV.n

    @block.scalar
    def _(sc_):
        for t in range(NT):
            sc_.wait_ge(DV.h, dv_r0 + DV_RT * t + 5)
            ACT.inc(sc_.activation(RSM[:, 16:24], RSM[:, 8:16], AF.Exp))
    assert ACT.n == act_r0 + NT

    @block.tensor
    def _(te):
        te.wait_ge(LDM.h, ldm_all)
        te.wait_ge(LDI.h, ldi_all)
        for t in range(NT):
            te.wait_ge(DV.h, dv_r0 + DV_RT * t + 1)
            ins = None
            for cc in range(NC):
                ins = te.transpose(TSTGF[:, cc * 128:(cc + 1) * 128],
                                   SCR8[:, (t % 2) * D + cc * 128:
                                        (t % 2) * D + (cc + 1) * 128],
                                   identf[:])
            PE.inc(ins)
            te.wait_ge(DV.h, dv_r0 + DV_RT * t + 2)
            ins = None
            for cc in range(NC):
                ins = te.matmul(PSCR, XLTF[:, cc * 128:(cc + 1) * 128],
                                WRF[:, cc * 8:(cc + 1) * 8],
                                start=(cc == 0), stop=(cc == NC - 1),
                                skip_group_check=True)
            PE.inc(ins)
    assert PE.n == pe_r0 + 2 * NT
    if MOE_TRUNC < 3:
        return

    CCO = p.sem("CCO")
    ccs0, cco0 = CCS.n, CCO.n

    @block.gpsimd
    def _(gp):
        gp.wait_ge(DV.h, dv_router_end)
        CCS.inc(gp.dma_start(out=AGIN.rearrange("(p t) e -> p t e", p=128),
                             in_=LSC[:].rearrange("p (t e) -> p t e", t=NT)),
                16)
        gp.wait_ge(CCS.h, ccs0 + 16)
        CCO.inc(gp.collective_compute(
            "AllGather", ALU.bypass, replica_groups=replica_groups,
            ins=[AGIN], outs=[AGOUT]), 1)
        gp.wait_ge(CCO.h, cco0 + 1)
        CCS.inc(gp.dma_start(out=SC2,
                             in_=AGOUT.rearrange("(p t) e -> p (t e)", p=128)),
                16)
    ccs_all = CCS.n
    if MOE_TRUNC < 4:
        return

    pe_b0, dv_b0 = PE.n, DV.n
    DV_BI = 12

    @block.tensor
    def _(te):
        for it in range(MOE_IT):
            if it == 0:
                te.wait_ge(CCS.h, ccs_all)
                te.wait_ge(DV.h, dv_b0)
            else:
                te.wait_ge(DV.h, dv_b0 + DV_BI * it)
            PE.inc(te.matmul(PMIDR, ONER[:], MID8[:], start=True, stop=True,
                             skip_group_check=True))
            te.wait_ge(DV.h, dv_b0 + DV_BI * it + 3)
            PE.inc(te.matmul(PCNT, ONESF[:], CNTP[:], start=True, stop=True,
                             skip_group_check=True))
        te.wait_ge(DV.h, dv_b0 + DV_BI * MOE_IT)
        PE.inc(te.matmul(PLOC, LO8[:], ONESF[0:1, 0:1], start=True, stop=True,
                         skip_group_check=True))
    assert PE.n == pe_b0 + 2 * MOE_IT + 1

    @block.vector
    def _(v):
        for it in range(MOE_IT):
            v.wait_ge(PE.h, pe_b0 + 2 * it + 1)
            DV.inc(v.tensor_copy(MIDR[:], PMIDR))
            v.wait_ge(DV.h, DV.n)
            DV.inc(v.tensor_tensor(
                CMPB.rearrange("p (t e) -> p t e", e=E),
                SC2.rearrange("p (t e) -> p t e", e=E),
                MIDR[:].unsqueeze(1).broadcast_to([128, 256, E]),
                ALU.is_gt))
            v.wait_ge(DV.h, DV.n)
            DV.inc(v.tensor_reduce(
                CNTP[:],
                CMPB.rearrange("p (t e) -> p t e", e=E).transpose([0, 2, 1]),
                axis=AX.X, op=ALU.add))
            v.wait_ge(PE.h, pe_b0 + 2 * it + 2)
            DV.inc(v.tensor_single_scalar(IND8[:], PCNT, float(MOE_K),
                                          ALU.is_ge))
            v.wait_ge(DV.h, DV.n)
            DV.inc(v.tensor_tensor(TT8[0:1, 0:8], MID8[:], LO8[:],
                                   ALU.subtract))
            v.wait_ge(DV.h, DV.n)
            DV.inc(v.tensor_tensor(TT8[0:1, 8:16], TT8[0:1, 0:8], IND8[:],
                                   ALU.mult))
            v.wait_ge(DV.h, DV.n)
            DV.inc(v.tensor_tensor(LO8[:], LO8[:], TT8[0:1, 8:16], ALU.add))
            DV.inc(v.tensor_tensor(TT8[0:1, 0:8], HI8[:], MID8[:],
                                   ALU.subtract))
            v.wait_ge(DV.h, DV.n)
            DV.inc(v.tensor_tensor(TT8[0:1, 8:16], TT8[0:1, 0:8], IND8[:],
                                   ALU.mult))
            v.wait_ge(DV.h, DV.n)
            DV.inc(v.tensor_tensor(HI8[:], MID8[:], TT8[0:1, 8:16], ALU.add))
            v.wait_ge(DV.h, DV.n)
            DV.inc(v.tensor_tensor(MID8[:], LO8[:], HI8[:], ALU.add))
            v.wait_ge(DV.h, DV.n)
            DV.inc(v.tensor_scalar(MID8[:], MID8[:], 0.5, None, ALU.mult))
        v.wait_ge(PE.h, pe_b0 + 2 * MOE_IT + 1)
        DV.inc(v.tensor_copy(LOCOL[:], PLOC))
    assert DV.n == dv_b0 + DV_BI * MOE_IT + 1
    if MOE_TRUNC < 5:
        return

    pe_m0, dv_m0, cp_m0 = PE.n, DV.n, CP.n

    @block.tensor
    def _(te):
        te.wait_ge(DV.h, dv_m0)
        for b8 in range(4):
            if b8 > 0:
                te.wait_ge(CP.h, cp_m0 + b8)
            ins = None
            for u in range(8):
                t = b8 * 8 + u
                ins = te.transpose(TSTGF[0:8, u * 128:(u + 1) * 128],
                                   LSC[:, t * 8:(t + 1) * 8], identf[:])
            PE.inc(ins)
        te.wait_ge(DV.h, dv_m0 + 6)
        for b8 in range(4):
            te.wait_ge(CP.h, cp_m0 + 4 + b8)
            ins = None
            for u in range(8):
                t = b8 * 8 + u
                ins = te.transpose(TSTGF[:, u * 8:(u + 1) * 8],
                                   POSB[0:8, t * 128:(t + 1) * 128],
                                   identf[0:8, 0:8])
            PE.inc(ins)
    assert PE.n == pe_m0 + 8

    @block.vector
    def _(v):
        for b8 in range(4):
            v.wait_ge(PE.h, pe_m0 + b8 + 1)
            CP.inc(v.tensor_copy(SLOCF[0:8, b8 * 1024:(b8 + 1) * 1024],
                                 TSTGF[0:8, 0:1024]))
        v.wait_ge(CP.h, cp_m0 + 4)
        v.wait_ge(DV.h, DV.n)
        DV.inc(v.tensor_scalar(SLOCF[:], SLOCF[:], LOCOL[:], None, ALU.is_gt))
        v.wait_ge(DV.h, DV.n)
        DV.inc(v.tensor_tensor_scan(POSB[:], SLOCF[:], ZER8[:], 0.0,
                                    ALU.add, ALU.add))
        v.wait_ge(DV.h, DV.n)
        DV.inc(v.tensor_tensor(POSB[:], POSB[:], SLOCF[:], ALU.subtract))
        v.wait_ge(DV.h, DV.n)
        DV.inc(v.tensor_scalar(POSB[:], POSB[:], ECC[:], None, ALU.add))
        v.wait_ge(DV.h, DV.n)
        DV.inc(v.tensor_scalar(SLOCF[:], SLOCF[:], -HUGEF, HUGEF,
                               ALU.mult, ALU.add))
        v.wait_ge(DV.h, DV.n)
        DV.inc(v.tensor_tensor(POSB[:], POSB[:], SLOCF[:], ALU.add))
        v.wait_ge(DV.h, DV.n)
        for b8 in range(4):
            v.wait_ge(PE.h, pe_m0 + 4 + b8 + 1)
            CP.inc(v.tensor_copy(DIDXT[:, b8 * 64:(b8 + 1) * 64],
                                 TSTGF[:, 0:64]))
    assert DV.n == dv_m0 + 6
    assert CP.n == cp_m0 + 8
    didxt_done = CP.n
    if MOE_TRUNC < 6:
        return

    @block.sync
    def _(sync):
        sync.wait_ge(DV.h, dv_pads)
        for r in range(E * C // 128):
            PRE.inc(sync.dma_start(out=XIN_d[r * 128:(r + 1) * 128, :],
                                   in_=PADR[:]), 16)
    pre_all = PRE.n

    @block.gpsimd
    def _(gp):
        gp.wait_ge(PRE.h, pre_all)
        gp.wait_ge(CP.h, didxt_done)
        gp.wait_ge(DV.h, dv_id0 + 2 * NT)
        bcr = gp.to_reg(E * C - 1)
        for t in range(NT):
            for e in range(E):
                SCT.inc(gp.indirect_dma_start(
                    XIN_d,
                    IndirectOffsetOnAxis(DIDXT[:, t * 8 + e:t * 8 + e + 1], 0),
                    XTOKE[:, t * 1026:(t + 1) * 1026], None,
                    bounds_check=bcr, oob_is_err=False), 16)
    sct_all = SCT.n
    if MOE_TRUNC < 7:
        return

    NQ, NST = 4, 10
    n_exp = min(E, MOE_TRUNC - 6)
    sad = DmaStream(p, "sad", 1)
    sad0 = sad.sems[0].n
    for e in range(n_exp):
        pe0, dv0, act0, cp0 = PE.n, DV.n, ACT.n, CP.n
        xld0, wld0 = XLD.n, WLD.n
        dvS1 = dv0 + 11
        dvS2 = dvS1 + 15
        dv_w2e = lambda q, k: dvS2 + 20 * q + k + 1
        dvS3 = dvS2 + 80
        dv_yc = lambda st: dvS3 + 2 * (st + 1)
        act_gelu = lambda gq: act0 + 1 + gq + 1
        pe_tr = lambda tt: pe0 + tt + 1
        pe_gt = lambda st: pe0 + 10 + st + 1
        pe_b2 = pe0 + 22
        pe_w1 = lambda gq: pe0 + 22 + (gq // 8) * 20 + gq + 1
        pe_w2 = lambda q, k: pe0 + 31 + 28 * q + k

        @block.sync
        def _(sync):
            sync.wait_ge(SCT.h, sct_all)
            if e >= 1:
                sync.wait_ge(PE.h, pe0 - 134 + 22)
            XLD.inc(sync.dma_start(out=B2RS[:], in_=b2_d[e:e + 1, :]), 16)
            for xh in range(2):
                if xh == 1:
                    sync.wait_ge(PE.h, pe_tr(4))
                    sync.wait_ge(DV.h, dv0 + 5)
                elif e >= 1:
                    sync.wait_ge(PE.h, pe0 - 134 + 10)
                    sync.wait_ge(DV.h, dv0 - 115 + 11)
                XLD.inc(sync.dma_start(
                    out=XINR[:].rearrange("p (t w) -> p t w", t=5),
                    in_=XIN_d[e * C + xh * 640:e * C + (xh + 1) * 640, :]
                        .rearrange("(t p) w -> p t w", p=128)), 16)
            if e == 0:
                sync.wait_ge(WCD.h, wcd_all)
            for q in range(NQ):
                if q >= 1:
                    sync.wait_ge(PE.h, pe_w2(q - 1, 19))
                elif e >= 1:
                    sync.wait_ge(PE.h, pe0)
                WLD.inc(sync.dma_start(out=W2Q[:], in_=W2B[e, q]), 16)
                for dhb in range(2):
                    gdb = q * 2 + dhb
                    if gdb >= 2:
                        sync.wait_ge(ACT.h, act_gelu(4 * (gdb - 1) - 1))
                    elif e >= 1:
                        sync.wait_ge(PE.h, pe0)
                    WLD.inc(sync.dma_start(
                        out=W1C[:, (dhb % 2) * 4096:(dhb % 2) * 4096 + 4096],
                        in_=W1B[e, q, dhb]), 16)

        @block.tensor
        def _(te):
            if e >= 1:
                te.wait_ge(DV.h, dv0)
            else:
                te.wait_ge(CP.h, didxt_done)
            for tt in range(10):
                xh, tl = tt // 5, tt % 5
                te.wait_ge(XLD.h, xld0 + 32 + 16 * xh)
                if tt > 0:
                    te.wait_ge(CP.h, cp0 + tt)
                ins = None
                for cc in range(NC):
                    ins = te.transpose(
                        TSTGB[:, cc * 128:(cc + 1) * 128],
                        XINR[:, tl * 1026 + cc * 128:tl * 1026 + (cc + 1) * 128],
                        identb[:])
                PE.inc(ins)
            te.wait_ge(CP.h, cp0 + 10)
            for st in range(NST):
                if st >= 1:
                    te.wait_ge(DV.h, dvS1 + st)
                ins = None
                for cc in range(NC):
                    ins = te.matmul(PGT,
                                    XINT[:, cc * 1280 + st * 128:
                                         cc * 1280 + (st + 1) * 128],
                                    WRBB[:, cc * 8:(cc + 1) * 8],
                                    start=(cc == 0), stop=(cc == NC - 1),
                                    skip_group_check=True)
                PE.inc(ins)
            te.wait_ge(XLD.h, xld0 + 48)
            te.wait_ge(DV.h, dvS1 + 10)
            PE.inc(te.matmul(PB2A, ONER[:], B2RS[0:1, 0:512], start=True,
                             stop=True, skip_group_check=True))
            PE.inc(te.matmul(PB2B, ONER[:], B2RS[0:1, 512:1024], start=True,
                             stop=True, skip_group_check=True))
            te.wait_ge(DV.h, dvS2)
            for q in range(NQ):
                for dhb in range(2):
                    gdb = q * 2 + dhb
                    te.wait_ge(WLD.h, wld0 + 48 * (q + 1))
                    for dhc in range(4):
                        gq = gdb * 4 + dhc
                        pw = PW1[gq % 2]
                        if gq >= 2:
                            te.wait_ge(ACT.h, act_gelu(gq - 2))
                        ins = None
                        for n3, (n0, nw) in enumerate(((0, 512), (512, 512),
                                                       (1024, 256))):
                            for cc in range(NC):
                                ins = te.matmul(
                                    pw[:, n0:n0 + nw],
                                    W1C[:, (dhb % 2) * 4096 + cc * 512 +
                                        dhc * 128:(dhb % 2) * 4096 + cc * 512 +
                                        (dhc + 1) * 128],
                                    XINT[:, cc * 1280 + n0:cc * 1280 + n0 + nw],
                                    start=(cc == 0),
                                    stop=(cc == NC - 1),
                                    skip_group_check=True)
                        PE.inc(ins)
                te.wait_ge(WLD.h, wld0 + 48 * (q + 1))
                te.wait_ge(ACT.h, act_gelu(q * 8 + 7))
                for st in range(NST):
                    for k2 in range(2):
                        k = st * 2 + k2
                        if q * 20 + k >= 2:
                            te.wait_ge(DV.h, dv_w2e(q, k - 2) if k >= 2
                                       else dv_w2e(q - 1, k + 18))
                        ins = None
                        for qd in range(8):
                            ins = te.matmul(
                                PW2[k % 2],
                                HT[:, qd * 1280 + st * 128:
                                   qd * 1280 + (st + 1) * 128],
                                W2Q[:, qd * 1024 + k2 * 512:
                                    qd * 1024 + (k2 + 1) * 512],
                                start=(qd == 0), stop=(qd == 7),
                                skip_group_check=True)
                        PE.inc(ins)
        assert PE.n == pe0 + 134

        @block.scalar
        def _(sc_):
            sc_.wait_ge(DV.h, dvS1 + 10)
            ACT.inc(sc_.activation(GG[:], GR[:], AF.Exp))
            for q in range(NQ):
                for dhb in range(2):
                    for dhc in range(4):
                        gq = (q * 2 + dhb) * 4 + dhc
                        sc_.wait_ge(PE.h, pe_w1(gq))
                        bcol = e * 32 + q * 8 + dhb * 4 + dhc
                        ACT.inc(sc_.activation(
                            HT[:].rearrange("p (qd w) -> p qd w", qd=8)[
                                :, (dhb * 4 + dhc):(dhb * 4 + dhc) + 1, :],
                            PW1[gq % 2][:, 0:1280].unsqueeze(1),
                            AF.Gelu, bias=B1SB[:, bcol:bcol + 1]))
        assert ACT.n == act0 + 33

        @block.vector
        def _(v):
            for tt in range(10):
                xh, tl = tt // 5, tt % 5
                v.wait_ge(PE.h, pe_tr(tt))
                dst = XINT.rearrange("p (c w) -> p c w", c=NC)[
                    :, :, tt * 128:(tt + 1) * 128]
                src = TSTGB.rearrange("p (c x) -> p c x", cc=NC) if False \
                    else TSTGB.rearrange("p (c x) -> p c x", c=NC)
                CP.inc(v.tensor_copy(dst, src))
                DV.inc(v.scalar_tensor_tensor(
                    TIDF[:, tt:tt + 1],
                    XINR[:, tl * 1026 + 1024:tl * 1026 + 1025], 256.0,
                    XINR[:, tl * 1026 + 1025:tl * 1026 + 1026],
                    ALU.mult, ALU.add))
            v.wait_ge(DV.h, DV.n)
            if e >= 1:
                v.wait_ge(sad.sems[0].h, sad0 + 16 * NST * e)
            DV.inc(v.tensor_copy(TIDX[:], TIDF[:]))
            assert DV.n == dvS1
            for st in range(NST):
                v.wait_ge(PE.h, pe_gt(st))
                DV.inc(v.tensor_copy(GR[:, st * 8:(st + 1) * 8], PGT))
            v.wait_ge(ACT.h, act0 + 1)
            DV.inc(v.reduce_sum(GRC[:, 0:10],
                                GG[:].rearrange("p (s w) -> p s w", s=NST),
                                axis=AX.X))
            v.wait_ge(DV.h, DV.n)
            DV.inc(v.reciprocal(GRC[:, 10:20], GRC[:, 0:10]))
            v.wait_ge(DV.h, DV.n)
            DV.inc(v.scalar_tensor_tensor(
                GG[:].rearrange("p (s w) -> p s w", s=NST),
                GG[:].rearrange("p (s w) -> p s w", s=NST), 1.0,
                GRC[:, 10:20].unsqueeze(2).broadcast_to([128, NST, E]),
                ALU.mult, ALU.mult))
            v.wait_ge(PE.h, pe_b2)
            DV.inc(v.tensor_copy(B2B[:, 0:512], PB2A))
            DV.inc(v.tensor_copy(B2B[:, 512:1024], PB2B))
            assert DV.n == dvS2
            for q in range(NQ):
                for st in range(NST):
                    for k2 in range(2):
                        k = st * 2 + k2
                        v.wait_ge(PE.h, pe_w2(q, k))
                        dst = OACC[:, st * 1024 + k2 * 512:
                                   st * 1024 + (k2 + 1) * 512]
                        if q == 0:
                            DV.inc(v.tensor_copy(dst, PW2[k % 2]))
                        else:
                            DV.inc(v.tensor_tensor(dst, dst, PW2[k % 2],
                                                   ALU.add))
            assert DV.n == dvS3
            for st in range(NST):
                if st >= 2:
                    sad.wait(v, e * NST + st - 2) if False else None
                ob = (st % 2) * D
                if st >= 2:
                    s2 = sad.sems[0]
                    v.wait_ge(s2.h, sad0 + 16 * (e * NST + st - 1))
                DV.inc(v.tensor_tensor(YCB[:, ob:ob + D],
                                       OACC[:, st * 1024:(st + 1) * 1024],
                                       B2B[:], ALU.add))
                v.wait_ge(DV.h, DV.n)
                DV.inc(v.tensor_scalar(YCB[:, ob:ob + D], YCB[:, ob:ob + D],
                                       GG[:, st * 8 + e:st * 8 + e + 1],
                                       None, ALU.mult))
            assert DV.n == dvS3 + 20

        @block.gpsimd
        def _(gp):
            gp.wait_ge(LDM.h, ldm_all)
            bcr2 = gp.to_reg(S - 1)
            for st in range(NST):
                gp.wait_ge(DV.h, dv_yc(st))
                i = e * NST + st
                if i >= 1:
                    gp.wait_ge(sad.sems[0].h, sad0 + 16 * i)
                sad.issue(gp.indirect_dma_start(
                    XOap,
                    IndirectOffsetOnAxis(TIDX[:, st:st + 1], 0),
                    YCB[:, (st % 2) * D:(st % 2 + 1) * D], None,
                    bounds_check=bcr2, oob_is_err=False,
                    compute_op=ALU.add))

    @block.gpsimd
    def _(gp):
        gp.wait_ge(sad.sems[0].h, sad0 + 16 * n_exp * NST)
    return (W1B, W2B)


def build_moe():
    p = Prog("moe")
    nc = p.nc
    X = nc.dram_tensor("X", [S, D], F32, kind="ExternalInput").ap()
    WR = nc.dram_tensor("WR", [D, 8], F32, kind="ExternalInput").ap()
    W1 = nc.dram_tensor("W1", [8, 4, 2, 128, 4096], F32,
                        kind="ExternalInput").ap()
    B1 = nc.dram_tensor("B1", [128, 256], F32, kind="ExternalInput").ap()
    W2 = nc.dram_tensor("W2", [8, 4, 128, 8192], F32,
                        kind="ExternalInput").ap()
    B2 = nc.dram_tensor("B2", [8, D], F32, kind="ExternalInput").ap()
    XO = nc.dram_tensor("XO", [S, D], F32, kind="ExternalOutput").ap()
    with nc.Block() as block:
        emit_moe(p, block, X, XO, WR, W1, B1, W2, B2)
    return p


def prep_moe_inputs(x_b, Wr, W1, b1, W2, b2):
    r = lambda a: np.ascontiguousarray(np.asarray(a))
    b1l = np.asarray(b1).reshape(8, 32, 128).transpose(2, 0, 1).reshape(128, 256)
    W1p = np.asarray(W1).reshape(8, 8, 128, 4, 2, 512).transpose(
        0, 3, 4, 2, 1, 5).reshape(8, 4, 2, 128, 4096)
    W2p = np.asarray(W2).reshape(8, 4, 8, 128, 1024).transpose(
        0, 1, 3, 2, 4).reshape(8, 4, 128, 8192)
    return {
        "X": r(x_b), "WR": r(Wr), "W1": r(W1p),
        "B1": r(b1l.astype(np.float32)),
        "W2": r(W2p), "B2": r(np.asarray(b2).reshape(8, D)),
    }


def stage_barrier(p, block):
    BAR = p.sem("BAR")
    waits = [(sm.h, sm.n) for nm, sm in p.sems.items()
             if sm.n > 0 and nm != "BAR"]

    @block.gpsimd
    def _(gp):
        for h, v in waits:
            gp.wait_ge(h, v)
        BAR.inc(gp.engine_nop())
    bt = BAR.n

    for sec in (block.sync, block.scalar, block.tensor, block.vector,
                block.gpsimd):
        @sec
        def _(eng):
            eng.wait_ge(BAR.h, bt)
    p.stage_reset()


def build_full():
    p = Prog("full")
    nc = p.nc
    p.enable_arena()
    ext = lambda n, sh: nc.dram_tensor(n, sh, F32, kind="ExternalInput").ap()
    X0 = ext("X", [S, D])
    XC = [nc.dram_tensor(f"XC{i}", [S, D], F32, kind="Internal").ap()
          for i in range(7)]
    XOF = nc.dram_tensor("XO", [S, D], F32, kind="ExternalOutput").ap()
    aW = [ext("a_WQT", [D, D]), ext("a_WKT", [D, D]), ext("a_BIAS", [128, 16])]
    mW = [ext("m_WR", [D, 8]), ext("m_W1", [8, 4, 2, 128, 4096]),
          ext("m_B1", [128, 256]), ext("m_W2", [8, 4, 128, 8192]),
          ext("m_B2", [8, D])]
    bW = [ext("b_WQKV", [D, 8 * 384]), ext("b_WOUT", [D, D]),
          ext("b_BOUT", [1, D]), ext("b_COSF", [128, 512]),
          ext("b_SINF", [128, 512])]
    lW = [ext("la_WQT", [D, D]), ext("la_WKT", [D, D]), ext("la_WVT", [D, D]),
          ext("la_WOT", [D, D]), ext("la_BQ", [1, D]), ext("la_BK", [1, D]),
          ext("la_BV", [1, D]), ext("la_BO", [1, D]), ext("la_CROW", [1, 32]),
          ext("la_SROW", [1, 32])]
    fW = [ext("f_WRT", [D, D]), ext("f_WKT", [D, D]), ext("f_WVT", [D, D]),
          ext("f_MUS", [128, 16])]
    nW = [ext("lb_WQKVT", [D, 16 * 192]), ext("lb_WOT", [D, D])]
    with nc.Block() as block:
        emit_attn_a(p, block, X0, XC[0], *aW)
        stage_barrier(p, block)
        pc = emit_moe(p, block, XC[0], XC[1], *mW)
        stage_barrier(p, block)
        emit_attn_b(p, block, XC[1], XC[2], *bW)
        stage_barrier(p, block)
        emit_moe(p, block, XC[2], XC[3], *mW, precast=pc)
        stage_barrier(p, block)
        emit_attn_l1a(p, block, XC[3], XC[4], *lW)
        stage_barrier(p, block)
        emit_l1ffn(p, block, XC[4], XC[5], *fW)
        stage_barrier(p, block)
        emit_attn_l1b(p, block, XC[5], XC[6], *nW)
        stage_barrier(p, block)
        emit_l1ffn(p, block, XC[6], XOF, *fW)
    assert len(p.sems) <= 100, len(p.sems)
    return p


def _fused_in_maps(g):
    ims = []
    for b in range(NCORE):
        x_b = np.ascontiguousarray(g["L0_attn_a_x"][b])
        im = {}
        d = prep_attn_a_inputs(x_b, g["L0_attn_a_Wq"], g["L0_attn_a_bq"],
                               g["L0_attn_a_Wk"], g["L0_attn_a_bk"])
        im["X"] = d.pop("X")
        im.update({"a_" + k: v for k, v in d.items()})
        d = prep_moe_inputs(x_b, g["L0_ffn_Wr"], g["L0_ffn_W1"],
                            g["L0_ffn_b1"], g["L0_ffn_W2"], g["L0_ffn_b2"])
        d.pop("X")
        im.update({"m_" + k: v for k, v in d.items()})
        d = prep_attn_b_inputs(x_b, g["L0_attn_b_Wq"], g["L0_attn_b_Wkv"],
                               g["L0_attn_b_Wout"], g["L0_attn_b_bout"])
        d.pop("X")
        im.update({"b_" + k: v for k, v in d.items()})
        d = prep_attn_l1a_inputs(
            x_b, g["L1_attn_a_wq"], g["L1_attn_a_bq"], g["L1_attn_a_wk"],
            g["L1_attn_a_bk"], g["L1_attn_a_wv"], g["L1_attn_a_bv"],
            g["L1_attn_a_wo"], g["L1_attn_a_bo"],
            g["L1_attn_a_rot_cos"], g["L1_attn_a_rot_sin"])
        d.pop("X")
        im.update({"la_" + k: v for k, v in d.items()})
        d = prep_l1ffn_inputs(x_b, g["L1_ffn_W_r"], g["L1_ffn_W_k"],
                              g["L1_ffn_W_v"], g["L1_ffn_mu_r"],
                              g["L1_ffn_mu_k"])
        d.pop("X")
        im.update({"f_" + k: v for k, v in d.items()})
        d = prep_attn_l1b_inputs(x_b, g["L1_attn_b_W_qkv"],
                                 g["L1_attn_b_W_out"])
        d.pop("X")
        im.update({"lb_" + k: v for k, v in d.items()})
        ims.append(im)
    return ims


_PROGS = {}


def _progs():
    if not _PROGS:
        _PROGS["a"] = build_attn_a()
        _PROGS["moe"] = build_moe()
        _PROGS["b"] = build_attn_b()
        _PROGS["l1a"] = build_attn_l1a()
        _PROGS["l1f"] = build_l1ffn()
        _PROGS["l1b"] = build_attn_l1b()
    return _PROGS


def kernel(**inputs):
    from concourse.bass_utils import run_bass_kernel_spmd
    g = {k: np.asarray(v) for k, v in inputs.items()}
    if "full" not in _PROGS:
        _PROGS["full"] = build_full()
    p = _PROGS["full"]
    ims = _fused_in_maps(g)
    res = run_bass_kernel_spmd(p.nc, ims, core_ids=list(range(NCORE)))
    return np.stack([np.asarray(res.results[c]["XO"])
                     for c in range(NCORE)]).astype(np.float32)


def kernel_unfused(**inputs):
    from concourse.bass_utils import run_bass_kernel_spmd
    g = {k: np.asarray(v) for k, v in inputs.items()}
    P = _progs()
    X = [np.ascontiguousarray(g["L0_attn_a_x"][b]) for b in range(NCORE)]

    def launch(p, prep):
        ims = [prep(b) for b in range(NCORE)]
        res = run_bass_kernel_spmd(p.nc, ims, core_ids=list(range(NCORE)))
        return [np.asarray(res.results[c]["XO"]) for c in range(NCORE)]

    moe_prep = lambda b: prep_moe_inputs(
        X[b], g["L0_ffn_Wr"], g["L0_ffn_W1"], g["L0_ffn_b1"],
        g["L0_ffn_W2"], g["L0_ffn_b2"])
    l1f_prep = lambda b: prep_l1ffn_inputs(
        X[b], g["L1_ffn_W_r"], g["L1_ffn_W_k"], g["L1_ffn_W_v"],
        g["L1_ffn_mu_r"], g["L1_ffn_mu_k"])

    X = launch(P["a"], lambda b: prep_attn_a_inputs(
        X[b], g["L0_attn_a_Wq"], g["L0_attn_a_bq"],
        g["L0_attn_a_Wk"], g["L0_attn_a_bk"]))
    X = launch(P["moe"], moe_prep)
    X = launch(P["b"], lambda b: prep_attn_b_inputs(
        X[b], g["L0_attn_b_Wq"], g["L0_attn_b_Wkv"],
        g["L0_attn_b_Wout"], g["L0_attn_b_bout"]))
    X = launch(P["moe"], moe_prep)
    X = launch(P["l1a"], lambda b: prep_attn_l1a_inputs(
        X[b], g["L1_attn_a_wq"], g["L1_attn_a_bq"], g["L1_attn_a_wk"],
        g["L1_attn_a_bk"], g["L1_attn_a_wv"], g["L1_attn_a_bv"],
        g["L1_attn_a_wo"], g["L1_attn_a_bo"],
        g["L1_attn_a_rot_cos"], g["L1_attn_a_rot_sin"]))
    X = launch(P["l1f"], l1f_prep)
    X = launch(P["l1b"], lambda b: prep_attn_l1b_inputs(
        X[b], g["L1_attn_b_W_qkv"], g["L1_attn_b_W_out"]))
    X = launch(P["l1f"], l1f_prep)
    return np.stack(X).astype(np.float32)
